# version 03
# baseline (speedup 1.0000x reference)
import numpy as np
from contextlib import ExitStack

import concourse.bass as bass
import concourse.mybir as mybir
from concourse.bass_utils import run_bass_kernel_spmd

F32 = mybir.dt.float32
BF16 = mybir.dt.bfloat16
AF = mybir.ActivationFunctionType
ALU = mybir.AluOpType

NCORE = 8
DM = 2048
DI = 4096
PROJ = 8512
L = 4096
CH = 128
NCH = L // CH
NH = 64
HD = 64
HG = 16
NHG = NH // HG
DFF = 5504
KT_DM = DM // 128
KT_DI = DI // 128
KT_FF = DFF // 128
XBC_T = 34
NEG_MASK = -60000.0
N_BISECT = 34


class Sched:

    ENGINES = ("pe", "v", "s", "sy", "gp")

    def __init__(self):
        self.steps = {e: [] for e in self.ENGINES}
        self.counts = {e: 0 for e in self.ENGINES}

    def step(self, eng, emit, deps=(), dma=False):
        inc = 16 if dma else 1
        kept = {}
        for d in deps:
            if d is None:
                continue
            de, dc, ddma = d
            if de == eng and not ddma:
                continue
            kept[de] = max(kept.get(de, 0), dc)
        self.steps[eng].append((emit, kept, inc))
        self.counts[eng] += inc
        return (eng, self.counts[eng], dma)

    def fence(self):
        return [(e, self.counts[e], True) for e in self.ENGINES if self.counts[e] > 0]

    def replay(self, eng, obj, sems):
        for emit, kept, inc in self.steps[eng]:
            for de, dc in kept.items():
                obj.wait_ge(sems[de], dc)
            emit(obj).then_inc(sems[eng], inc)

    def finalize(self, nc, sems):
        with nc.Block() as block:
            @block.tensor
            def _(t):
                self.replay("pe", t, sems)

            @block.vector
            def _(v):
                self.replay("v", v, sems)

            @block.scalar
            def _(s):
                self.replay("s", s, sems)

            @block.gpsimd
            def _(g):
                self.replay("gp", g, sems)

            @block.sync
            def _(sy):
                self.replay("sy", sy, sems)
                for e in self.ENGINES:
                    if self.counts[e] > 0:
                        sy.wait_ge(sems[e], self.counts[e])


def bc_tail(ap_, n_last):
    return ap_.to_broadcast([ap_.shape[0], ap_.shape[1], n_last])


def bc_mid(ap_, n_mid):
    return ap_.unsqueeze(1).to_broadcast([ap_.shape[0], n_mid, ap_.shape[1]])


def r3(ap_, h):
    return ap_.rearrange("p (h q) -> p h q", h=h)



DEBUG_INTERNALS = [False]


def build_A():
    nc = bass.Bass("TRN2", target_bir_lowering=False)
    din = lambda n, s, d=F32: nc.dram_tensor(n, list(s), d, kind="ExternalInput").ap()
    dout = lambda n, s, d=F32: nc.dram_tensor(n, list(s), d, kind="ExternalOutput").ap()
    dint = lambda n, s, d=F32: nc.dram_tensor(
        n, list(s), d, kind=("ExternalOutput" if DEBUG_INTERNALS[0] else "Internal")).ap()

    u_T = din("u_T", (DM, L))
    W_in = din("W_in", (DM, PROJ))
    conv_w_p = din("conv_w_p", (128, 34 * 4))
    conv_b_p = din("conv_b_p", (128, 34))
    dt_bias = din("dt_bias", (64, 1))
    A_log = din("A_log", (64, 1))
    D_param = din("D_param", (64, 1))
    norm_w = din("norm_w", (1, DI))
    W_out = din("W_out", (DI, DM))
    router_w = din("router_w", (DM, 8))
    router_b = din("router_b", (8, 1))
    u42_T = din("u42_T", (8, L))
    tril_c = din("tril_c", (128, 128))
    mask_c = din("mask_c", (128, 128))
    idf_c = din("idf_c", (128, 128))
    ones_c = din("ones_c", (1, 128))

    x_ln = dout("x_ln", (DM, L))
    sm_out = dout("sm", (8, L))

    x_dram = dint("x_dram", (DI, L), BF16)
    dbg_xs = dint("dbg_xs", (128, DI), BF16)
    dbg_gm = dint("dbg_gm", (128, HG * 128), BF16)
    dbg_y = dint("dbg_y", (128, DI))
    dbg_st = dint("dbg_st", (128, DI))
    z_dram = dint("z_dram", (L, DI))
    acum_dram = dint("acum_dram", (NCH, NH, CH))
    alast_dram = dint("alast_dram", (NCH, 64))
    yn_dram = dint("yn_dram", (L, DI), BF16)
    wo_bf = dint("wo_bf", (KT_DM, DI, 128), BF16)

    S = Sched()
    root = ExitStack()
    sems = {e: root.enter_context(nc.semaphore(f"sem_{e}")) for e in Sched.ENGINES}
    sb = lambda st, n, s, d=F32: st.enter_context(nc.sbuf_tensor(n, list(s), d))
    pt = lambda st, n, s, d=F32: st.enter_context(nc.psum_tensor(n, list(s), d))

    P = root
    s_tril = sb(P, "s_tril", (128, 128))
    s_mask = sb(P, "s_mask", (128, 128))
    s_idf = sb(P, "s_idf", (128, 128))
    s_idb = sb(P, "s_idb", (128, 128), BF16)
    s_ones = sb(P, "s_ones", (1, 128))
    s_onesb = sb(P, "s_onesb", (128, 1), BF16)
    s_ones8 = sb(P, "s_ones8", (8, 1))
    s_convw = sb(P, "s_convw", (128, 34 * 4))
    s_convb = sb(P, "s_convb", (128, 34))
    s_dtb = sb(P, "s_dtb", (64, 1))
    s_alog = sb(P, "s_alog", (64, 1))
    s_dpar = sb(P, "s_dpar", (64, 1))
    s_rb = sb(P, "s_rb", (8, 1))
    sBc = sb(P, "sBc", (128, L), BF16)
    sCc = sb(P, "sCc", (128, L), BF16)

    ld = None
    for t, d in ((s_tril, tril_c), (s_mask, mask_c), (s_idf, idf_c),
                 (s_ones, ones_c), (s_convw, conv_w_p), (s_convb, conv_b_p),
                 (s_dtb, dt_bias), (s_alog, A_log), (s_dpar, D_param),
                 (s_rb, router_b)):
        ld = S.step("sy", lambda sy, t=t, d=d: sy.dma_start(out=t[:], in_=d), dma=True)
    LDC = ld
    idb_cast = S.step("v", lambda v: v.tensor_copy(s_idb[:], s_idf[:]), deps=[LDC])
    onesb_set = S.step("v", lambda v: v.memset(s_onesb[:], 1.0))
    ones8_set = S.step("v", lambda v: v.memset(s_ones8[:], 1.0))
    s_eps5 = sb(P, "s_eps5", (128, 1))
    s_eps10 = sb(P, "s_eps10", (128, 1))
    eps5_set = S.step("v", lambda v: v.memset(s_eps5[:], 1e-5))
    eps10_set = S.step("v", lambda v: v.memset(s_eps10[:], 1e-10))

    P12 = ExitStack()
    s_dtraw = P12.enter_context(nc.sbuf_tensor("s_dtraw", [64, L], F32, side="right"))
    P1 = ExitStack()
    su = sb(P1, "su", (128, KT_DM * L), BF16)
    s_wblk = [sb(P1, f"s_wblk{i}", (128, KT_DM * 128), BF16) for i in range(2)]
    p_mm = [pt(P1, f"p_mm{i}", (128, 512)) for i in range(4)]
    P1a = ExitStack()
    s_stage = [sb(P1a, f"s_stage{i}", (128, 4104), BF16) for i in range(2)]
    s_cacc = sb(P1a, "s_cacc", (128, L))

    stage_zero = S.step("v", lambda v: [v.memset(s_stage[0][:, 0:3], 0.0),
                                        v.memset(s_stage[1][:, 0:3], 0.0)][-1])
    u_last = None
    for k in range(KT_DM):
        u_last = S.step("gp", lambda g, k=k: g.dma_start(
            out=su[:, k * L:(k + 1) * L], in_=u_T[k * 128:(k + 1) * 128, :]), dma=True)
    U_ALL = u_last

    def wblk_load(i, c0, cw, deps):
        return S.step("gp", lambda g, i=i, c0=c0, cw=cw: g.dma_start(
            out=s_wblk[i % 2][:, 0:KT_DM * cw].rearrange("p (k m) -> p k m", k=KT_DM),
            in_=W_in[:, c0:c0 + cw].rearrange("(k p) m -> p k m", p=128)),
            deps=deps, dma=True)

    mm_done = {}
    ev_done = {}
    g = 0
    w_dep = {0: wblk_load(0, DI + 4352, 64, [U_ALL])}
    for nt in range(8):
        mm_done[g] = S.step("pe", lambda t, nt=nt, pb=p_mm[g % 4]: [
            t.matmul(pb[0:64, :], s_wblk[0][:, k * 64:(k + 1) * 64],
                     su[:, k * L + nt * 512:k * L + nt * 512 + 512],
                     start=(k == 0), stop=(k == KT_DM - 1))
            for k in range(KT_DM)][-1], deps=[w_dep[0], U_ALL])
        ev_done[g] = S.step("s", lambda s, nt=nt, pb=p_mm[g % 4]: s.copy(
            s_dtraw[:, nt * 512:(nt + 1) * 512], pb[0:64, :]), deps=[mm_done[g]])
        g += 1
    w_dep[1] = wblk_load(1, DI, 128, [mm_done[7]])

    stage_free = {}
    spill_prev = None
    for j in range(XBC_T):
        wi = j + 1
        for nt in range(8):
            deps = [w_dep[wi]]
            if g >= 4:
                deps.append(ev_done[g - 4])
            if nt == 0 and stage_free.get(j % 2) is not None:
                deps.append(stage_free[j % 2])
            mm_done[g] = S.step("pe", lambda t, nt=nt, wi=wi, pb=p_mm[g % 4]: [
                t.matmul(pb[:], s_wblk[wi % 2][:, k * 128:(k + 1) * 128],
                         su[:, k * L + nt * 512:k * L + nt * 512 + 512],
                         start=(k == 0), stop=(k == KT_DM - 1))
                for k in range(KT_DM)][-1], deps=deps)
            ev_deps = [mm_done[g], stage_zero]
            if stage_free.get(j % 2) is not None:
                ev_deps.append(stage_free[j % 2])
            ev_done[g] = S.step("s", lambda s, j=j, nt=nt, pb=p_mm[g % 4]: s.copy(
                s_stage[j % 2][:, 3 + nt * 512:3 + nt * 512 + 512], pb[:]), deps=ev_deps)
            g += 1
        if wi + 1 <= XBC_T:
            w_dep[wi + 1] = wblk_load(wi + 1, DI + wi * 128, 128, [mm_done[g - 1]])

        def emit_conv(v, j=j):
            stg = s_stage[j % 2]
            v.tensor_scalar(s_cacc[:], stg[:, 0:L], s_convw[:, j * 4:j * 4 + 1],
                            s_convb[:, j:j + 1], ALU.mult, ALU.add)
            last = None
            for w in (1, 2, 3):
                last = v.scalar_tensor_tensor(
                    s_cacc[:], stg[:, w:w + L], s_convw[:, j * 4 + w:j * 4 + w + 1],
                    s_cacc[:], ALU.mult, ALU.add)
            return last
        conv_deps = [ev_done[g - 1], LDC]
        if j > 0:
            conv_deps.append(prev_sil)
        conv = S.step("v", emit_conv, deps=conv_deps)
        if j < 32:
            sil_deps = [conv]
            if spill_prev is not None:
                sil_deps.append(spill_prev)
            sil = S.step("s", lambda s, j=j: s.activation(
                s_stage[j % 2][:, 4:4 + L], s_cacc[:], AF.Silu), deps=sil_deps)
            spill_prev = S.step("sy", lambda sy, j=j: sy.dma_start(
                out=x_dram[j * 128:(j + 1) * 128, :], in_=s_stage[j % 2][:, 4:4 + L]),
                deps=[sil], dma=True)
            stage_free[j % 2] = spill_prev
            prev_sil = sil
        else:
            dstt = sBc if j == 32 else sCc
            sil = S.step("s", lambda s, dstt=dstt: s.activation(dstt[:], s_cacc[:],
                                                                AF.Silu), deps=[conv])
            stage_free[j % 2] = sil
            prev_sil = sil
    BC_READY = stage_free[33 % 2]
    X_SPILLED = spill_prev

    P1a.close()
    s_wz = [sb(P1, f"s_wz{i}", (128, KT_DM * 512), BF16) for i in range(2)]
    s_zst = [sb(P1, f"s_zst{i}", (128, 512)) for i in range(2)]
    zst_free = {}
    wz_dep = {}
    for nb in range(8):
        deps = [stage_free[0], stage_free[1]] if nb < 2 else [mm_done[("z", nb - 2, 31)]]
        wz_dep[nb] = S.step("gp", lambda gp, nb=nb: gp.dma_start(
            out=s_wz[nb % 2][:].rearrange("p (k m) -> p k m", k=KT_DM),
            in_=W_in[:, nb * 512:(nb + 1) * 512].rearrange("(k p) m -> p k m", p=128)),
            deps=deps, dma=True)
        for tt in range(32):
            deps = [wz_dep[nb]]
            if g >= 4:
                deps.append(ev_done[g - 4])
            mm_done[("z", nb, tt)] = S.step("pe", lambda t, nb=nb, tt=tt, pb=p_mm[g % 4]: [
                t.matmul(pb[:], su[:, k * L + tt * 128:k * L + tt * 128 + 128],
                         s_wz[nb % 2][:, k * 512:(k + 1) * 512],
                         start=(k == 0), stop=(k == KT_DM - 1))
                for k in range(KT_DM)][-1], deps=deps)
            mm_done[g] = mm_done[("z", nb, tt)]
            zi = (nb * 32 + tt) % 2
            ev_deps = [mm_done[g]]
            if zst_free.get(zi) is not None:
                ev_deps.append(zst_free[zi])
            ev_done[g] = S.step("s", lambda s, zi=zi, pb=p_mm[g % 4]: s.copy(
                s_zst[zi][:], pb[:]), deps=ev_deps)
            zst_free[zi] = S.step("sy", lambda sy, nb=nb, tt=tt, zi=zi: sy.dma_start(
                out=z_dram[tt * 128:(tt + 1) * 128, nb * 512:(nb + 1) * 512],
                in_=s_zst[zi][:]), deps=[ev_done[g]], dma=True)
            g += 1
    P1_END = S.fence()
    P1.close()

    P23 = ExitStack()
    s_dtT = sb(P23, "s_dtT", (128, NCH * 64), BF16)
    s_cfT = sb(P23, "s_cfT", (128, NCH * 64), BF16)
    s_acA = sb(P23, "s_acA", (128, NCH * 64))
    s_ds = sb(P23, "s_ds", (128, NCH * 64), BF16)
    s_od = sb(P23, "s_od", (128, NCH * 64))
    s_dA = sb(P23, "s_dA", (128, NCH * 64))
    sBT = sb(P23, "sBT", (128, NCH * 128), BF16)
    s_nwrep = sb(P23, "s_nwrep", (128, DI))
    P2 = ExitStack()
    s_dt = sb(P2, "s_dt", (64, L))
    s_a = sb(P2, "s_a", (64, L))
    s_aT = sb(P2, "s_aT", (128, NCH * 64))
    s_acT = sb(P2, "s_acT", (64, NCH * 128))
    s_nw = sb(P2, "s_nw", (1, DI))
    s_alf = sb(P2, "s_alf", (1, NCH * 64))
    p_tr = pt(P2, "p_tr", (128, 512))
    p_sm = pt(P2, "p_sm", (128, 64))

    nw_ld = S.step("sy", lambda sy: sy.dma_start(out=s_nw[:], in_=norm_w),
                   deps=P1_END, dma=True)
    c1 = S.step("s", lambda s: s.activation(s_dt[:], s_dtraw[:], AF.Identity,
                                            bias=s_dtb[:, 0:1]), deps=P1_END + [LDC])
    c2 = S.step("s", lambda s: s.activation(s_dtraw[:], s_dt[:], AF.Abs), deps=[c1])
    c3 = S.step("s", lambda s: s.activation(s_dtraw[:], s_dtraw[:], AF.Exp, scale=-1.0),
                deps=[c2])
    c4 = S.step("s", lambda s: s.activation(s_dtraw[:], s_dtraw[:], AF.Ln, bias=1.0),
                deps=[c3])
    c5 = S.step("v", lambda v: v.tensor_relu(s_a[:], s_dt[:]), deps=[c1])
    c6 = S.step("v", lambda v: v.tensor_add(s_dt[:], s_a[:], s_dtraw[:]), deps=[c4, c5])
    e1 = S.step("s", lambda s: s.activation(s_alog[:], s_alog[:], AF.Exp), deps=[LDC])
    a1 = S.step("v", lambda v: v.tensor_scalar(s_a[:], s_dt[:], s_alog[:, 0:1], None,
                                               ALU.mult), deps=[c6, e1])
    r1 = S.step("v", lambda v: v.reciprocal(s_dtraw[:], s_dt[:]), deps=[c6])
    r2 = S.step("v", lambda v: v.tensor_scalar(s_dtraw[:], s_dtraw[:], s_dpar[:, 0:1],
                                               None, ALU.mult), deps=[r1])

    prev_ev = None
    for (src, dst, dd) in ((s_a, s_aT, a1), (s_dt, s_dtT, c6), (s_dtraw, s_cfT, r2)):
        for b in range(4):
            deps = [dd] + ([prev_ev] if prev_ev else [])
            mm = S.step("pe", lambda t, src=src, b=b: [
                t.transpose(p_tr[:, q * 64:(q + 1) * 64],
                            src[:, (b * 8 + q) * 128:(b * 8 + q + 1) * 128],
                            s_idf[0:64, 0:64])
                for q in range(8)][-1], deps=deps)
            prev_ev = S.step("v", lambda v, dst=dst, b=b: v.tensor_copy(
                dst[:, b * 512:(b + 1) * 512], p_tr[:]), deps=[mm])
    for b in range(4):
        mm = S.step("pe", lambda t, b=b: [
            t.matmul(p_tr[:, q * 64:(q + 1) * 64], s_tril[:],
                     s_aT[:, (b * 8 + q) * 64:(b * 8 + q + 1) * 64],
                     start=True, stop=True)
            for q in range(8)][-1], deps=[prev_ev, LDC])
        prev_ev = S.step("v", lambda v, b=b: v.tensor_copy(
            s_acA[:, b * 512:(b + 1) * 512], p_tr[:]), deps=[mm])
    CUM = prev_ev
    for b in range(8):
        mm = S.step("pe", lambda t, b=b: [
            t.transpose(p_tr[0:64, q * 128:(q + 1) * 128],
                        s_acA[:, (b * 4 + q) * 64:(b * 4 + q + 1) * 64], s_idf[:])
            for q in range(4)][-1], deps=[CUM, prev_ev])
        prev_ev = S.step("v", lambda v, b=b: v.tensor_copy(
            s_acT[:, b * 512:(b + 1) * 512], p_tr[0:64, :]), deps=[mm])
    ACT_EV = prev_ev
    AC_SPILL = S.step("sy", lambda sy: sy.dma_start(
        out=acum_dram.rearrange("c h l -> h c l"),
        in_=s_acT[:].rearrange("p (c l) -> p c l", c=NCH)),
        deps=[ACT_EV], dma=True)
    al_sp = S.step("sy", lambda sy: sy.dma_start(
        out=alast_dram.rearrange("c h -> h c").unsqueeze(2),
        in_=s_acT[:].rearrange("p (c l) -> p c l", c=NCH)[:, :, 127:128],
        allow_slow_non_contiguous=True), deps=[ACT_EV], dma=True)
    al_ld = S.step("sy", lambda sy: sy.dma_start(
        out=s_alf[:], in_=alast_dram.rearrange("c h -> (c h)").unsqueeze(0),
        allow_slow_non_contiguous=True), deps=[al_sp], dma=True)
    al_ev = None
    for b in range(4):
        mm = S.step("pe", lambda t, b=b: t.matmul(
            p_tr[:], s_ones[:, :], s_alf[:, b * 512:(b + 1) * 512],
            start=True, stop=True), deps=[al_ld, ACT_EV] + ([al_ev] if al_ev else []))
        al_ev = S.step("v", lambda v, b=b: v.tensor_copy(
            s_dA[:, b * 512:(b + 1) * 512], p_tr[:]), deps=[mm])
    od1 = S.step("s", lambda s: s.activation(s_od[:], s_acA[:], AF.Exp), deps=[CUM])
    ds1 = S.step("v", lambda v: v.tensor_sub(s_aT[:, 0:NCH * 64], s_dA[:], s_acA[:]),
                 deps=[al_ev])
    ds2 = S.step("s", lambda s: s.activation(s_ds[:], s_aT[:, 0:NCH * 64], AF.Exp),
                 deps=[ds1])
    dA1 = S.step("s", lambda s: s.activation(s_dA[:], s_dA[:], AF.Exp), deps=[ds1])
    nw_ev = None
    for i in range(8):
        mm = S.step("pe", lambda t, i=i: t.matmul(
            p_tr[:], s_ones[:, :], s_nw[:, i * 512:(i + 1) * 512],
            start=True, stop=True), deps=[nw_ld, al_ev] + ([nw_ev] if nw_ev else []))
        nw_ev = S.step("v", lambda v, i=i: v.tensor_copy(
            s_nwrep[:, i * 512:(i + 1) * 512], p_tr[:]), deps=[mm])
    bt_prev = nw_ev
    for b in range(8):
        mm = S.step("pe", lambda t, b=b: [
            t.transpose(p_tr.ap().bitcast(BF16)[:, q * 128:(q + 1) * 128],
                        sBc[:, (b * 4 + q) * 128:(b * 4 + q + 1) * 128], s_idb[:])
            for q in range(4)][-1], deps=[idb_cast, bt_prev] + (P1_END if b == 0 else []))
        bt_prev = S.step("v", lambda v, b=b: v.tensor_copy(
            sBT[:, b * 512:(b + 1) * 512], p_tr.ap().bitcast(BF16)[:, 0:512]), deps=[mm])
    P2_END = S.fence()
    P2.close()
    P12.close()

    P3 = ExitStack()
    s_xT = [sb(P3, f"s_xT{i}", (128, DI), BF16) for i in range(2)]
    s_xs = sb(P3, "s_xs", (128, DI), BF16)
    s_xds = sb(P3, "s_xds", (128, DI), BF16)
    s_xcf = sb(P3, "s_xcf", (128, DI), BF16)
    s_arep = [sb(P3, f"s_arep{i}", (128, HG * 128)) for i in range(2)]
    s_D = sb(P3, "s_D", (128, HG * 128))
    s_Gm = sb(P3, "s_Gm", (128, HG * 128), BF16)
    s_Gt = sb(P3, "s_Gt", (128, 128), BF16)
    s_St = sb(P3, "s_St", (128, DI))
    s_Stmp = sb(P3, "s_Stmp", (128, HG * HD))
    s_Sbf = sb(P3, "s_Sbf", (128, DI), BF16)
    s_Y = sb(P3, "s_Y", (128, DI))
    s_zT = sb(P3, "s_zT", (128, DI))
    s_ss = sb(P3, "s_ss", (128, 1))
    s_rr = sb(P3, "s_rr", (128, 1))
    p_gt = pt(P3, "p_gt", (128, 128))
    p_yd = pt(P3, "p_yd", (128, HG * HD))
    p_yo = pt(P3, "p_yo", (128, HG * HD))
    p_cs = pt(P3, "p_cs", (128, HG * HD))

    xt_ld = {}
    z_ld = {}
    arep_ld = {}
    arep_free = {}
    pe_hg = {}
    prev_eD = [None]
    sbf_done = {}
    sq_done = {}
    yf_done = {}
    xs_done = {}
    hg_evac = {}
    yn_spill = {}
    prev_gt_ev = None
    init_S = S.step("v", lambda v: v.memset(s_St[:], 0.0), deps=P2_END)

    for ch in range(NCH):
        deps = ([xs_done[ch - 2], pe_hg[(ch - 2) * NHG + NHG - 1]]
                if ch >= 2 else [X_SPILLED] + P2_END)
        xt_ld[ch] = S.step("sy", lambda sy, ch=ch: sy.dma_start_transpose(
            s_xT[ch % 2][:], x_dram[:, ch * CH:(ch + 1) * CH]), deps=deps, dma=True)
        deps = [sq_done[ch - 1]] if ch >= 1 else P2_END
        z_ld[ch] = S.step("sy", lambda sy, ch=ch: sy.dma_start(
            out=s_zT[:], in_=z_dram[ch * 128:(ch + 1) * 128, :]), deps=deps, dma=True)

        x1 = S.step("v", lambda v, ch=ch: v.tensor_tensor(
            r3(s_xs[:], NH), r3(s_xT[ch % 2][:], NH),
            bc_tail(s_dtT[:, ch * 64:(ch + 1) * 64], HD), ALU.mult),
            deps=[xt_ld[ch]] + ([hg_evac[(ch - 1) * NHG + NHG - 1],
                                yn_spill[ch - 1]] if ch else []))
        x2 = S.step("v", lambda v, ch=ch: v.tensor_tensor(
            r3(s_xds[:], NH), r3(s_xs[:], NH),
            bc_tail(s_ds[:, ch * 64:(ch + 1) * 64], HD), ALU.mult), deps=[x1])
        x3 = S.step("v", lambda v, ch=ch: v.tensor_tensor(
            r3(s_xcf[:], NH), r3(s_xs[:], NH),
            bc_tail(s_cfT[:, ch * 64:(ch + 1) * 64], HD), ALU.mult), deps=[x2])
        xs_done[ch] = x3
        if DEBUG_INTERNALS[0] and ch == 1:
            x3 = S.step("sy", lambda sy: sy.dma_start(out=dbg_xs, in_=s_xs[:]),
                        deps=[x3], dma=True)
            xs_done[ch] = x3

        gt_deps = ([prev_gt_ev] if prev_gt_ev else []) + (P2_END if ch == 0 else [])
        gt_mm = S.step("pe", lambda t, ch=ch: t.matmul(
            p_gt[:], sBc[:, ch * CH:(ch + 1) * CH], sCc[:, ch * CH:(ch + 1) * CH],
            start=True, stop=True), deps=gt_deps)
        gt_ev = S.step("s", lambda s: s.copy(s_Gt[:], p_gt[:]), deps=[gt_mm])
        prev_gt_ev = gt_ev

        for hg in range(NHG):
            idx = ch * NHG + hg
            deps = [AC_SPILL]
            if idx >= 2:
                deps.append(arep_free[idx - 2])
            arep_ld[idx] = S.step("gp", lambda gp, ch=ch, hg=hg, idx=idx: gp.dma_start(
                out=s_arep[idx % 2][:],
                in_=acum_dram[ch, hg * HG:(hg + 1) * HG, :]
                .rearrange("h l -> (h l)").unsqueeze(0).partition_broadcast(128)
                .squeeze(1)), deps=deps, dma=True)
            d1 = S.step("v", lambda g, idx=idx: g.tensor_tensor(
                r3(s_D[:], HG), r3(s_arep[idx % 2][:], HG), bc_mid(s_mask[:], HG),
                ALU.add), deps=[arep_ld[idx], LDC] +
                ([prev_eD[0]] if prev_eD[0] else []))
            d2 = S.step("v", lambda v, ch=ch, hg=hg: v.tensor_tensor(
                r3(s_D[:], HG), r3(s_D[:], HG),
                bc_tail(s_acA[:, ch * 64 + hg * HG:ch * 64 + (hg + 1) * HG], 128),
                ALU.subtract), deps=[d1] + ([prev_eD[0]] if prev_eD[0] else []))
            eDap = s_arep[idx % 2].ap().bitcast(BF16)[:, 0:HG * 128]
            eD = S.step("s", lambda s, eDap=eDap: s.activation(eDap, s_D[:], AF.Exp),
                        deps=[d2])
            prev_eD[0] = eD
            gm = S.step("v", lambda v, eDap=eDap: v.tensor_tensor(
                r3(s_Gm[:], HG), r3(eDap, HG), bc_mid(s_Gt[:], HG), ALU.mult),
                deps=[eD, gt_ev])
            arep_free[idx] = gm

            def emit_pe(t, ch=ch, hg=hg):
                for hh in range(HG):
                    h = hg * HG + hh
                    t.matmul(p_yd[:, hh * HD:(hh + 1) * HD],
                             s_Gm[:, hh * 128:(hh + 1) * 128],
                             s_xs[:, h * HD:(h + 1) * HD],
                             start=(hh % 8 == 0), stop=False)
                for q in range(2):
                    t.matmul(p_yd[:, q * 512:(q + 1) * 512], s_idb[:],
                             s_xcf[:, hg * 1024 + q * 512:hg * 1024 + q * 512 + 512],
                             start=False, stop=True)
                if ch > 0:
                    for q in range(2):
                        t.matmul(p_yo[:, q * 512:(q + 1) * 512],
                                 sCc[:, ch * CH:(ch + 1) * CH],
                                 s_Sbf[:, hg * 1024 + q * 512:hg * 1024 + q * 512 + 512],
                                 start=True, stop=True)
                last = None
                for q in range(2):
                    last = t.matmul(p_cs[:, q * 512:(q + 1) * 512],
                                    sBT[:, ch * CH:(ch + 1) * CH],
                                    s_xds[:, hg * 1024 + q * 512:hg * 1024 + q * 512 + 512],
                                    start=True, stop=True)
                return last
            pe_deps = [gm, x3, idb_cast]
            if ch > 0:
                pe_deps.append(sbf_done[(ch - 1) * NHG + hg])
            if idx >= 1:
                pe_deps.append(hg_evac[idx - 1])
            if DEBUG_INTERNALS[0] and ch == 1 and hg == 0:
                gm = S.step("sy", lambda sy: sy.dma_start(out=dbg_gm, in_=s_Gm[:]),
                            deps=[gm], dma=True)
            pe_done = S.step("pe", emit_pe, deps=pe_deps)
            pe_hg[idx] = pe_done

            if ch > 0:
                y1 = S.step("v", lambda v, ch=ch, hg=hg: v.tensor_tensor(
                    r3(s_Stmp[:], HG), r3(p_yo[:], HG),
                    bc_tail(s_od[:, ch * 64 + hg * HG:ch * 64 + (hg + 1) * HG], HD),
                    ALU.mult), deps=[pe_done])
                y2 = S.step("v", lambda v, hg=hg: v.tensor_add(
                    s_Y[:, hg * 1024:(hg + 1) * 1024], p_yd[:], s_Stmp[:]), deps=[y1])
            else:
                y2 = S.step("v", lambda v, hg=hg: v.tensor_copy(
                    s_Y[:, hg * 1024:(hg + 1) * 1024], p_yd[:]), deps=[pe_done])
            st1 = S.step("v", lambda g, ch=ch, hg=hg: g.tensor_tensor(
                r3(s_Stmp[:], HG), r3(s_St[:, hg * 1024:(hg + 1) * 1024], HG),
                bc_tail(s_dA[:, ch * 64 + hg * HG:ch * 64 + (hg + 1) * HG], HD),
                ALU.mult), deps=[y2, init_S, dA1])
            st2 = S.step("v", lambda v, hg=hg: v.tensor_add(
                s_St[:, hg * 1024:(hg + 1) * 1024], s_Stmp[:], p_cs[:]), deps=[st1])
            hg_evac[idx] = st2
            sbf_done[idx] = S.step("s", lambda s, hg=hg: s.copy(
                s_Sbf[:, hg * 1024:(hg + 1) * 1024],
                s_St[:, hg * 1024:(hg + 1) * 1024]), deps=[st2])

        if DEBUG_INTERNALS[0] and ch == 1:
            hg_evac[ch * NHG + NHG - 1] = S.step(
                "sy", lambda sy: sy.dma_start(out=dbg_y, in_=s_Y[:]),
                deps=[hg_evac[ch * NHG + NHG - 1]], dma=True)
            hg_evac[ch * NHG + NHG - 1] = S.step(
                "sy", lambda sy: sy.dma_start(out=dbg_st, in_=s_St[:]),
                deps=[hg_evac[ch * NHG + NHG - 1]], dma=True)
        sil = S.step("s", lambda s: s.activation(s_zT[:], s_zT[:], AF.Silu),
                     deps=[z_ld[ch]] + ([yf_done[ch - 1]] if ch else []))
        yf = S.step("v", lambda v: v.tensor_mul(s_Y[:], s_Y[:], s_zT[:]),
                    deps=[sil, hg_evac[ch * NHG + NHG - 1], od1, ds2])
        yf_done[ch] = yf
        sq = S.step("s", lambda s: s.activation(s_zT[:], s_Y[:], AF.Square,
                                                accum_out=s_ss[:, 0:1]), deps=[yf])
        sq_done[ch] = sq
        rr1 = S.step("s", lambda s: s.activation(s_ss[:], s_ss[:], AF.Ln,
                                                 scale=1.0 / DI, bias=s_eps5[:, 0:1]),
                     deps=[sq, eps5_set])
        rr2 = S.step("s", lambda s: s.activation(s_rr[:], s_ss[:], AF.Exp, scale=-0.5),
                     deps=[rr1])
        yn1 = S.step("v", lambda v: v.tensor_scalar(s_Y[:], s_Y[:], s_rr[:, 0:1], None,
                                                    ALU.mult), deps=[rr2])
        yn_deps = [yn1, nw_ev]
        if ch:
            yn_deps.append(yn_spill[ch - 1])
        yn2 = S.step("v", lambda v: v.tensor_mul(s_xcf[:], s_Y[:], s_nwrep[:]),
                     deps=yn_deps)
        yn_spill[ch] = S.step("sy", lambda sy, ch=ch: sy.dma_start(
            out=yn_dram[ch * 128:(ch + 1) * 128, :], in_=s_xcf[:]), deps=[yn2], dma=True)

    P3_END = S.fence()
    P3.close()
    P23.close()

    P4 = ExitStack()
    TGT = 512
    NTG = L // TGT
    s_ynT = [sb(P4, f"s_ynT{i}", (128, KT_DI * TGT), BF16) for i in range(2)]
    s_wo = [sb(P4, f"s_wo{i}", (128, KT_DI * 128), BF16) for i in range(2)]
    s_xmid = sb(P4, "s_xmid", (128, KT_DM * TGT))
    s_ut = [sb(P4, f"s_ut{i}", (128, TGT)) for i in range(2)]
    s_xbf = [sb(P4, f"s_xbf{i}", (128, TGT), BF16) for i in range(2)]
    s_xqb = [sb(P4, f"s_xqb{i}", (128, TGT), BF16) for i in range(2)]
    s_mean = sb(P4, "s_mean", (1, TGT))
    s_var = sb(P4, "s_var", (1, TGT))
    s_m2 = sb(P4, "s_m2", (1, TGT))
    s_rstd = sb(P4, "s_rstd", (1, TGT))
    s_mrep = sb(P4, "s_mrep", (128, TGT))
    s_rrep = sb(P4, "s_rrep", (128, TGT))
    s_xlt = [sb(P4, f"s_xlt{i}", (128, TGT)) for i in range(2)]
    s_xlb = [sb(P4, f"s_xlb{i}", (128, TGT), BF16) for i in range(2)]
    s_rwb = sb(P4, "s_rwb", (128, KT_DM * 8), BF16)
    s_hsc = sb(P4, "s_hsc", (8, L))
    s_u42 = sb(P4, "s_u42", (8, L))
    s_dinv = sb(P4, "s_dinv", (1, L))
    p_o = [pt(P4, f"p_o{i}", (128, TGT)) for i in range(2)]
    p_st1 = pt(P4, "p_st1", (1, TGT))
    p_st2 = pt(P4, "p_st2", (1, TGT))
    p_scg = pt(P4, "p_scg", (8, TGT))
    p_den = pt(P4, "p_den", (1, TGT))
    p_rep = pt(P4, "p_rep", (128, TGT))

    rw_ld = S.step("gp", lambda gp: gp.dma_start(
        out=s_rwb[:].rearrange("p (k m) -> p k m", k=KT_DM),
        in_=router_w.rearrange("(k p) m -> p k m", p=128)), deps=P3_END, dma=True)
    wo_cast = None
    for mt in range(KT_DM):
        wo_cast = S.step("gp", lambda gp, mt=mt: gp.dma_start(
            out=wo_bf[mt], in_=W_out[:, mt * 128:(mt + 1) * 128]),
            deps=P2_END, dma=True)
    u42_ld = S.step("sy", lambda sy: sy.dma_start(out=s_u42[:], in_=u42_T),
                    deps=P3_END, dma=True)
    g1 = S.step("s", lambda s: s.activation(s_u42[:], s_u42[:], AF.Ln,
                                            bias=s_eps10[0:8, 0:1]),
                deps=[u42_ld, eps10_set])
    g2 = S.step("s", lambda s: s.activation(s_u42[:], s_u42[:], AF.Ln, scale=-1.0,
                                            bias=s_eps10[0:8, 0:1]), deps=[g1])

    mm4 = {}
    xm_done = {}
    stat_done = {}
    store_done = {}
    xlb_done = {}
    sc_mm = {}
    sc_free = None
    ut_ld = {}
    wo_ld = {}
    ynT_ld = {}

    for tg in range(NTG):
        for k in range(KT_DI):
            deps = P3_END if tg < 2 else [mm4[(tg - 2, KT_DM - 1)]]
            ynT_ld[(tg, k)] = S.step("sy", lambda sy, tg=tg, k=k: sy.dma_start_transpose(
                s_ynT[tg % 2][:, k * TGT:(k + 1) * TGT],
                yn_dram[tg * TGT:(tg + 1) * TGT, k * 128:(k + 1) * 128]),
                deps=deps, dma=True)
        for m in range(KT_DM):
            i = tg * KT_DM + m
            wdeps = P3_END if i < 2 else [mm4[(tg, m - 2) if m >= 2 else
                                              (tg - 1, KT_DM - 2 + m)]]
            wo_ld[i] = S.step("gp", lambda gp, m=m, i=i: gp.dma_start(
                out=s_wo[i % 2][:].rearrange("p (k m) -> p k m", k=KT_DI),
                in_=wo_bf[m].rearrange("(k p) m -> p k m", p=128)),
                deps=list(wdeps) + [wo_cast], dma=True)
            udeps = P3_END if i < 2 else [xm_done[i - 2]]
            ut_ld[i] = S.step("sy", lambda sy, tg=tg, m=m, i=i: sy.dma_start(
                out=s_ut[i % 2][:],
                in_=u_T[m * 128:(m + 1) * 128, tg * TGT:(tg + 1) * TGT]),
                deps=udeps, dma=True)
            deps = [wo_ld[i], ynT_ld[(tg, KT_DI - 1)]]
            if i >= 2:
                deps.append(xm_done[i - 2])
            mm4[(tg, m)] = S.step("pe", lambda t, i=i, tg=tg: [
                t.matmul(p_o[i % 2][:], s_wo[i % 2][:, k * 128:(k + 1) * 128],
                         s_ynT[tg % 2][:, k * TGT:(k + 1) * TGT],
                         start=(k == 0), stop=(k == KT_DI - 1))
                for k in range(KT_DI)][-1], deps=deps)
            cdeps = [mm4[(tg, m)], ut_ld[i]]
            if i >= 2:
                cdeps.append(stat_done[i - 2])
            xm_done[i] = S.step("v", lambda v, m=m, i=i: v.tensor_add(
                s_xmid[:, m * TGT:(m + 1) * TGT], p_o[i % 2][:], s_ut[i % 2][:]),
                deps=cdeps)
            xq = S.step("v", lambda v, m=m, i=i: v.tensor_mul(
                s_xqb[i % 2][:], s_xmid[:, m * TGT:(m + 1) * TGT],
                s_xmid[:, m * TGT:(m + 1) * TGT]), deps=[xm_done[i]])
            xb = S.step("s", lambda s, m=m, i=i: s.copy(
                s_xbf[i % 2][:], s_xmid[:, m * TGT:(m + 1) * TGT]), deps=[xm_done[i]])
            stat_done[i] = S.step("pe", lambda t, m=m, i=i: (
                t.matmul(p_st1[:], s_onesb[:, :], s_xbf[i % 2][:],
                         start=(m == 0), stop=(m == KT_DM - 1)),
                t.matmul(p_st2[:], s_onesb[:, :], s_xqb[i % 2][:],
                         start=(m == 0), stop=(m == KT_DM - 1)))[1],
                deps=[xq, xb, onesb_set])
        ilast = tg * KT_DM + KT_DM - 1
        v1 = S.step("v", lambda v: v.tensor_scalar_mul(s_mean[:], p_st1[:], 1.0 / DM),
                    deps=[stat_done[ilast]])
        v2 = S.step("v", lambda v: v.tensor_mul(s_m2[:], s_mean[:], s_mean[:]), deps=[v1])
        v3 = S.step("v", lambda v: v.scalar_tensor_tensor(
            s_var[:], p_st2[:], 1.0 / DM, s_m2[:], ALU.mult, ALU.subtract), deps=[v2])
        v4 = S.step("s", lambda s: s.activation(s_var[:], s_var[:], AF.Ln,
                                                bias=s_eps5[0:1, 0:1]),
                    deps=[v3, eps5_set])
        v5 = S.step("s", lambda s: s.activation(s_rstd[:], s_var[:], AF.Exp, scale=-0.5),
                    deps=[v4])
        mmr1 = S.step("pe", lambda t: t.matmul(p_rep[:], s_ones[:, :], s_mean[:],
                                               start=True, stop=True), deps=[v5])
        evr1 = S.step("v", lambda v: v.tensor_copy(s_mrep[:], p_rep[:]), deps=[mmr1])
        mmr2 = S.step("pe", lambda t: t.matmul(p_rep[:], s_ones[:, :], s_rstd[:],
                                               start=True, stop=True), deps=[evr1, v5])
        evr2 = S.step("v", lambda v: v.tensor_copy(s_rrep[:], p_rep[:]), deps=[mmr2])
        for m in range(KT_DM):
            i = tg * KT_DM + m
            adeps = [evr2]
            if m >= 2:
                adeps += [store_done[(tg, m - 2)], sc_mm[(tg, m - 2)]]
            elif tg >= 1:
                adeps += [store_done[(tg - 1, KT_DM - 2 + m)],
                          sc_mm[(tg - 1, KT_DM - 2 + m)]]
            ap1 = S.step("v", lambda v, m=m: v.tensor_sub(
                s_xlt[m % 2][:], s_xmid[:, m * TGT:(m + 1) * TGT], s_mrep[:]),
                deps=adeps)
            ap2 = S.step("v", lambda v, m=m: v.tensor_mul(
                s_xlt[m % 2][:], s_xlt[m % 2][:], s_rrep[:]), deps=[ap1])
            store_done[(tg, m)] = S.step("sy", lambda sy, tg=tg, m=m: sy.dma_start(
                out=x_ln[m * 128:(m + 1) * 128, tg * TGT:(tg + 1) * TGT],
                in_=s_xlt[m % 2][:]), deps=[ap2], dma=True)
            xlb_done[(tg, m)] = S.step("s", lambda s, m=m: s.copy(
                s_xlb[m % 2][:], s_xlt[m % 2][:]), deps=[ap2])
            scdeps = [xlb_done[(tg, m)], rw_ld]
            if m == 0 and sc_free is not None:
                scdeps.append(sc_free)
            sc_mm[(tg, m)] = S.step("pe", lambda t, m=m: t.matmul(
                p_scg[:], s_rwb[:, m * 8:(m + 1) * 8], s_xlb[m % 2][:],
                start=(m == 0), stop=(m == KT_DM - 1)), deps=scdeps)
        sc_ev = S.step("s", lambda s, tg=tg: s.activation(
            s_hsc[:, tg * TGT:(tg + 1) * TGT], p_scg[:], AF.Identity,
            bias=s_rb[:, 0:1]), deps=[sc_mm[(tg, KT_DM - 1)], LDC])
        sc_free = S.step("v", lambda v, tg=tg: v.tensor_sub(
            s_hsc[:, tg * TGT:(tg + 1) * TGT], s_hsc[:, tg * TGT:(tg + 1) * TGT],
            s_u42[:, tg * TGT:(tg + 1) * TGT]), deps=[sc_ev, g2])

    he = S.step("s", lambda s: s.activation(s_hsc[:], s_hsc[:], AF.Exp), deps=[sc_free])
    den_prev = None
    sm_deps = []
    for q in range(NTG):
        dmm = S.step("pe", lambda t, q=q: t.matmul(
            p_den[:], s_ones8[:, :], s_hsc[:, q * TGT:(q + 1) * TGT],
            start=True, stop=True), deps=[he, ones8_set] + ([den_prev] if den_prev else []))
        den_prev = S.step("v", lambda v, q=q: v.reciprocal(
            s_dinv[:, q * TGT:(q + 1) * TGT], p_den[:]), deps=[dmm])
    rep_prev = None
    for q in range(NTG):
        rmm = S.step("pe", lambda t, q=q: t.matmul(
            p_scg[:], s_ones[:, 0:8], s_dinv[:, q * TGT:(q + 1) * TGT],
            start=True, stop=True), deps=[den_prev] + ([rep_prev] if rep_prev else []))
        rep_prev = S.step("v", lambda v, q=q: v.tensor_mul(
            s_hsc[:, q * TGT:(q + 1) * TGT], s_hsc[:, q * TGT:(q + 1) * TGT],
            p_scg[:]), deps=[rmm])
        sm_deps.append(rep_prev)
    S.step("sy", lambda sy: sy.dma_start(out=sm_out, in_=s_hsc[:]),
           deps=sm_deps, dma=True)
    P4.close()

    S.finalize(nc, sems)
    root.close()
    return nc



def build_B():
    nc = bass.Bass("TRN2", target_bir_lowering=False)
    din = lambda n, s, d=F32: nc.dram_tensor(n, list(s), d, kind="ExternalInput").ap()
    dout = lambda n, s, d=F32: nc.dram_tensor(n, list(s), d, kind="ExternalOutput").ap()
    dint = lambda n, s, d=F32: nc.dram_tensor(
        n, list(s), d, kind=("ExternalOutput" if DEBUG_INTERNALS[0] else "Internal")).ap()

    x_ln = din("x_ln", (DM, L))
    smp = din("smp", (128, 2048))
    w_gate = din("w_gate", (DM, DFF))
    b_gate = din("b_gate", (128, KT_FF))
    w_up = din("w_up", (DM, DFF))
    b_up = din("b_up", (128, KT_FF))
    w_down = din("w_down", (DFF, DM))
    b_down = din("b_down", (128, KT_DM))
    sel_e = din("sel_e", (128, 8))
    sel_et = din("sel_et", (8, 128))
    sel_j = din("sel_j", (128, 16))
    ones_c = din("ones_c", (1, 128))
    bstep_c = din("bstep_c", (8, N_BISECT))

    out = dout("out", (DM, L))
    mfl_dram = dint("mfl_dram", (16, 256))
    wg_bf = dint("wg_bf", (KT_FF, DM, 128), BF16)
    wu_bf = dint("wu_bf", (KT_FF, DM, 128), BF16)
    wd_bf = dint("wd_bf", (KT_DM, DFF, 128), BF16)
    dbg_cnt = dint("dbg_cnt", (8, N_BISECT))
    dbg_thr = dint("dbg_thr", (128, 1))
    dbg_lo = dint("dbg_lo", (8, 1))
    dbg_xm2 = dint("dbg_xm2", (128, KT_DM * 512))
    dbg_hb = dint("dbg_hb", (128, KT_FF * 512), BF16)

    S = Sched()
    root = ExitStack()
    sems = {e: root.enter_context(nc.semaphore(f"sem_{e}")) for e in Sched.ENGINES}
    sb = lambda st, n, s, d=F32: st.enter_context(nc.sbuf_tensor(n, list(s), d))
    pt = lambda st, n, s, d=F32: st.enter_context(nc.psum_tensor(n, list(s), d))
    P = root

    s_smp = sb(P, "s_smp", (128, 2048))
    s_sele = sb(P, "s_sele", (128, 8))
    s_selet = sb(P, "s_selet", (8, 128))
    s_selj = sb(P, "s_selj", (128, 16))
    s_ones = sb(P, "s_ones", (1, 128))
    s_onesb = sb(P, "s_onesb", (128, 1), BF16)
    s_lo = sb(P, "s_lo", (8, 1))
    s_hi = sb(P, "s_hi", (8, 1))
    s_thr8 = sb(P, "s_thr8", (8, 1))
    s_step = sb(P, "s_step", (8, 1))
    s_lstep = sb(P, "s_lstep", (8, 1))
    s_bstep = sb(P, "s_bstep", (8, N_BISECT))
    s_c8 = sb(P, "s_c8", (8, 1))
    s_thr = sb(P, "s_thr", (128, 1))
    s_pc = sb(P, "s_pc", (128, 1))
    s_mrep = sb(P, "s_mrep", (128, L))
    s_bg = sb(P, "s_bg", (128, KT_FF))
    s_bu = sb(P, "s_bu", (128, KT_FF))
    s_bd = sb(P, "s_bd", (128, KT_DM))
    s_cntlog = sb(P, "s_cntlog", (8, N_BISECT))

    lds = None
    for t, d in ((s_smp, smp), (s_sele, sel_e), (s_selet, sel_et), (s_selj, sel_j),
                 (s_ones, ones_c), (s_bg, b_gate), (s_bu, b_up), (s_bd, b_down)):
        lds = S.step("sy", lambda sy, t=t, d=d: sy.dma_start(out=t[:], in_=d), dma=True)
    LDC = lds
    LD_BS = S.step("sy", lambda sy: sy.dma_start(out=s_bstep[:], in_=bstep_c), dma=True)
    wg_cast = wu_cast = wd_cast = None
    for f in range(KT_FF):
        wg_cast = S.step("gp", lambda gp, f=f: gp.dma_start(
            out=wg_bf[f], in_=w_gate[:, f * 128:(f + 1) * 128]), dma=True)
        wu_cast = S.step("gp", lambda gp, f=f: gp.dma_start(
            out=wu_bf[f], in_=w_up[:, f * 128:(f + 1) * 128]), dma=True)
    for mt in range(KT_DM):
        wd_cast = S.step("gp", lambda gp, mt=mt: gp.dma_start(
            out=wd_bf[mt], in_=w_down[:, mt * 128:(mt + 1) * 128]), dma=True)
    onesb_set = S.step("v", lambda v: v.memset(s_onesb[:], 1.0))
    s_eps5 = sb(P, "s_eps5", (1, 1))
    eps5_set = S.step("v", lambda v: v.memset(s_eps5[:], 1e-5))

    PB = ExitStack()
    s_junk = sb(PB, "s_junk", (128, 2048))
    s_mfl = sb(PB, "s_mfl", (16, 256))
    s_m1 = sb(PB, "s_m1", (1, L))
    p_thr = pt(PB, "p_thr", (128, 1))
    p_c8 = pt(PB, "p_c8", (8, 1))
    p_m16 = pt(PB, "p_m16", (16, 256))
    p_repm = pt(PB, "p_repm", (128, 512))

    z0 = S.step("v", lambda v: v.memset(s_lo[:], 0.0))
    z2 = S.step("v", lambda v: v.memset(s_thr8[:], 0.5))
    t_prev = None
    bstep_prev = None
    for it in range(N_BISECT):
        tm = S.step("pe", lambda t: t.matmul(p_thr[:], s_selet[:, :], s_thr8[:],
                                             start=True, stop=True),
                    deps=[z2, LDC] + ([t_prev] if t_prev else []) +
                         ([u2] if it > 0 else []))
        te = S.step("s", lambda s: s.copy(s_thr[:], p_thr[:]), deps=[tm])
        t_prev = te
        c_a = S.step("v", lambda v: v.tensor_scalar(
            s_junk[:], s_smp[:], s_thr[:, 0:1], None, ALU.is_gt), deps=[te, LDC])
        c_b = S.step("v", lambda v: v.reduce_sum(s_pc[:, 0:1], s_junk[:],
                                                 mybir.AxisListType.X), deps=[c_a])
        csum = S.step("pe", lambda t: t.matmul(p_c8[:], s_sele[:, :], s_pc[:],
                                               start=True, stop=True), deps=[c_b])
        if DEBUG_INTERNALS[0]:
            S.step("v", lambda v, it=it: v.tensor_copy(s_cntlog[:, it:it + 1], p_c8[:]),
                   deps=[csum])
        b1 = S.step("v", lambda v: v.tensor_scalar(s_c8[:], p_c8[:], 4096.0, None,
                                                   ALU.is_ge), deps=[csum, z0])
        u1 = S.step("s", lambda s, it=it: s.activation(
            s_step[:], s_c8[:], AF.Identity, scale=float(2.0 ** (-(it + 2)) * 2),
            bias=s_bstep[:, it:it + 1]), deps=[b1, LD_BS] + ([u4] if it > 0 else []))
        u2 = S.step("v", lambda v: v.tensor_add(s_thr8[:], s_thr8[:], s_step[:]),
                    deps=[u1])
        u3 = S.step("s", lambda s, it=it: s.activation(
            s_lstep[:], s_c8[:], AF.Identity, scale=float(2.0 ** (-(it + 1)))),
            deps=[b1, u1])
        u4 = S.step("v", lambda v: v.tensor_add(s_lo[:], s_lo[:], s_lstep[:]),
                    deps=[u3, u2])
        tm = S.step("pe", lambda t: t.matmul(p_thr[:], s_selet[:, :], s_lo[:],
                                         start=True, stop=True), deps=[u4, t_prev])
    te = S.step("s", lambda s: s.copy(s_thr[:], p_thr[:]), deps=[tm])
    if DEBUG_INTERNALS[0]:
        S.step("sy", lambda sy: sy.dma_start(out=dbg_cnt, in_=s_cntlog[:]),
               deps=[te], dma=True)
        te = S.step("sy", lambda sy: sy.dma_start(out=dbg_thr, in_=s_thr[:]),
                    deps=[te], dma=True)
        te = S.step("sy", lambda sy: sy.dma_start(out=dbg_lo, in_=s_lo[:]),
                    deps=[te], dma=True)
    ws = S.step("v", lambda v: v.scalar_tensor_tensor(
        s_junk[:], s_smp[:], s_thr[:, 0:1], s_smp[:], ALU.is_gt, ALU.mult), deps=[te])
    m16 = S.step("pe", lambda t: t.matmul(p_m16[:], s_selj[:, :], s_junk[:, 0:256],
                                          start=True, stop=True), deps=[ws])
    mf_ev = S.step("v", lambda v: v.tensor_copy(s_mfl[:], p_m16[:]), deps=[m16])
    mf_sp = S.step("sy", lambda sy: sy.dma_start(out=mfl_dram, in_=s_mfl[:]),
                   deps=[mf_ev], dma=True)
    m1_ld = S.step("sy", lambda sy: sy.dma_start(
        out=s_m1[:], in_=mfl_dram.rearrange("a b -> (a b)").unsqueeze(0)),
        deps=[mf_sp], dma=True)
    mrep_ev = None
    for q in range(8):
        rmm = S.step("pe", lambda t, q=q: t.matmul(
            p_repm[:], s_ones[:, :], s_m1[:, q * 512:(q + 1) * 512],
            start=True, stop=True), deps=[m1_ld] + ([mrep_ev] if mrep_ev else []))
        mrep_ev = S.step("v", lambda v, q=q: v.tensor_copy(
            s_mrep[:, q * 512:(q + 1) * 512], p_repm[:]), deps=[rmm])
    M_READY = mrep_ev
    B_END = S.fence()
    PB.close()

    TB = 512
    NTB = L // TB
    s_xb = sb(P, "s_xb", (128, KT_DM * TB), BF16)
    s_hb = sb(P, "s_hb", (128, KT_FF * TB), BF16)
    s_wg = [sb(P, f"s_wg{i}", (128, KT_DM * 128), BF16) for i in range(3)]
    s_wu = [sb(P, f"s_wu{i}", (128, KT_DM * 128), BF16) for i in range(3)]
    s_wd = [sb(P, f"s_wd{i}", (128, KT_FF * 128), BF16) for i in range(3)]
    s_gbf = sb(P, "s_gbf", (128, TB), BF16)
    s_xm2 = sb(P, "s_xm2", (128, KT_DM * TB))
    s_xr = [sb(P, f"s_xr{i}", (128, TB)) for i in range(2)]
    s_xqb = [sb(P, f"s_xqb{i}", (128, TB), BF16) for i in range(2)]
    s_xbf = [sb(P, f"s_xbf{i}", (128, TB), BF16) for i in range(2)]
    s_mean = sb(P, "s_mean", (1, TB))
    s_var = sb(P, "s_var", (1, TB))
    s_m2 = sb(P, "s_m2", (1, TB))
    s_rstd = sb(P, "s_rstd", (1, TB))
    s_mrep2 = sb(P, "s_mrep2", (128, TB))
    s_rrep2 = sb(P, "s_rrep2", (128, TB))
    s_ot = [sb(P, f"s_ot{i}", (128, TB)) for i in range(2)]
    p_gu = [pt(P, f"p_gua{i}", (128, TB)) for i in range(2)]
    p_gu2 = [pt(P, f"p_gub{i}", (128, TB)) for i in range(2)]
    p_dn = [pt(P, f"p_dn{i}", (128, TB)) for i in range(2)]
    p_st1 = pt(P, "p_st1", (1, TB))
    p_st2 = pt(P, "p_st2", (1, TB))

    xb_ld = {}
    gu_ld = {}
    gmm = {}
    umm = {}
    sil = {}
    hb_done = {}
    wd_ld = {}
    dmm = {}
    x1_done = {}
    x2_done = {}
    stat_done = {}
    apply_done = {}
    ot_store = {}
    xr_ld = {}
    dm_last = {}

    for tb in range(NTB):
        xdeps = B_END if tb == 0 else [gmm[(tb - 1, KT_FF - 1)], umm[(tb - 1, KT_FF - 1)]]
        for k in range(KT_DM):
            xb_ld[tb] = S.step("gp", lambda gp, tb=tb, k=k: gp.dma_start(
                out=s_xb[:, k * TB:(k + 1) * TB],
                in_=x_ln[k * 128:(k + 1) * 128, tb * TB:(tb + 1) * TB]),
                deps=xdeps, dma=True)
        for f in range(KT_FF):
            i = tb * KT_FF + f
            gdeps = [hb_done[i - 2]] if i >= 2 else (B_END if tb == 0 else [])

            S.step("sy", lambda sy, f=f, i=i: sy.dma_start(
                out=s_wg[i % 3][:].rearrange("p (k m) -> p k m", k=KT_DM),
                in_=wg_bf[f].rearrange("(k p) m -> p k m", p=128)),
                deps=list(gdeps) + [wg_cast], dma=True)
            gu_ld[i] = S.step("sy", lambda sy, f=f, i=i: sy.dma_start(
                out=s_wu[i % 3][:].rearrange("p (k m) -> p k m", k=KT_DM),
                in_=wu_bf[f].rearrange("(k p) m -> p k m", p=128)),
                deps=list(gdeps) + [wu_cast], dma=True)
            gm_deps = [gu_ld[i], xb_ld[tb]]
            if i >= 2:
                gm_deps.append(sil[i - 2])
            gmm[(tb, f)] = S.step("pe", lambda t, i=i: [
                t.matmul(p_gu[i % 2][:], s_wg[i % 3][:, k * 128:(k + 1) * 128],
                         s_xb[:, k * TB:(k + 1) * TB],
                         start=(k == 0), stop=(k == KT_DM - 1))
                for k in range(KT_DM)][-1], deps=gm_deps)
            um_deps = [gu_ld[i], xb_ld[tb]]
            if i >= 2:
                um_deps.append(hb_done[i - 2])
            umm[(tb, f)] = S.step("pe", lambda t, i=i: [
                t.matmul(p_gu2[i % 2][:], s_wu[i % 3][:, k * 128:(k + 1) * 128],
                         s_xb[:, k * TB:(k + 1) * TB],
                         start=(k == 0), stop=(k == KT_DM - 1))
                for k in range(KT_DM)][-1], deps=um_deps)
            sil_deps = [gmm[(tb, f)], LDC]
            if i >= 1:
                sil_deps.append(hb_done[i - 1])
            sil[i] = S.step("s", lambda s, f=f, i=i: s.activation(
                s_gbf[:], p_gu[i % 2][:], AF.Silu, bias=s_bg[:, f:f + 1]), deps=sil_deps)
            hbdeps = [umm[(tb, f)], sil[i]]
            if tb >= 1 and f < 2:
                hbdeps.append(dm_last[tb - 1])
            hb_done[i] = S.step("v", lambda v, f=f, i=i: v.scalar_tensor_tensor(
                s_hb[:, f * TB:(f + 1) * TB], p_gu2[i % 2][:], s_bu[:, f:f + 1],
                s_gbf[:], ALU.add, ALU.mult), deps=hbdeps)
        HB_ALL = hb_done[tb * KT_FF + KT_FF - 1]
        if DEBUG_INTERNALS[0] and tb == 0:
            HB_ALL = S.step("sy", lambda sy: sy.dma_start(out=dbg_hb, in_=s_hb[:]),
                            deps=[HB_ALL], dma=True)
        for m in range(KT_DM):
            im = tb * KT_DM + m
            wdeps = [dmm[(tb, m - 3)]] if m >= 3 else (
                [dm_last[tb - 1]] if tb >= 1 else [])
            wd_ld[im] = S.step("sy", lambda sy, m=m, im=im: sy.dma_start(
                out=s_wd[im % 3][:].rearrange("p (k m) -> p k m", k=KT_FF),
                in_=wd_bf[m].rearrange("(k p) m -> p k m", p=128)),
                deps=list(wdeps) + [wd_cast], dma=True)
            xr_deps = [x2_done[im - 2]] if m >= 2 else (
                [apply_done[(tb - 1, KT_DM - 1)]] if tb >= 1 else [])
            xr_ld[im] = S.step("sy", lambda sy, tb=tb, m=m, im=im: sy.dma_start(
                out=s_xr[im % 2][:],
                in_=x_ln[m * 128:(m + 1) * 128, tb * TB:(tb + 1) * TB]),
                deps=xr_deps, dma=True)
            dm_deps = [wd_ld[im], HB_ALL]
            if m >= 2:
                dm_deps.append(x1_done[im - 2])
            dmm[(tb, m)] = S.step("pe", lambda t, m=m, im=im: [
                t.matmul(p_dn[m % 2][:], s_wd[im % 3][:, k * 128:(k + 1) * 128],
                         s_hb[:, k * TB:(k + 1) * TB],
                         start=(k == 0), stop=(k == KT_FF - 1))
                for k in range(KT_FF)][-1], deps=dm_deps)
            x1_deps = [dmm[(tb, m)], M_READY, LDC]
            if tb >= 1:
                x1_deps.append(apply_done[(tb - 1, m)])
            x1_done[im] = S.step("v", lambda v, tb=tb, m=m: v.scalar_tensor_tensor(
                s_xm2[:, m * TB:(m + 1) * TB], p_dn[m % 2][:], s_bd[:, m:m + 1],
                s_mrep[:, tb * TB:(tb + 1) * TB], ALU.add, ALU.mult),
                deps=x1_deps)
            x2_done[im] = S.step("v", lambda v, m=m, im=im: v.tensor_add(
                s_xm2[:, m * TB:(m + 1) * TB], s_xm2[:, m * TB:(m + 1) * TB],
                s_xr[im % 2][:]), deps=[x1_done[im], xr_ld[im]])
            cdeps = [x2_done[im]]
            if m >= 2:
                cdeps.append(stat_done[im - 2])
            xq = S.step("v", lambda v, m=m, im=im: v.tensor_mul(
                s_xqb[im % 2][:], s_xm2[:, m * TB:(m + 1) * TB],
                s_xm2[:, m * TB:(m + 1) * TB]), deps=cdeps)
            xb2 = S.step("s", lambda s, m=m, im=im: s.copy(
                s_xbf[im % 2][:], s_xm2[:, m * TB:(m + 1) * TB]), deps=cdeps)
            stdeps = [xq, xb2, onesb_set]
            if m == 0 and tb >= 1:
                stdeps.append(vv3[tb - 1])
            stat_done[im] = S.step("pe", lambda t, m=m, im=im: (
                t.matmul(p_st1[:], s_onesb[:, :], s_xbf[im % 2][:],
                         start=(m == 0), stop=(m == KT_DM - 1)),
                t.matmul(p_st2[:], s_onesb[:, :], s_xqb[im % 2][:],
                         start=(m == 0), stop=(m == KT_DM - 1)))[1], deps=stdeps)
        if tb == 0:
            vv3 = {}
        ilast = tb * KT_DM + KT_DM - 1
        v1 = S.step("v", lambda v: v.tensor_scalar_mul(s_mean[:], p_st1[:], 1.0 / DM),
                    deps=[stat_done[ilast]])
        v2 = S.step("v", lambda v: v.tensor_mul(s_m2[:], s_mean[:], s_mean[:]), deps=[v1])
        vv3[tb] = S.step("v", lambda v: v.scalar_tensor_tensor(
            s_var[:], p_st2[:], 1.0 / DM, s_m2[:], ALU.mult, ALU.subtract), deps=[v2])
        v4 = S.step("s", lambda s: s.activation(s_var[:], s_var[:], AF.Ln,
                                                bias=s_eps5[0:1, 0:1]),
                    deps=[vv3[tb], eps5_set])
        v5 = S.step("s", lambda s: s.activation(s_rstd[:], s_var[:], AF.Exp, scale=-0.5),
                    deps=[v4])
        mm1 = S.step("pe", lambda t: t.matmul(p_dn[0][:], s_ones[:, :], s_mean[:],
                                              start=True, stop=True),
                     deps=[v5, x1_done[ilast]])
        ev1 = S.step("v", lambda v: v.tensor_copy(s_mrep2[:], p_dn[0][:]), deps=[mm1])
        mm2 = S.step("pe", lambda t: t.matmul(p_dn[0][:], s_ones[:, :], s_rstd[:],
                                              start=True, stop=True), deps=[ev1, v5])
        ev2 = S.step("v", lambda v: v.tensor_copy(s_rrep2[:], p_dn[0][:]), deps=[mm2])
        for m in range(KT_DM):
            im = tb * KT_DM + m
            adeps = [ev2, x2_done[im]]
            if m >= 2:
                adeps.append(ot_store[(tb, m - 2)])
            elif tb >= 1:
                adeps.append(ot_store[(tb - 1, KT_DM - 2 + m)])
            a1 = S.step("v", lambda v, m=m, im=im: v.tensor_sub(
                s_ot[im % 2][:], s_xm2[:, m * TB:(m + 1) * TB], s_mrep2[:]), deps=adeps)
            apply_done[(tb, m)] = S.step("v", lambda v, m=m, im=im: v.tensor_mul(
                s_ot[im % 2][:], s_ot[im % 2][:], s_rrep2[:]), deps=[a1])
            ot_store[(tb, m)] = S.step("sy", lambda sy, tb=tb, m=m, im=im: sy.dma_start(
                out=out[m * 128:(m + 1) * 128, tb * TB:(tb + 1) * TB],
                in_=s_ot[im % 2][:]), deps=[apply_done[(tb, m)]], dma=True)
        if DEBUG_INTERNALS[0] and tb == 0:
            S.step("sy", lambda sy: sy.dma_start(out=dbg_xm2, in_=s_xm2[:]),
                   deps=[x2_done[tb * KT_DM + KT_DM - 1],
                         xq if False else x2_done[tb * KT_DM + KT_DM - 1]], dma=True)
        dm_last[tb] = dmm[(tb, KT_DM - 1)]

    S.finalize(nc, sems)
    root.close()
    return nc



_BUILT = {}
_CAPTURE = {}


def _get_built(which):
    if which not in _BUILT:
        _BUILT[which] = build_A() if which == "A" else build_B()
    return _BUILT[which]


_U42 = [None]


def _get_u42():
    if _U42[0] is not None:
        return _U42[0]
    import os
    import subprocess
    import sys
    import tempfile
    fd, path = tempfile.mkstemp(suffix=".npy")
    os.close(fd)
    code = ("import os; os.environ['JAX_PLATFORMS']='cpu'; "
            "import numpy, jax, jax.numpy as jnp; "
            "numpy.save(%r, numpy.asarray(jax.random.uniform("
            "jax.random.key(42), (%d, 8), dtype=jnp.float32)))" % (path, NCORE * L))
    env = dict(os.environ)
    env.pop("PYTHONPATH", None)
    env["JAX_PLATFORMS"] = "cpu"
    try:
        subprocess.run([sys.executable, "-c", code], env=env, check=True)
        arr = np.load(path)
    except Exception:
        import jax
        import jax.numpy as jnp
        with jax.default_device(jax.devices("cpu")[0]):
            arr = np.asarray(jax.random.uniform(jax.random.key(42), (NCORE * L, 8),
                                                dtype=jnp.float32))
    finally:
        try:
            os.remove(path)
        except OSError:
            pass
    _U42[0] = arr
    return arr


def kernel(L0_attn_u, L0_attn_W_in, L0_attn_conv_w, L0_attn_conv_b, L0_attn_dt_bias,
           L0_attn_A_log, L0_attn_D_param, L0_attn_norm_w, L0_attn_W_out,
           L0_ffn_router_w, L0_ffn_router_b, L0_ffn_w_gate, L0_ffn_b_gate,
           L0_ffn_w_up, L0_ffn_b_up, L0_ffn_w_down, L0_ffn_b_down):
    f32 = np.float32
    u = np.asarray(L0_attn_u, dtype=f32)
    W_in = np.ascontiguousarray(np.asarray(L0_attn_W_in, f32))
    conv_w = np.asarray(L0_attn_conv_w, f32)
    conv_b = np.asarray(L0_attn_conv_b, f32)
    dt_bias = np.asarray(L0_attn_dt_bias, f32).reshape(64, 1)
    A_log = np.asarray(L0_attn_A_log, f32).reshape(64, 1)
    D_param = np.asarray(L0_attn_D_param, f32).reshape(64, 1)
    norm_w = np.asarray(L0_attn_norm_w, f32).reshape(1, DI)
    W_out = np.ascontiguousarray(np.asarray(L0_attn_W_out, f32))
    router_w = np.ascontiguousarray(np.asarray(L0_ffn_router_w, f32))
    router_b = np.asarray(L0_ffn_router_b, f32).reshape(8, 1)

    conv_w_p = np.ascontiguousarray(conv_w.reshape(34, 128, 4).transpose(1, 0, 2)
                                    .reshape(128, 34 * 4))
    conv_b_p = np.ascontiguousarray(conv_b.reshape(34, 128).T)
    tril_c = -np.triu(np.ones((128, 128), f32))
    sidx = np.arange(128)
    mask_c = np.where(sidx[None, :] >= sidx[:, None], 0.0, NEG_MASK).astype(f32)
    idf_c = np.eye(128, dtype=f32)
    ones_c = np.ones((1, 128), f32)

    u42_T = _get_u42().T

    nc_a = _get_built("A")
    in_maps = []
    for c in range(NCORE):
        in_maps.append({
            "u_T": np.ascontiguousarray(u[c].T),
            "W_in": W_in,
            "conv_w_p": conv_w_p,
            "conv_b_p": conv_b_p,
            "dt_bias": dt_bias,
            "A_log": A_log,
            "D_param": D_param,
            "norm_w": norm_w,
            "W_out": W_out,
            "router_w": router_w,
            "router_b": router_b,
            "u42_T": np.ascontiguousarray(u42_T[:, c * L:(c + 1) * L]),
            "tril_c": tril_c,
            "mask_c": mask_c,
            "idf_c": idf_c,
            "ones_c": ones_c,
        })
    resA = run_bass_kernel_spmd(nc_a, in_maps, core_ids=list(range(NCORE)))
    _CAPTURE['A'] = resA

    smA = np.stack([np.asarray(resA.results[c]["sm"]) for c in range(NCORE)])
    xlnA = [np.asarray(resA.results[c]["x_ln"]) for c in range(NCORE)]

    smp_cores = []
    for c in range(NCORE):
        arr = np.empty((128, 2048), f32)
        for e in range(8):
            for src in range(NCORE):
                cc = (src - c) % NCORE
                blk = smA[src, e].reshape(16, 256)
                arr[e * 16:(e + 1) * 16, cc * 256:(cc + 1) * 256] = blk
        smp_cores.append(arr)

    pad_g = DFF - 5461
    w_gate = np.ascontiguousarray(np.pad(np.asarray(L0_ffn_w_gate, f32),
                                         ((0, 0), (0, pad_g))))
    w_up = np.ascontiguousarray(np.pad(np.asarray(L0_ffn_w_up, f32),
                                       ((0, 0), (0, pad_g))))
    w_down = np.ascontiguousarray(np.pad(np.asarray(L0_ffn_w_down, f32),
                                         ((0, pad_g), (0, 0))))
    b_gate = np.ascontiguousarray(np.pad(np.asarray(L0_ffn_b_gate, f32), (0, pad_g))
                                  .reshape(KT_FF, 128).T)
    b_up = np.ascontiguousarray(np.pad(np.asarray(L0_ffn_b_up, f32), (0, pad_g))
                                .reshape(KT_FF, 128).T)
    b_down = np.ascontiguousarray(np.asarray(L0_ffn_b_down, f32).reshape(KT_DM, 128).T)
    p128 = np.arange(128)
    sel_e = (p128[:, None] // 16 == np.arange(8)[None, :]).astype(f32)
    sel_et = np.ascontiguousarray(sel_e.T)
    sel_j = (p128[:, None] % 16 == np.arange(16)[None, :]).astype(f32)
    bstep_c = np.broadcast_to(
        -np.asarray([2.0 ** (-(k + 2)) for k in range(N_BISECT)], f32), (8, N_BISECT)
    ).copy()

    nc_b = _get_built("B")
    in_maps_b = []
    for c in range(NCORE):
        in_maps_b.append({
            "x_ln": xlnA[c],
            "smp": smp_cores[c],
            "w_gate": w_gate, "b_gate": b_gate,
            "w_up": w_up, "b_up": b_up,
            "w_down": w_down, "b_down": b_down,
            "sel_e": sel_e, "sel_et": sel_et, "sel_j": sel_j,
            "ones_c": ones_c, "bstep_c": bstep_c,
        })
    resB = run_bass_kernel_spmd(nc_b, in_maps_b, core_ids=list(range(NCORE)))
    _CAPTURE['B'] = resB

    out = np.empty((NCORE, L, DM), f32)
    for c in range(NCORE):
        out[c] = np.asarray(resB.results[c]["out"]).T
    return out


# version 04
# speedup vs baseline: 1.1599x; 1.1599x over previous
import numpy as np
from contextlib import ExitStack

import concourse.bass as bass
import concourse.mybir as mybir
from concourse.bass_utils import run_bass_kernel_spmd

F32 = mybir.dt.float32
BF16 = mybir.dt.bfloat16
AF = mybir.ActivationFunctionType
ALU = mybir.AluOpType

NCORE = 8
DM = 2048
DI = 4096
PROJ = 8512
L = 4096
CH = 128
NCH = L // CH
NH = 64
HD = 64
HG = 16
NHG = NH // HG
DFF = 5504
KT_DM = DM // 128
KT_DI = DI // 128
KT_FF = DFF // 128
XBC_T = 34
NEG_MASK = -60000.0
N_BISECT = 34


class Sched:

    ENGINES = ("pe", "v", "s", "sy", "gp")

    def __init__(self):
        self.steps = {e: [] for e in self.ENGINES}
        self.counts = {e: 0 for e in self.ENGINES}

    def step(self, eng, emit, deps=(), dma=False):
        inc = 16 if dma else 1
        kept = {}
        for d in deps:
            if d is None:
                continue
            de, dc, ddma = d
            if de == eng and not ddma:
                continue
            kept[de] = max(kept.get(de, 0), dc)
        self.steps[eng].append((emit, kept, inc))
        self.counts[eng] += inc
        return (eng, self.counts[eng], dma)

    def fence(self):
        return [(e, self.counts[e], True) for e in self.ENGINES if self.counts[e] > 0]

    def replay(self, eng, obj, sems):
        for emit, kept, inc in self.steps[eng]:
            for de, dc in kept.items():
                obj.wait_ge(sems[de], dc)
            emit(obj).then_inc(sems[eng], inc)

    def finalize(self, nc, sems):
        with nc.Block() as block:
            @block.tensor
            def _(t):
                self.replay("pe", t, sems)

            @block.vector
            def _(v):
                self.replay("v", v, sems)

            @block.scalar
            def _(s):
                self.replay("s", s, sems)

            @block.gpsimd
            def _(g):
                self.replay("gp", g, sems)

            @block.sync
            def _(sy):
                self.replay("sy", sy, sems)
                for e in self.ENGINES:
                    if self.counts[e] > 0:
                        sy.wait_ge(sems[e], self.counts[e])


def bc_tail(ap_, n_last):
    return ap_.to_broadcast([ap_.shape[0], ap_.shape[1], n_last])


def bc_mid(ap_, n_mid):
    return ap_.unsqueeze(1).to_broadcast([ap_.shape[0], n_mid, ap_.shape[1]])


def r3(ap_, h):
    return ap_.rearrange("p (h q) -> p h q", h=h)



DEBUG_INTERNALS = [False]


def build_A():
    nc = bass.Bass("TRN2", target_bir_lowering=False)
    din = lambda n, s, d=F32: nc.dram_tensor(n, list(s), d, kind="ExternalInput").ap()
    dout = lambda n, s, d=F32: nc.dram_tensor(n, list(s), d, kind="ExternalOutput").ap()
    dint = lambda n, s, d=F32: nc.dram_tensor(
        n, list(s), d, kind=("ExternalOutput" if DEBUG_INTERNALS[0] else "Internal")).ap()

    u_T = din("u_T", (DM, L))
    W_in = din("W_in", (DM, PROJ))
    conv_w_p = din("conv_w_p", (128, 34 * 4))
    conv_b_p = din("conv_b_p", (128, 34))
    dt_bias = din("dt_bias", (64, 1))
    A_log = din("A_log", (64, 1))
    D_param = din("D_param", (64, 1))
    norm_w = din("norm_w", (1, DI))
    W_out = din("W_out", (DI, DM))
    router_w = din("router_w", (DM, 8))
    router_b = din("router_b", (8, 1))
    u42_T = din("u42_T", (8, L))
    tril_c = din("tril_c", (128, 128))
    mask_c = din("mask_c", (128, 128))
    idf_c = din("idf_c", (128, 128))
    ones_c = din("ones_c", (1, 128))

    x_ln = dout("x_ln", (DM, L))
    sm_out = dout("sm", (8, L))

    x_dram = dint("x_dram", (DI, L), BF16)
    dbg_xs = dint("dbg_xs", (128, DI), BF16)
    dbg_gm = dint("dbg_gm", (128, HG * 128), BF16)
    dbg_y = dint("dbg_y", (128, DI))
    dbg_st = dint("dbg_st", (128, DI))
    z_dram = dint("z_dram", (L, DI))
    acum_dram = dint("acum_dram", (NCH, NH, CH))
    alast_dram = dint("alast_dram", (NCH, 64))
    yn_dram = dint("yn_dram", (L, DI), BF16)
    wo_bf = dint("wo_bf", (KT_DM, DI, 128), BF16)

    S = Sched()
    root = ExitStack()
    sems = {e: root.enter_context(nc.semaphore(f"sem_{e}")) for e in Sched.ENGINES}
    sb = lambda st, n, s, d=F32: st.enter_context(nc.sbuf_tensor(n, list(s), d))
    pt = lambda st, n, s, d=F32: st.enter_context(nc.psum_tensor(n, list(s), d))

    P = root
    s_tril = sb(P, "s_tril", (128, 128))
    s_mask = sb(P, "s_mask", (128, 128))
    s_idf = sb(P, "s_idf", (128, 128))
    s_idb = sb(P, "s_idb", (128, 128), BF16)
    s_ones = sb(P, "s_ones", (1, 128))
    s_onesb = sb(P, "s_onesb", (128, 1), BF16)
    s_ones8 = sb(P, "s_ones8", (8, 1))
    s_convw = sb(P, "s_convw", (128, 34 * 4))
    s_convb = sb(P, "s_convb", (128, 34))
    s_dtb = sb(P, "s_dtb", (64, 1))
    s_alog = sb(P, "s_alog", (64, 1))
    s_dpar = sb(P, "s_dpar", (64, 1))
    s_rb = sb(P, "s_rb", (8, 1))
    sBc = sb(P, "sBc", (128, L), BF16)
    sCc = sb(P, "sCc", (128, L), BF16)

    ld = None
    for t, d in ((s_tril, tril_c), (s_mask, mask_c), (s_idf, idf_c),
                 (s_ones, ones_c), (s_convw, conv_w_p), (s_convb, conv_b_p),
                 (s_dtb, dt_bias), (s_alog, A_log), (s_dpar, D_param),
                 (s_rb, router_b)):
        ld = S.step("sy", lambda sy, t=t, d=d: sy.dma_start(out=t[:], in_=d), dma=True)
    LDC = ld
    idb_cast = S.step("v", lambda v: v.tensor_copy(s_idb[:], s_idf[:]), deps=[LDC])
    onesb_set = S.step("v", lambda v: v.memset(s_onesb[:], 1.0))
    ones8_set = S.step("v", lambda v: v.memset(s_ones8[:], 1.0))
    s_eps5 = sb(P, "s_eps5", (128, 1))
    s_eps10 = sb(P, "s_eps10", (128, 1))
    eps5_set = S.step("v", lambda v: v.memset(s_eps5[:], 1e-5))
    eps10_set = S.step("v", lambda v: v.memset(s_eps10[:], 1e-10))

    P12 = ExitStack()
    s_dtraw = P12.enter_context(nc.sbuf_tensor("s_dtraw", [64, L], F32, side="right"))
    P1 = ExitStack()
    su = sb(P1, "su", (128, KT_DM * L), BF16)
    s_wblk = [sb(P1, f"s_wblk{i}", (128, KT_DM * 128), BF16) for i in range(2)]
    p_mm = [pt(P1, f"p_mm{i}", (128, 512)) for i in range(4)]
    P1a = ExitStack()
    s_stage = [sb(P1a, f"s_stage{i}", (128, 4104), BF16) for i in range(2)]
    s_cacc = sb(P1a, "s_cacc", (128, L))

    stage_zero = S.step("v", lambda v: [v.memset(s_stage[0][:, 0:3], 0.0),
                                        v.memset(s_stage[1][:, 0:3], 0.0)][-1])
    u_last = None
    for k in range(KT_DM):
        u_last = S.step("gp", lambda g, k=k: g.dma_start(
            out=su[:, k * L:(k + 1) * L], in_=u_T[k * 128:(k + 1) * 128, :]), dma=True)
    U_ALL = u_last

    def wblk_load(i, c0, cw, deps):
        return S.step("gp", lambda g, i=i, c0=c0, cw=cw: g.dma_start(
            out=s_wblk[i % 2][:, 0:KT_DM * cw].rearrange("p (k m) -> p k m", k=KT_DM),
            in_=W_in[:, c0:c0 + cw].rearrange("(k p) m -> p k m", p=128)),
            deps=deps, dma=True)

    mm_done = {}
    ev_done = {}
    g = 0
    w_dep = {0: wblk_load(0, DI + 4352, 64, [U_ALL])}
    for nt in range(8):
        mm_done[g] = S.step("pe", lambda t, nt=nt, pb=p_mm[g % 4]: [
            t.matmul(pb[0:64, :], s_wblk[0][:, k * 64:(k + 1) * 64],
                     su[:, k * L + nt * 512:k * L + nt * 512 + 512],
                     start=(k == 0), stop=(k == KT_DM - 1))
            for k in range(KT_DM)][-1], deps=[w_dep[0], U_ALL])
        ev_done[g] = S.step("s", lambda s, nt=nt, pb=p_mm[g % 4]: s.copy(
            s_dtraw[:, nt * 512:(nt + 1) * 512], pb[0:64, :]), deps=[mm_done[g]])
        g += 1
    w_dep[1] = wblk_load(1, DI, 128, [mm_done[7]])

    stage_free = {}
    spill_prev = None
    for j in range(XBC_T):
        wi = j + 1
        for nt in range(8):
            deps = [w_dep[wi]]
            if g >= 4:
                deps.append(ev_done[g - 4])
            if nt == 0 and stage_free.get(j % 2) is not None:
                deps.append(stage_free[j % 2])
            mm_done[g] = S.step("pe", lambda t, nt=nt, wi=wi, pb=p_mm[g % 4]: [
                t.matmul(pb[:], s_wblk[wi % 2][:, k * 128:(k + 1) * 128],
                         su[:, k * L + nt * 512:k * L + nt * 512 + 512],
                         start=(k == 0), stop=(k == KT_DM - 1))
                for k in range(KT_DM)][-1], deps=deps)
            ev_deps = [mm_done[g], stage_zero]
            if stage_free.get(j % 2) is not None:
                ev_deps.append(stage_free[j % 2])
            ev_done[g] = S.step("s", lambda s, j=j, nt=nt, pb=p_mm[g % 4]: s.copy(
                s_stage[j % 2][:, 3 + nt * 512:3 + nt * 512 + 512], pb[:]), deps=ev_deps)
            g += 1
        if wi + 1 <= XBC_T:
            w_dep[wi + 1] = wblk_load(wi + 1, DI + wi * 128, 128, [mm_done[g - 1]])

        def emit_conv(v, j=j):
            stg = s_stage[j % 2]
            v.tensor_scalar(s_cacc[:], stg[:, 0:L], s_convw[:, j * 4:j * 4 + 1],
                            s_convb[:, j:j + 1], ALU.mult, ALU.add)
            last = None
            for w in (1, 2, 3):
                last = v.scalar_tensor_tensor(
                    s_cacc[:], stg[:, w:w + L], s_convw[:, j * 4 + w:j * 4 + w + 1],
                    s_cacc[:], ALU.mult, ALU.add)
            return last
        conv_deps = [ev_done[g - 1], LDC]
        if j > 0:
            conv_deps.append(prev_sil)
        conv = S.step("v", emit_conv, deps=conv_deps)
        if j < 32:
            sil_deps = [conv]
            if spill_prev is not None:
                sil_deps.append(spill_prev)
            sil = S.step("s", lambda s, j=j: s.activation(
                s_stage[j % 2][:, 4:4 + L], s_cacc[:], AF.Silu), deps=sil_deps)
            spill_prev = S.step("sy", lambda sy, j=j: sy.dma_start(
                out=x_dram[j * 128:(j + 1) * 128, :], in_=s_stage[j % 2][:, 4:4 + L]),
                deps=[sil], dma=True)
            stage_free[j % 2] = spill_prev
            prev_sil = sil
        else:
            dstt = sBc if j == 32 else sCc
            sil = S.step("s", lambda s, dstt=dstt: s.activation(dstt[:], s_cacc[:],
                                                                AF.Silu), deps=[conv])
            stage_free[j % 2] = sil
            prev_sil = sil
    BC_READY = stage_free[33 % 2]
    X_SPILLED = spill_prev

    P1a.close()
    s_wz = [sb(P1, f"s_wz{i}", (128, KT_DM * 512), BF16) for i in range(2)]
    s_zst = [sb(P1, f"s_zst{i}", (128, 512)) for i in range(2)]
    zst_free = {}
    wz_dep = {}
    for nb in range(8):
        deps = [stage_free[0], stage_free[1]] if nb < 2 else [mm_done[("z", nb - 2, 31)]]
        wz_dep[nb] = S.step("gp", lambda gp, nb=nb: gp.dma_start(
            out=s_wz[nb % 2][:].rearrange("p (k m) -> p k m", k=KT_DM),
            in_=W_in[:, nb * 512:(nb + 1) * 512].rearrange("(k p) m -> p k m", p=128)),
            deps=deps, dma=True)
        for tt in range(32):
            deps = [wz_dep[nb]]
            if g >= 4:
                deps.append(ev_done[g - 4])
            mm_done[("z", nb, tt)] = S.step("pe", lambda t, nb=nb, tt=tt, pb=p_mm[g % 4]: [
                t.matmul(pb[:], su[:, k * L + tt * 128:k * L + tt * 128 + 128],
                         s_wz[nb % 2][:, k * 512:(k + 1) * 512],
                         start=(k == 0), stop=(k == KT_DM - 1))
                for k in range(KT_DM)][-1], deps=deps)
            mm_done[g] = mm_done[("z", nb, tt)]
            zi = (nb * 32 + tt) % 2
            ev_deps = [mm_done[g]]
            if zst_free.get(zi) is not None:
                ev_deps.append(zst_free[zi])
            ev_done[g] = S.step("s", lambda s, zi=zi, pb=p_mm[g % 4]: s.copy(
                s_zst[zi][:], pb[:]), deps=ev_deps)
            zst_free[zi] = S.step("sy", lambda sy, nb=nb, tt=tt, zi=zi: sy.dma_start(
                out=z_dram[tt * 128:(tt + 1) * 128, nb * 512:(nb + 1) * 512],
                in_=s_zst[zi][:]), deps=[ev_done[g]], dma=True)
            g += 1
    P1_END = S.fence()
    P1.close()

    P23 = ExitStack()
    s_dtT = sb(P23, "s_dtT", (128, NCH * 64), BF16)
    s_acA = sb(P23, "s_acA", (128, NCH * 64))
    s_ds = sb(P23, "s_ds", (128, NCH * 64), BF16)
    s_dprep = sb(P23, "s_dprep", (128, 64))
    s_od = sb(P23, "s_od", (128, NCH * 64), BF16)
    s_dA = sb(P23, "s_dA", (128, NCH * 64))
    sBT = sb(P23, "sBT", (128, NCH * 128), BF16)
    s_nwrep = sb(P23, "s_nwrep", (128, DI), BF16)
    P2 = ExitStack()
    s_dt = sb(P2, "s_dt", (64, L))
    s_a = sb(P2, "s_a", (64, L))
    s_aT = sb(P2, "s_aT", (128, NCH * 64))
    s_acT = sb(P2, "s_acT", (64, NCH * 128))
    s_nw = sb(P2, "s_nw", (1, DI))
    s_alf = sb(P2, "s_alf", (1, NCH * 64))
    s_dprow = sb(P2, "s_dprow", (1, 64))
    p_tr = pt(P2, "p_tr", (128, 512))
    p_sm2 = pt(P2, "p_sm2", (128, 64))
    p_sm = pt(P2, "p_sm", (128, 64))

    nw_ld = S.step("sy", lambda sy: sy.dma_start(out=s_nw[:], in_=norm_w),
                   deps=P1_END, dma=True)
    dprow_ld = S.step("sy", lambda sy: sy.dma_start(
        out=s_dprow[:], in_=D_param.rearrange("a b -> b a")), deps=P1_END, dma=True)
    c1 = S.step("s", lambda s: s.activation(s_dt[:], s_dtraw[:], AF.Identity,
                                            bias=s_dtb[:, 0:1]), deps=P1_END + [LDC])
    c2 = S.step("s", lambda s: s.activation(s_dtraw[:], s_dt[:], AF.Abs), deps=[c1])
    c3 = S.step("s", lambda s: s.activation(s_dtraw[:], s_dtraw[:], AF.Exp, scale=-1.0),
                deps=[c2])
    c4 = S.step("s", lambda s: s.activation(s_dtraw[:], s_dtraw[:], AF.Ln, bias=1.0),
                deps=[c3])
    c5 = S.step("v", lambda v: v.tensor_relu(s_a[:], s_dt[:]), deps=[c1])
    c6 = S.step("v", lambda v: v.tensor_add(s_dt[:], s_a[:], s_dtraw[:]), deps=[c4, c5])
    e1 = S.step("s", lambda s: s.activation(s_alog[:], s_alog[:], AF.Exp), deps=[LDC])
    a1 = S.step("v", lambda v: v.tensor_scalar(s_a[:], s_dt[:], s_alog[:, 0:1], None,
                                               ALU.mult), deps=[c6, e1])

    prev_ev = None
    for (src, dst, dd) in ((s_a, s_aT, a1), (s_dt, s_dtT, c6)):
        for b in range(4):
            deps = [dd] + ([prev_ev] if prev_ev else [])
            mm = S.step("pe", lambda t, src=src, b=b: [
                t.transpose(p_tr[:, q * 64:(q + 1) * 64],
                            src[:, (b * 8 + q) * 128:(b * 8 + q + 1) * 128],
                            s_idf[0:64, 0:64])
                for q in range(8)][-1], deps=deps)
            prev_ev = S.step("v", lambda v, dst=dst, b=b: v.tensor_copy(
                dst[:, b * 512:(b + 1) * 512], p_tr[:]), deps=[mm])
    for b in range(4):
        mm = S.step("pe", lambda t, b=b: [
            t.matmul(p_tr[:, q * 64:(q + 1) * 64], s_tril[:],
                     s_aT[:, (b * 8 + q) * 64:(b * 8 + q + 1) * 64],
                     start=True, stop=True)
            for q in range(8)][-1], deps=[prev_ev, LDC])
        prev_ev = S.step("v", lambda v, b=b: v.tensor_copy(
            s_acA[:, b * 512:(b + 1) * 512], p_tr[:]), deps=[mm])
    CUM = prev_ev
    for b in range(8):
        mm = S.step("pe", lambda t, b=b: [
            t.transpose(p_tr[0:64, q * 128:(q + 1) * 128],
                        s_acA[:, (b * 4 + q) * 64:(b * 4 + q + 1) * 64], s_idf[:])
            for q in range(4)][-1], deps=[CUM, prev_ev])
        prev_ev = S.step("v", lambda v, b=b: v.tensor_copy(
            s_acT[:, b * 512:(b + 1) * 512], p_tr[0:64, :]), deps=[mm])
    ACT_EV = prev_ev
    AC_SPILL = S.step("sy", lambda sy: sy.dma_start(
        out=acum_dram.rearrange("c h l -> h c l"),
        in_=s_acT[:].rearrange("p (c l) -> p c l", c=NCH)),
        deps=[ACT_EV], dma=True)
    al_sp = S.step("sy", lambda sy: sy.dma_start(
        out=alast_dram.rearrange("c h -> h c").unsqueeze(2),
        in_=s_acT[:].rearrange("p (c l) -> p c l", c=NCH)[:, :, 127:128],
        allow_slow_non_contiguous=True), deps=[ACT_EV], dma=True)
    al_ld = S.step("sy", lambda sy: sy.dma_start(
        out=s_alf[:], in_=alast_dram.rearrange("c h -> (c h)").unsqueeze(0),
        allow_slow_non_contiguous=True), deps=[al_sp], dma=True)
    al_ev = None
    for b in range(4):
        mm = S.step("pe", lambda t, b=b: t.matmul(
            p_tr[:], s_ones[:, :], s_alf[:, b * 512:(b + 1) * 512],
            start=True, stop=True), deps=[al_ld, ACT_EV] + ([al_ev] if al_ev else []))
        al_ev = S.step("v", lambda v, b=b: v.tensor_copy(
            s_dA[:, b * 512:(b + 1) * 512], p_tr[:]), deps=[mm])
    od1 = S.step("s", lambda s: s.activation(s_od[:], s_acA[:], AF.Exp), deps=[CUM])
    ds1 = S.step("v", lambda v: v.tensor_sub(s_aT[:, 0:NCH * 64], s_dA[:], s_acA[:]),
                 deps=[al_ev])
    ds2 = S.step("s", lambda s: s.activation(s_ds[:], s_aT[:, 0:NCH * 64], AF.Exp),
                 deps=[ds1])
    dA1 = S.step("s", lambda s: s.activation(s_dA[:], s_dA[:], AF.Exp), deps=[ds1])
    dtds = S.step("v", lambda v: v.tensor_mul(s_ds[:], s_ds[:], s_dtT[:]),
                  deps=[ds2])
    dp_mm = S.step("pe", lambda t: t.matmul(p_sm2[:, 0:64], s_ones[:, :],
                                            s_dprow[:], start=True, stop=True),
                   deps=[dprow_ld, CUM])
    dp_ev = S.step("v", lambda v: v.tensor_copy(s_dprep[:], p_sm2[:, 0:64]),
                   deps=[dp_mm])
    nw_ev = None
    for i in range(8):
        mm = S.step("pe", lambda t, i=i: t.matmul(
            p_tr[:], s_ones[:, :], s_nw[:, i * 512:(i + 1) * 512],
            start=True, stop=True), deps=[nw_ld, al_ev] + ([nw_ev] if nw_ev else []))
        nw_ev = S.step("v", lambda v, i=i: v.tensor_copy(
            s_nwrep[:, i * 512:(i + 1) * 512], p_tr[:]), deps=[mm])
    bt_prev = nw_ev
    for b in range(8):
        mm = S.step("pe", lambda t, b=b: [
            t.transpose(p_tr.ap().bitcast(BF16)[:, q * 128:(q + 1) * 128],
                        sBc[:, (b * 4 + q) * 128:(b * 4 + q + 1) * 128], s_idb[:])
            for q in range(4)][-1], deps=[idb_cast, bt_prev] + (P1_END if b == 0 else []))
        bt_prev = S.step("v", lambda v, b=b: v.tensor_copy(
            sBT[:, b * 512:(b + 1) * 512], p_tr.ap().bitcast(BF16)[:, 0:512]), deps=[mm])
    P2_END = S.fence()
    P2.close()
    P12.close()

    P3 = ExitStack()
    s_xT = [sb(P3, f"s_xT{i}", (128, DI), BF16) for i in range(2)]
    s_xs = sb(P3, "s_xs", (128, DI), BF16)
    s_xds = sb(P3, "s_xds", (128, DI), BF16)
    s_xcf = sb(P3, "s_xcf", (128, DI), BF16)
    s_arep = [sb(P3, f"s_arep{i}", (128, HG * 128)) for i in range(2)]
    s_D = [sb(P3, f"s_D{i}", (128, HG * 128)) for i in range(2)]
    s_Gm = [sb(P3, f"s_Gm{i}", (128, HG * 128), BF16) for i in range(2)]
    s_Gt = sb(P3, "s_Gt", (128, 128), BF16)
    s_St = sb(P3, "s_St", (128, DI))
    s_Stmp = sb(P3, "s_Stmp", (128, HG * HD))
    s_Sbf = sb(P3, "s_Sbf", (128, DI), BF16)
    s_Y = sb(P3, "s_Y", (128, DI))
    s_zT = sb(P3, "s_zT", (128, DI))
    s_ss = sb(P3, "s_ss", (128, 1))
    s_rr = sb(P3, "s_rr", (128, 1))
    p_gt = pt(P3, "p_gt", (128, 128))
    p_yd = pt(P3, "p_yd", (128, HG * HD))
    p_yo = pt(P3, "p_yo", (128, HG * HD))
    p_cs = pt(P3, "p_cs", (128, HG * HD))

    xt_ld = {}
    z_ld = {}
    arep_ld = {}
    arep_free = {}
    pe_hg = {}
    eD_done = {}
    sbf_done = {}
    sq_done = {}
    yf_done = {}
    xs_done = {}
    hg_evac = {}
    yn_spill = {}
    prev_gt_ev = None
    init_S = S.step("v", lambda v: v.memset(s_St[:], 0.0), deps=P2_END)

    for ch in range(NCH):
        deps = ([xs_done[ch - 2], pe_hg[(ch - 2) * NHG + NHG - 1]]
                if ch >= 2 else [X_SPILLED] + P2_END)
        xt_ld[ch] = S.step("sy", lambda sy, ch=ch: sy.dma_start_transpose(
            s_xT[ch % 2][:], x_dram[:, ch * CH:(ch + 1) * CH]), deps=deps, dma=True)
        deps = [sq_done[ch - 1]] if ch >= 1 else P2_END
        z_ld[ch] = S.step("sy", lambda sy, ch=ch: sy.dma_start(
            out=s_zT[:], in_=z_dram[ch * 128:(ch + 1) * 128, :]), deps=deps, dma=True)

        x1 = S.step("v", lambda v, ch=ch: v.tensor_tensor(
            r3(s_xs[:], NH), r3(s_xT[ch % 2][:], NH),
            bc_tail(s_dtT[:, ch * 64:(ch + 1) * 64], HD), ALU.mult),
            deps=[xt_ld[ch]] + ([hg_evac[(ch - 1) * NHG + NHG - 1],
                                yn_spill[ch - 1]] if ch else []))
        x2 = S.step("v", lambda v, ch=ch: v.tensor_tensor(
            r3(s_xds[:], NH), r3(s_xT[ch % 2][:], NH),
            bc_tail(s_ds[:, ch * 64:(ch + 1) * 64], HD), ALU.mult),
            deps=[xt_ld[ch], dtds] + ([hg_evac[(ch - 1) * NHG + NHG - 1]] if ch else []))
        x3 = S.step("v", lambda v, ch=ch: v.tensor_tensor(
            r3(s_xcf[:], NH), r3(s_xT[ch % 2][:], NH),
            bc_tail(s_dprep[:], HD), ALU.mult),
            deps=[xt_ld[ch], dp_ev] + ([hg_evac[(ch - 1) * NHG + NHG - 1]] if ch else []))
        xs_done[ch] = x3
        if DEBUG_INTERNALS[0] and ch == 1:
            x3 = S.step("sy", lambda sy: sy.dma_start(out=dbg_xs, in_=s_xs[:]),
                        deps=[x3], dma=True)
            xs_done[ch] = x3

        gt_deps = ([prev_gt_ev] if prev_gt_ev else []) + (P2_END if ch == 0 else [])
        gt_mm = S.step("pe", lambda t, ch=ch: t.matmul(
            p_gt[:], sBc[:, ch * CH:(ch + 1) * CH], sCc[:, ch * CH:(ch + 1) * CH],
            start=True, stop=True), deps=gt_deps)
        gt_ev = S.step("s", lambda s: s.copy(s_Gt[:], p_gt[:]), deps=[gt_mm])
        prev_gt_ev = gt_ev

        for hg in range(NHG):
            idx = ch * NHG + hg
            deps = [AC_SPILL]
            if idx >= 2:
                deps.append(arep_free[idx - 2])
            arep_ld[idx] = S.step("gp", lambda gp, ch=ch, hg=hg, idx=idx: gp.dma_start(
                out=s_arep[idx % 2][:],
                in_=acum_dram[ch, hg * HG:(hg + 1) * HG, :]
                .rearrange("h l -> (h l)").unsqueeze(0).partition_broadcast(128)
                .squeeze(1)), deps=deps, dma=True)
            d1 = S.step("v", lambda g, idx=idx: g.tensor_tensor(
                r3(s_D[idx % 2][:], HG), r3(s_arep[idx % 2][:], HG),
                bc_mid(s_mask[:], HG), ALU.add),
                deps=[arep_ld[idx], LDC] +
                ([eD_done[idx - 2]] if idx >= 2 else []))
            d2 = S.step("v", lambda v, ch=ch, hg=hg, idx=idx: v.tensor_tensor(
                r3(s_D[idx % 2][:], HG), r3(s_D[idx % 2][:], HG),
                bc_tail(s_acA[:, ch * 64 + hg * HG:ch * 64 + (hg + 1) * HG], 128),
                ALU.subtract), deps=[d1])
            eDap = s_arep[idx % 2].ap().bitcast(BF16)[:, 0:HG * 128]
            eD = S.step("s", lambda s, eDap=eDap, idx=idx: s.activation(
                eDap, s_D[idx % 2][:], AF.Exp), deps=[d2])
            eD_done[idx] = eD
            gm = S.step("v", lambda v, eDap=eDap, idx=idx: v.tensor_tensor(
                r3(s_Gm[idx % 2][:], HG), r3(eDap, HG), bc_mid(s_Gt[:], HG), ALU.mult),
                deps=[eD, gt_ev])
            arep_free[idx] = gm

            def emit_pe(t, ch=ch, hg=hg):
                idx = ch * NHG + hg
                for hh in range(HG):
                    h = hg * HG + hh
                    t.matmul(p_yd[:, hh * HD:(hh + 1) * HD],
                             s_Gm[idx % 2][:, hh * 128:(hh + 1) * 128],
                             s_xs[:, h * HD:(h + 1) * HD],
                             start=(hh % 8 == 0), stop=False)
                for q in range(2):
                    t.matmul(p_yd[:, q * 512:(q + 1) * 512], s_idb[:],
                             s_xcf[:, hg * 1024 + q * 512:hg * 1024 + q * 512 + 512],
                             start=False, stop=True)
                if ch > 0:
                    for q in range(2):
                        t.matmul(p_yo[:, q * 512:(q + 1) * 512],
                                 sCc[:, ch * CH:(ch + 1) * CH],
                                 s_Sbf[:, hg * 1024 + q * 512:hg * 1024 + q * 512 + 512],
                                 start=True, stop=True)
                last = None
                for q in range(2):
                    last = t.matmul(p_cs[:, q * 512:(q + 1) * 512],
                                    sBT[:, ch * CH:(ch + 1) * CH],
                                    s_xds[:, hg * 1024 + q * 512:hg * 1024 + q * 512 + 512],
                                    start=True, stop=True)
                return last
            pe_deps = [gm, x3, idb_cast]
            if ch > 0:
                pe_deps.append(sbf_done[(ch - 1) * NHG + hg])
            if idx >= 1:
                pe_deps.append(hg_evac[idx - 1])
            if DEBUG_INTERNALS[0] and ch == 1 and hg == 0:
                gm = S.step("sy", lambda sy, idx=idx: sy.dma_start(
                    out=dbg_gm, in_=s_Gm[idx % 2][:]), deps=[gm], dma=True)
            pe_done = S.step("pe", emit_pe, deps=pe_deps)
            pe_hg[idx] = pe_done

            if ch > 0:
                y1 = S.step("v", lambda v, ch=ch, hg=hg: v.tensor_tensor(
                    r3(s_Stmp[:], HG), r3(p_yo[:], HG),
                    bc_tail(s_od[:, ch * 64 + hg * HG:ch * 64 + (hg + 1) * HG], HD),
                    ALU.mult), deps=[pe_done])
                y2 = S.step("v", lambda v, hg=hg: v.tensor_add(
                    s_Y[:, hg * 1024:(hg + 1) * 1024], p_yd[:], s_Stmp[:]), deps=[y1])
            else:
                y2 = S.step("v", lambda v, hg=hg: v.tensor_copy(
                    s_Y[:, hg * 1024:(hg + 1) * 1024], p_yd[:]), deps=[pe_done])
            st1 = S.step("v", lambda g, ch=ch, hg=hg: g.tensor_tensor(
                r3(s_Stmp[:], HG), r3(s_St[:, hg * 1024:(hg + 1) * 1024], HG),
                bc_tail(s_dA[:, ch * 64 + hg * HG:ch * 64 + (hg + 1) * HG], HD),
                ALU.mult), deps=[y2, init_S, dA1])
            st2 = S.step("v", lambda v, hg=hg: v.tensor_add(
                s_St[:, hg * 1024:(hg + 1) * 1024], s_Stmp[:], p_cs[:]), deps=[st1])
            hg_evac[idx] = st2
            sbf_done[idx] = S.step("s", lambda s, hg=hg: s.copy(
                s_Sbf[:, hg * 1024:(hg + 1) * 1024],
                s_St[:, hg * 1024:(hg + 1) * 1024]), deps=[st2])

        if DEBUG_INTERNALS[0] and ch == 1:
            hg_evac[ch * NHG + NHG - 1] = S.step(
                "sy", lambda sy: sy.dma_start(out=dbg_y, in_=s_Y[:]),
                deps=[hg_evac[ch * NHG + NHG - 1]], dma=True)
            hg_evac[ch * NHG + NHG - 1] = S.step(
                "sy", lambda sy: sy.dma_start(out=dbg_st, in_=s_St[:]),
                deps=[hg_evac[ch * NHG + NHG - 1]], dma=True)
        sil = S.step("s", lambda s: s.activation(s_zT[:], s_zT[:], AF.Silu),
                     deps=[z_ld[ch]] + ([yf_done[ch - 1]] if ch else []))
        yf = S.step("v", lambda v: v.tensor_mul(s_Y[:], s_Y[:], s_zT[:]),
                    deps=[sil, hg_evac[ch * NHG + NHG - 1], od1, ds2])
        yf_done[ch] = yf
        sq = S.step("s", lambda s: s.activation(s_zT[:], s_Y[:], AF.Square,
                                                accum_out=s_ss[:, 0:1]), deps=[yf])
        sq_done[ch] = sq
        rr1 = S.step("s", lambda s: s.activation(s_ss[:], s_ss[:], AF.Ln,
                                                 scale=1.0 / DI, bias=s_eps5[:, 0:1]),
                     deps=[sq, eps5_set])
        rr2 = S.step("s", lambda s: s.activation(s_rr[:], s_ss[:], AF.Exp, scale=-0.5),
                     deps=[rr1])
        yn1 = S.step("v", lambda v: v.tensor_scalar(s_Y[:], s_Y[:], s_rr[:, 0:1], None,
                                                    ALU.mult), deps=[rr2])
        yn_deps = [yn1, nw_ev]
        if ch:
            yn_deps.append(yn_spill[ch - 1])
        yn2 = S.step("v", lambda v: v.tensor_mul(s_xcf[:], s_Y[:], s_nwrep[:]),
                     deps=yn_deps)
        yn_spill[ch] = S.step("sy", lambda sy, ch=ch: sy.dma_start(
            out=yn_dram[ch * 128:(ch + 1) * 128, :], in_=s_xcf[:]), deps=[yn2], dma=True)

    P3_END = S.fence()
    P3.close()
    P23.close()

    P4 = ExitStack()
    TGT = 512
    NTG = L // TGT
    s_ynT = [sb(P4, f"s_ynT{i}", (128, KT_DI * TGT), BF16) for i in range(2)]
    s_wo = [sb(P4, f"s_wo{i}", (128, KT_DI * 128), BF16) for i in range(2)]
    s_xmid = sb(P4, "s_xmid", (128, KT_DM * TGT))
    s_ut = [sb(P4, f"s_ut{i}", (128, TGT)) for i in range(2)]
    s_xbf = [sb(P4, f"s_xbf{i}", (128, TGT), BF16) for i in range(2)]
    s_xqb = [sb(P4, f"s_xqb{i}", (128, TGT), BF16) for i in range(2)]
    s_mean = sb(P4, "s_mean", (1, TGT))
    s_var = sb(P4, "s_var", (1, TGT))
    s_m2 = sb(P4, "s_m2", (1, TGT))
    s_rstd = sb(P4, "s_rstd", (1, TGT))
    s_mrep = sb(P4, "s_mrep", (128, TGT))
    s_rrep = sb(P4, "s_rrep", (128, TGT))
    s_xlt = [sb(P4, f"s_xlt{i}", (128, TGT)) for i in range(2)]
    s_xlb = [sb(P4, f"s_xlb{i}", (128, TGT), BF16) for i in range(2)]
    s_rwb = sb(P4, "s_rwb", (128, KT_DM * 8), BF16)
    s_hsc = sb(P4, "s_hsc", (8, L))
    s_u42 = sb(P4, "s_u42", (8, L))
    s_dinv = sb(P4, "s_dinv", (1, L))
    p_o = [pt(P4, f"p_o{i}", (128, TGT)) for i in range(2)]
    p_st1 = pt(P4, "p_st1", (1, TGT))
    p_st2 = pt(P4, "p_st2", (1, TGT))
    p_scg = pt(P4, "p_scg", (8, TGT))
    p_den = pt(P4, "p_den", (1, TGT))
    p_rep = pt(P4, "p_rep", (128, TGT))

    rw_ld = S.step("gp", lambda gp: gp.dma_start(
        out=s_rwb[:].rearrange("p (k m) -> p k m", k=KT_DM),
        in_=router_w.rearrange("(k p) m -> p k m", p=128)), deps=P3_END, dma=True)
    wo_cast = None
    for mt in range(KT_DM):
        wo_cast = S.step("gp", lambda gp, mt=mt: gp.dma_start(
            out=wo_bf[mt], in_=W_out[:, mt * 128:(mt + 1) * 128]),
            deps=P2_END, dma=True)
    u42_ld = S.step("sy", lambda sy: sy.dma_start(out=s_u42[:], in_=u42_T),
                    deps=P3_END, dma=True)
    g1 = S.step("s", lambda s: s.activation(s_u42[:], s_u42[:], AF.Ln,
                                            bias=s_eps10[0:8, 0:1]),
                deps=[u42_ld, eps10_set])
    g2 = S.step("s", lambda s: s.activation(s_u42[:], s_u42[:], AF.Ln, scale=-1.0,
                                            bias=s_eps10[0:8, 0:1]), deps=[g1])

    mm4 = {}
    xm_done = {}
    stat_done = {}
    cast_q = {}
    cast_b = {}
    store_done = {}
    xlb_done = {}
    sc_mm = {}
    sc_free = None
    ut_ld = {}
    wo_ld = {}
    ynT_ld = {}

    for tg in range(NTG):
        for k in range(KT_DI):
            deps = P3_END if tg < 2 else [mm4[(tg - 2, KT_DM - 1)]]
            ynT_ld[(tg, k)] = S.step("sy", lambda sy, tg=tg, k=k: sy.dma_start_transpose(
                s_ynT[tg % 2][:, k * TGT:(k + 1) * TGT],
                yn_dram[tg * TGT:(tg + 1) * TGT, k * 128:(k + 1) * 128]),
                deps=deps, dma=True)
        def emit_stat(m, i):
            stat_done[i] = S.step("pe", lambda t, m=m, i=i: (
                t.matmul(p_st1[:], s_onesb[:, :], s_xbf[i % 2][:],
                         start=(m == 0), stop=(m == KT_DM - 1)),
                t.matmul(p_st2[:], s_onesb[:, :], s_xqb[i % 2][:],
                         start=(m == 0), stop=(m == KT_DM - 1)))[1],
                deps=[cast_q[i], cast_b[i], onesb_set])

        for m in range(KT_DM):
            i = tg * KT_DM + m
            wdeps = P3_END if i < 2 else [mm4[(tg, m - 2) if m >= 2 else
                                              (tg - 1, KT_DM - 2 + m)]]
            wo_ld[i] = S.step("gp", lambda gp, m=m, i=i: gp.dma_start(
                out=s_wo[i % 2][:].rearrange("p (k m) -> p k m", k=KT_DI),
                in_=wo_bf[m].rearrange("(k p) m -> p k m", p=128)),
                deps=list(wdeps) + [wo_cast], dma=True)
            udeps = P3_END if i < 2 else [xm_done[i - 2]]
            ut_ld[i] = S.step("sy", lambda sy, tg=tg, m=m, i=i: sy.dma_start(
                out=s_ut[i % 2][:],
                in_=u_T[m * 128:(m + 1) * 128, tg * TGT:(tg + 1) * TGT]),
                deps=udeps, dma=True)
            deps = [wo_ld[i], ynT_ld[(tg, KT_DI - 1)]]
            if i >= 2:
                deps.append(xm_done[i - 2])
            mm4[(tg, m)] = S.step("pe", lambda t, i=i, tg=tg: [
                t.matmul(p_o[i % 2][:], s_wo[i % 2][:, k * 128:(k + 1) * 128],
                         s_ynT[tg % 2][:, k * TGT:(k + 1) * TGT],
                         start=(k == 0), stop=(k == KT_DI - 1))
                for k in range(KT_DI)][-1], deps=deps)
            if m >= 2:
                emit_stat(m - 2, i - 2)
            cdeps = [mm4[(tg, m)], ut_ld[i]]
            if i >= 2:
                cdeps.append(stat_done.get(i - 2) or xm_done[i - 2])
            xm_done[i] = S.step("v", lambda v, m=m, i=i: v.tensor_add(
                s_xmid[:, m * TGT:(m + 1) * TGT], p_o[i % 2][:], s_ut[i % 2][:]),
                deps=cdeps)
            cast_q[i] = S.step("v", lambda v, m=m, i=i: v.tensor_mul(
                s_xqb[i % 2][:], s_xmid[:, m * TGT:(m + 1) * TGT],
                s_xmid[:, m * TGT:(m + 1) * TGT]),
                deps=[xm_done[i]] + ([stat_done[i - 2]] if (i - 2) in stat_done else []))
            cast_b[i] = S.step("s", lambda s, m=m, i=i: s.copy(
                s_xbf[i % 2][:], s_xmid[:, m * TGT:(m + 1) * TGT]),
                deps=[xm_done[i]] + ([stat_done[i - 2]] if (i - 2) in stat_done else []))
        emit_stat(KT_DM - 2, tg * KT_DM + KT_DM - 2)
        emit_stat(KT_DM - 1, tg * KT_DM + KT_DM - 1)
        ilast = tg * KT_DM + KT_DM - 1
        v1 = S.step("v", lambda v: v.tensor_scalar_mul(s_mean[:], p_st1[:], 1.0 / DM),
                    deps=[stat_done[ilast]])
        v2 = S.step("v", lambda v: v.tensor_mul(s_m2[:], s_mean[:], s_mean[:]), deps=[v1])
        v3 = S.step("v", lambda v: v.scalar_tensor_tensor(
            s_var[:], p_st2[:], 1.0 / DM, s_m2[:], ALU.mult, ALU.subtract), deps=[v2])
        v4 = S.step("s", lambda s: s.activation(s_var[:], s_var[:], AF.Ln,
                                                bias=s_eps5[0:1, 0:1]),
                    deps=[v3, eps5_set])
        v5 = S.step("s", lambda s: s.activation(s_rstd[:], s_var[:], AF.Exp, scale=-0.5),
                    deps=[v4])
        mmr1 = S.step("pe", lambda t: t.matmul(p_rep[:], s_ones[:, :], s_mean[:],
                                               start=True, stop=True), deps=[v5])
        evr1 = S.step("v", lambda v: v.tensor_copy(s_mrep[:], p_rep[:]), deps=[mmr1])
        mmr2 = S.step("pe", lambda t: t.matmul(p_rep[:], s_ones[:, :], s_rstd[:],
                                               start=True, stop=True), deps=[evr1, v5])
        evr2 = S.step("v", lambda v: v.tensor_copy(s_rrep[:], p_rep[:]), deps=[mmr2])
        for m in range(KT_DM):
            i = tg * KT_DM + m
            adeps = [evr2]
            if m >= 2:
                adeps += [store_done[(tg, m - 2)], sc_mm[(tg, m - 2)]]
            elif tg >= 1:
                adeps += [store_done[(tg - 1, KT_DM - 2 + m)],
                          sc_mm[(tg - 1, KT_DM - 2 + m)]]
            ap1 = S.step("v", lambda v, m=m: v.tensor_sub(
                s_xlt[m % 2][:], s_xmid[:, m * TGT:(m + 1) * TGT], s_mrep[:]),
                deps=adeps)
            ap2 = S.step("v", lambda v, m=m: v.tensor_mul(
                s_xlt[m % 2][:], s_xlt[m % 2][:], s_rrep[:]), deps=[ap1])
            store_done[(tg, m)] = S.step("sy", lambda sy, tg=tg, m=m: sy.dma_start(
                out=x_ln[m * 128:(m + 1) * 128, tg * TGT:(tg + 1) * TGT],
                in_=s_xlt[m % 2][:]), deps=[ap2], dma=True)
            xlb_done[(tg, m)] = S.step("s", lambda s, m=m: s.copy(
                s_xlb[m % 2][:], s_xlt[m % 2][:]), deps=[ap2])
            scdeps = [xlb_done[(tg, m)], rw_ld]
            if m == 0 and sc_free is not None:
                scdeps.append(sc_free)
            sc_mm[(tg, m)] = S.step("pe", lambda t, m=m: t.matmul(
                p_scg[:], s_rwb[:, m * 8:(m + 1) * 8], s_xlb[m % 2][:],
                start=(m == 0), stop=(m == KT_DM - 1)), deps=scdeps)
        sc_ev = S.step("s", lambda s, tg=tg: s.activation(
            s_hsc[:, tg * TGT:(tg + 1) * TGT], p_scg[:], AF.Identity,
            bias=s_rb[:, 0:1]), deps=[sc_mm[(tg, KT_DM - 1)], LDC])
        sc_free = S.step("v", lambda v, tg=tg: v.tensor_sub(
            s_hsc[:, tg * TGT:(tg + 1) * TGT], s_hsc[:, tg * TGT:(tg + 1) * TGT],
            s_u42[:, tg * TGT:(tg + 1) * TGT]), deps=[sc_ev, g2])

    he = S.step("s", lambda s: s.activation(s_hsc[:], s_hsc[:], AF.Exp), deps=[sc_free])
    den_prev = None
    sm_deps = []
    for q in range(NTG):
        dmm = S.step("pe", lambda t, q=q: t.matmul(
            p_den[:], s_ones8[:, :], s_hsc[:, q * TGT:(q + 1) * TGT],
            start=True, stop=True), deps=[he, ones8_set] + ([den_prev] if den_prev else []))
        den_prev = S.step("v", lambda v, q=q: v.reciprocal(
            s_dinv[:, q * TGT:(q + 1) * TGT], p_den[:]), deps=[dmm])
    rep_prev = None
    for q in range(NTG):
        rmm = S.step("pe", lambda t, q=q: t.matmul(
            p_scg[:], s_ones[:, 0:8], s_dinv[:, q * TGT:(q + 1) * TGT],
            start=True, stop=True), deps=[den_prev] + ([rep_prev] if rep_prev else []))
        rep_prev = S.step("v", lambda v, q=q: v.tensor_mul(
            s_hsc[:, q * TGT:(q + 1) * TGT], s_hsc[:, q * TGT:(q + 1) * TGT],
            p_scg[:]), deps=[rmm])
        sm_deps.append(rep_prev)
    S.step("sy", lambda sy: sy.dma_start(out=sm_out, in_=s_hsc[:]),
           deps=sm_deps, dma=True)
    P4.close()

    S.finalize(nc, sems)
    root.close()
    return nc



def build_B():
    nc = bass.Bass("TRN2", target_bir_lowering=False)
    din = lambda n, s, d=F32: nc.dram_tensor(n, list(s), d, kind="ExternalInput").ap()
    dout = lambda n, s, d=F32: nc.dram_tensor(n, list(s), d, kind="ExternalOutput").ap()
    dint = lambda n, s, d=F32: nc.dram_tensor(
        n, list(s), d, kind=("ExternalOutput" if DEBUG_INTERNALS[0] else "Internal")).ap()

    x_ln = din("x_ln", (DM, L))
    smp = din("smp", (128, 2048))
    w_gate = din("w_gate", (DM, DFF))
    b_gate = din("b_gate", (128, KT_FF))
    w_up = din("w_up", (DM, DFF))
    b_up = din("b_up", (128, KT_FF))
    w_down = din("w_down", (DFF, DM))
    b_down = din("b_down", (128, KT_DM))
    sel_e = din("sel_e", (128, 8))
    sel_et = din("sel_et", (8, 128))
    sel_j = din("sel_j", (128, 16))
    ones_c = din("ones_c", (1, 128))
    bstep_c = din("bstep_c", (8, N_BISECT))

    out = dout("out", (DM, L))
    mfl_dram = dint("mfl_dram", (16, 256))
    wg_bf = dint("wg_bf", (KT_FF, DM, 128), BF16)
    wu_bf = dint("wu_bf", (KT_FF, DM, 128), BF16)
    wd_bf = dint("wd_bf", (KT_DM, DFF, 128), BF16)
    dbg_cnt = dint("dbg_cnt", (8, N_BISECT))
    dbg_thr = dint("dbg_thr", (128, 1))
    dbg_lo = dint("dbg_lo", (8, 1))
    dbg_xm2 = dint("dbg_xm2", (128, KT_DM * 512))
    dbg_hb = dint("dbg_hb", (128, KT_FF * 512), BF16)

    S = Sched()
    root = ExitStack()
    sems = {e: root.enter_context(nc.semaphore(f"sem_{e}")) for e in Sched.ENGINES}
    sb = lambda st, n, s, d=F32: st.enter_context(nc.sbuf_tensor(n, list(s), d))
    pt = lambda st, n, s, d=F32: st.enter_context(nc.psum_tensor(n, list(s), d))
    P = root

    s_smp = sb(P, "s_smp", (128, 2048))
    s_sele = sb(P, "s_sele", (128, 8))
    s_selet = sb(P, "s_selet", (8, 128))
    s_selj = sb(P, "s_selj", (128, 16))
    s_ones = sb(P, "s_ones", (1, 128))
    s_onesb = sb(P, "s_onesb", (128, 1), BF16)
    s_lo = sb(P, "s_lo", (8, 1))
    s_hi = sb(P, "s_hi", (8, 1))
    s_thr8 = sb(P, "s_thr8", (8, 1))
    s_step = sb(P, "s_step", (8, 1))
    s_lstep = sb(P, "s_lstep", (8, 1))
    s_bstep = sb(P, "s_bstep", (8, N_BISECT))
    s_c8 = sb(P, "s_c8", (8, 1))
    s_thr = sb(P, "s_thr", (128, 1))
    s_pc = sb(P, "s_pc", (128, 1))
    s_mrep = sb(P, "s_mrep", (128, L))
    s_bg = sb(P, "s_bg", (128, KT_FF))
    s_bu = sb(P, "s_bu", (128, KT_FF))
    s_bd = sb(P, "s_bd", (128, KT_DM))
    s_cntlog = sb(P, "s_cntlog", (8, N_BISECT))

    lds = None
    for t, d in ((s_smp, smp), (s_sele, sel_e), (s_selet, sel_et), (s_selj, sel_j),
                 (s_ones, ones_c), (s_bg, b_gate), (s_bu, b_up), (s_bd, b_down)):
        lds = S.step("sy", lambda sy, t=t, d=d: sy.dma_start(out=t[:], in_=d), dma=True)
    LDC = lds
    LD_BS = S.step("sy", lambda sy: sy.dma_start(out=s_bstep[:], in_=bstep_c), dma=True)
    wg_cast = wu_cast = wd_cast = None
    for f in range(KT_FF):
        wg_cast = S.step("gp", lambda gp, f=f: gp.dma_start(
            out=wg_bf[f], in_=w_gate[:, f * 128:(f + 1) * 128]), dma=True)
        wu_cast = S.step("gp", lambda gp, f=f: gp.dma_start(
            out=wu_bf[f], in_=w_up[:, f * 128:(f + 1) * 128]), dma=True)
    for mt in range(KT_DM):
        wd_cast = S.step("gp", lambda gp, mt=mt: gp.dma_start(
            out=wd_bf[mt], in_=w_down[:, mt * 128:(mt + 1) * 128]), dma=True)
    onesb_set = S.step("v", lambda v: v.memset(s_onesb[:], 1.0))
    s_eps5 = sb(P, "s_eps5", (1, 1))
    eps5_set = S.step("v", lambda v: v.memset(s_eps5[:], 1e-5))

    PB = ExitStack()
    s_junk = sb(PB, "s_junk", (128, 2048))
    s_mfl = sb(PB, "s_mfl", (16, 256))
    s_m1 = sb(PB, "s_m1", (1, L))
    p_thr = pt(PB, "p_thr", (128, 1))
    p_c8 = pt(PB, "p_c8", (8, 1))
    p_m16 = pt(PB, "p_m16", (16, 256))
    p_repm = pt(PB, "p_repm", (128, 512))

    z0 = S.step("v", lambda v: v.memset(s_lo[:], 0.0))
    z2 = S.step("v", lambda v: v.memset(s_thr8[:], 0.5))
    t_prev = None
    bstep_prev = None
    for it in range(N_BISECT):
        tm = S.step("pe", lambda t: t.matmul(p_thr[:], s_selet[:, :], s_thr8[:],
                                             start=True, stop=True),
                    deps=[z2, LDC] + ([t_prev] if t_prev else []) +
                         ([u2] if it > 0 else []))
        te = S.step("s", lambda s: s.copy(s_thr[:], p_thr[:]), deps=[tm])
        t_prev = te
        c_a = S.step("v", lambda v: v.tensor_scalar(
            s_junk[:], s_smp[:], s_thr[:, 0:1], None, ALU.is_gt), deps=[te, LDC])
        c_b = S.step("v", lambda v: v.reduce_sum(s_pc[:, 0:1], s_junk[:],
                                                 mybir.AxisListType.X), deps=[c_a])
        csum = S.step("pe", lambda t: t.matmul(p_c8[:], s_sele[:, :], s_pc[:],
                                               start=True, stop=True), deps=[c_b])
        if DEBUG_INTERNALS[0]:
            S.step("v", lambda v, it=it: v.tensor_copy(s_cntlog[:, it:it + 1], p_c8[:]),
                   deps=[csum])
        b1 = S.step("v", lambda v: v.tensor_scalar(s_c8[:], p_c8[:], 4096.0, None,
                                                   ALU.is_ge), deps=[csum, z0])
        u1 = S.step("s", lambda s, it=it: s.activation(
            s_step[:], s_c8[:], AF.Identity, scale=float(2.0 ** (-(it + 2)) * 2),
            bias=s_bstep[:, it:it + 1]), deps=[b1, LD_BS] + ([u4] if it > 0 else []))
        u2 = S.step("v", lambda v: v.tensor_add(s_thr8[:], s_thr8[:], s_step[:]),
                    deps=[u1])
        u3 = S.step("s", lambda s, it=it: s.activation(
            s_lstep[:], s_c8[:], AF.Identity, scale=float(2.0 ** (-(it + 1)))),
            deps=[b1, u1])
        u4 = S.step("v", lambda v: v.tensor_add(s_lo[:], s_lo[:], s_lstep[:]),
                    deps=[u3, u2])
        tm = S.step("pe", lambda t: t.matmul(p_thr[:], s_selet[:, :], s_lo[:],
                                         start=True, stop=True), deps=[u4, t_prev])
    te = S.step("s", lambda s: s.copy(s_thr[:], p_thr[:]), deps=[tm])
    if DEBUG_INTERNALS[0]:
        S.step("sy", lambda sy: sy.dma_start(out=dbg_cnt, in_=s_cntlog[:]),
               deps=[te], dma=True)
        te = S.step("sy", lambda sy: sy.dma_start(out=dbg_thr, in_=s_thr[:]),
                    deps=[te], dma=True)
        te = S.step("sy", lambda sy: sy.dma_start(out=dbg_lo, in_=s_lo[:]),
                    deps=[te], dma=True)
    ws = S.step("v", lambda v: v.scalar_tensor_tensor(
        s_junk[:], s_smp[:], s_thr[:, 0:1], s_smp[:], ALU.is_gt, ALU.mult), deps=[te])
    m16 = S.step("pe", lambda t: t.matmul(p_m16[:], s_selj[:, :], s_junk[:, 0:256],
                                          start=True, stop=True), deps=[ws])
    mf_ev = S.step("v", lambda v: v.tensor_copy(s_mfl[:], p_m16[:]), deps=[m16])
    mf_sp = S.step("sy", lambda sy: sy.dma_start(out=mfl_dram, in_=s_mfl[:]),
                   deps=[mf_ev], dma=True)
    m1_ld = S.step("sy", lambda sy: sy.dma_start(
        out=s_m1[:], in_=mfl_dram.rearrange("a b -> (a b)").unsqueeze(0)),
        deps=[mf_sp], dma=True)
    mrep_ev = None
    for q in range(8):
        rmm = S.step("pe", lambda t, q=q: t.matmul(
            p_repm[:], s_ones[:, :], s_m1[:, q * 512:(q + 1) * 512],
            start=True, stop=True), deps=[m1_ld] + ([mrep_ev] if mrep_ev else []))
        mrep_ev = S.step("v", lambda v, q=q: v.tensor_copy(
            s_mrep[:, q * 512:(q + 1) * 512], p_repm[:]), deps=[rmm])
    M_READY = mrep_ev
    B_END = S.fence()
    PB.close()

    TB = 512
    NTB = L // TB
    s_xb = sb(P, "s_xb", (128, KT_DM * TB), BF16)
    s_hb = sb(P, "s_hb", (128, KT_FF * TB), BF16)
    s_wg = [sb(P, f"s_wg{i}", (128, KT_DM * 128), BF16) for i in range(3)]
    s_wu = [sb(P, f"s_wu{i}", (128, KT_DM * 128), BF16) for i in range(3)]
    s_wd = [sb(P, f"s_wd{i}", (128, KT_FF * 128), BF16) for i in range(3)]
    s_gbf = sb(P, "s_gbf", (128, TB), BF16)
    s_xm2 = sb(P, "s_xm2", (128, KT_DM * TB))
    s_xr = [sb(P, f"s_xr{i}", (128, TB)) for i in range(2)]
    s_xqb = [sb(P, f"s_xqb{i}", (128, TB), BF16) for i in range(2)]
    s_xbf = [sb(P, f"s_xbf{i}", (128, TB), BF16) for i in range(2)]
    s_mean = sb(P, "s_mean", (1, TB))
    s_var = sb(P, "s_var", (1, TB))
    s_m2 = sb(P, "s_m2", (1, TB))
    s_rstd = sb(P, "s_rstd", (1, TB))
    s_mrep2 = sb(P, "s_mrep2", (128, TB))
    s_rrep2 = sb(P, "s_rrep2", (128, TB))
    s_ot = [sb(P, f"s_ot{i}", (128, TB)) for i in range(2)]
    p_gu = [pt(P, f"p_gua{i}", (128, TB)) for i in range(2)]
    p_gu2 = [pt(P, f"p_gub{i}", (128, TB)) for i in range(2)]
    p_dn = [pt(P, f"p_dn{i}", (128, TB)) for i in range(2)]
    p_st1 = pt(P, "p_st1", (1, TB))
    p_st2 = pt(P, "p_st2", (1, TB))

    xb_ld = {}
    gu_ld = {}
    gmm = {}
    umm = {}
    sil = {}
    hb_done = {}
    wd_ld = {}
    dmm = {}
    x1_done = {}
    x2_done = {}
    stat_done = {}
    apply_done = {}
    ot_store = {}
    xr_ld = {}
    dm_last = {}

    for tb in range(NTB):
        xdeps = B_END if tb == 0 else [gmm[(tb - 1, KT_FF - 1)], umm[(tb - 1, KT_FF - 1)]]
        for k in range(KT_DM):
            xb_ld[tb] = S.step("gp", lambda gp, tb=tb, k=k: gp.dma_start(
                out=s_xb[:, k * TB:(k + 1) * TB],
                in_=x_ln[k * 128:(k + 1) * 128, tb * TB:(tb + 1) * TB]),
                deps=xdeps, dma=True)
        for f in range(KT_FF):
            i = tb * KT_FF + f
            gdeps = [hb_done[i - 2]] if i >= 2 else (B_END if tb == 0 else [])

            S.step("sy", lambda sy, f=f, i=i: sy.dma_start(
                out=s_wg[i % 3][:].rearrange("p (k m) -> p k m", k=KT_DM),
                in_=wg_bf[f].rearrange("(k p) m -> p k m", p=128)),
                deps=list(gdeps) + [wg_cast], dma=True)
            gu_ld[i] = S.step("sy", lambda sy, f=f, i=i: sy.dma_start(
                out=s_wu[i % 3][:].rearrange("p (k m) -> p k m", k=KT_DM),
                in_=wu_bf[f].rearrange("(k p) m -> p k m", p=128)),
                deps=list(gdeps) + [wu_cast], dma=True)
            gm_deps = [gu_ld[i], xb_ld[tb]]
            if i >= 2:
                gm_deps.append(sil[i - 2])
            gmm[(tb, f)] = S.step("pe", lambda t, i=i: [
                t.matmul(p_gu[i % 2][:], s_wg[i % 3][:, k * 128:(k + 1) * 128],
                         s_xb[:, k * TB:(k + 1) * TB],
                         start=(k == 0), stop=(k == KT_DM - 1))
                for k in range(KT_DM)][-1], deps=gm_deps)
            um_deps = [gu_ld[i], xb_ld[tb]]
            if i >= 2:
                um_deps.append(hb_done[i - 2])
            umm[(tb, f)] = S.step("pe", lambda t, i=i: [
                t.matmul(p_gu2[i % 2][:], s_wu[i % 3][:, k * 128:(k + 1) * 128],
                         s_xb[:, k * TB:(k + 1) * TB],
                         start=(k == 0), stop=(k == KT_DM - 1))
                for k in range(KT_DM)][-1], deps=um_deps)
            sil_deps = [gmm[(tb, f)], LDC]
            if i >= 1:
                sil_deps.append(hb_done[i - 1])
            sil[i] = S.step("s", lambda s, f=f, i=i: s.activation(
                s_gbf[:], p_gu[i % 2][:], AF.Silu, bias=s_bg[:, f:f + 1]), deps=sil_deps)
            hbdeps = [umm[(tb, f)], sil[i]]
            if tb >= 1 and f < 2:
                hbdeps.append(dm_last[tb - 1])
            hb_done[i] = S.step("v", lambda v, f=f, i=i: v.scalar_tensor_tensor(
                s_hb[:, f * TB:(f + 1) * TB], p_gu2[i % 2][:], s_bu[:, f:f + 1],
                s_gbf[:], ALU.add, ALU.mult), deps=hbdeps)
        HB_ALL = hb_done[tb * KT_FF + KT_FF - 1]
        if DEBUG_INTERNALS[0] and tb == 0:
            HB_ALL = S.step("sy", lambda sy: sy.dma_start(out=dbg_hb, in_=s_hb[:]),
                            deps=[HB_ALL], dma=True)
        for m in range(KT_DM):
            im = tb * KT_DM + m
            wdeps = [dmm[(tb, m - 3)]] if m >= 3 else (
                [dm_last[tb - 1]] if tb >= 1 else [])
            wd_ld[im] = S.step("sy", lambda sy, m=m, im=im: sy.dma_start(
                out=s_wd[im % 3][:].rearrange("p (k m) -> p k m", k=KT_FF),
                in_=wd_bf[m].rearrange("(k p) m -> p k m", p=128)),
                deps=list(wdeps) + [wd_cast], dma=True)
            xr_deps = [x2_done[im - 2]] if m >= 2 else (
                [apply_done[(tb - 1, KT_DM - 1)]] if tb >= 1 else [])
            xr_ld[im] = S.step("sy", lambda sy, tb=tb, m=m, im=im: sy.dma_start(
                out=s_xr[im % 2][:],
                in_=x_ln[m * 128:(m + 1) * 128, tb * TB:(tb + 1) * TB]),
                deps=xr_deps, dma=True)
            dm_deps = [wd_ld[im], HB_ALL]
            if m >= 2:
                dm_deps.append(x1_done[im - 2])
            dmm[(tb, m)] = S.step("pe", lambda t, m=m, im=im: [
                t.matmul(p_dn[m % 2][:], s_wd[im % 3][:, k * 128:(k + 1) * 128],
                         s_hb[:, k * TB:(k + 1) * TB],
                         start=(k == 0), stop=(k == KT_FF - 1))
                for k in range(KT_FF)][-1], deps=dm_deps)
            x1_deps = [dmm[(tb, m)], M_READY, LDC]
            if tb >= 1:
                x1_deps.append(apply_done[(tb - 1, m)])
            x1_done[im] = S.step("v", lambda v, tb=tb, m=m: v.scalar_tensor_tensor(
                s_xm2[:, m * TB:(m + 1) * TB], p_dn[m % 2][:], s_bd[:, m:m + 1],
                s_mrep[:, tb * TB:(tb + 1) * TB], ALU.add, ALU.mult),
                deps=x1_deps)
            x2_done[im] = S.step("v", lambda v, m=m, im=im: v.tensor_add(
                s_xm2[:, m * TB:(m + 1) * TB], s_xm2[:, m * TB:(m + 1) * TB],
                s_xr[im % 2][:]), deps=[x1_done[im], xr_ld[im]])
            cdeps = [x2_done[im]]
            if m >= 2:
                cdeps.append(stat_done[im - 2])
            xq = S.step("v", lambda v, m=m, im=im: v.tensor_mul(
                s_xqb[im % 2][:], s_xm2[:, m * TB:(m + 1) * TB],
                s_xm2[:, m * TB:(m + 1) * TB]), deps=cdeps)
            xb2 = S.step("s", lambda s, m=m, im=im: s.copy(
                s_xbf[im % 2][:], s_xm2[:, m * TB:(m + 1) * TB]), deps=cdeps)
            stdeps = [xq, xb2, onesb_set]
            if m == 0 and tb >= 1:
                stdeps.append(vv3[tb - 1])
            stat_done[im] = S.step("pe", lambda t, m=m, im=im: (
                t.matmul(p_st1[:], s_onesb[:, :], s_xbf[im % 2][:],
                         start=(m == 0), stop=(m == KT_DM - 1)),
                t.matmul(p_st2[:], s_onesb[:, :], s_xqb[im % 2][:],
                         start=(m == 0), stop=(m == KT_DM - 1)))[1], deps=stdeps)
        if tb == 0:
            vv3 = {}
        ilast = tb * KT_DM + KT_DM - 1
        v1 = S.step("v", lambda v: v.tensor_scalar_mul(s_mean[:], p_st1[:], 1.0 / DM),
                    deps=[stat_done[ilast]])
        v2 = S.step("v", lambda v: v.tensor_mul(s_m2[:], s_mean[:], s_mean[:]), deps=[v1])
        vv3[tb] = S.step("v", lambda v: v.scalar_tensor_tensor(
            s_var[:], p_st2[:], 1.0 / DM, s_m2[:], ALU.mult, ALU.subtract), deps=[v2])
        v4 = S.step("s", lambda s: s.activation(s_var[:], s_var[:], AF.Ln,
                                                bias=s_eps5[0:1, 0:1]),
                    deps=[vv3[tb], eps5_set])
        v5 = S.step("s", lambda s: s.activation(s_rstd[:], s_var[:], AF.Exp, scale=-0.5),
                    deps=[v4])
        mm1 = S.step("pe", lambda t: t.matmul(p_dn[0][:], s_ones[:, :], s_mean[:],
                                              start=True, stop=True),
                     deps=[v5, x1_done[ilast]])
        ev1 = S.step("v", lambda v: v.tensor_copy(s_mrep2[:], p_dn[0][:]), deps=[mm1])
        mm2 = S.step("pe", lambda t: t.matmul(p_dn[0][:], s_ones[:, :], s_rstd[:],
                                              start=True, stop=True), deps=[ev1, v5])
        ev2 = S.step("v", lambda v: v.tensor_copy(s_rrep2[:], p_dn[0][:]), deps=[mm2])
        for m in range(KT_DM):
            im = tb * KT_DM + m
            adeps = [ev2, x2_done[im]]
            if m >= 2:
                adeps.append(ot_store[(tb, m - 2)])
            elif tb >= 1:
                adeps.append(ot_store[(tb - 1, KT_DM - 2 + m)])
            a1 = S.step("v", lambda v, m=m, im=im: v.tensor_sub(
                s_ot[im % 2][:], s_xm2[:, m * TB:(m + 1) * TB], s_mrep2[:]), deps=adeps)
            apply_done[(tb, m)] = S.step("v", lambda v, m=m, im=im: v.tensor_mul(
                s_ot[im % 2][:], s_ot[im % 2][:], s_rrep2[:]), deps=[a1])
            ot_store[(tb, m)] = S.step("sy", lambda sy, tb=tb, m=m, im=im: sy.dma_start(
                out=out[m * 128:(m + 1) * 128, tb * TB:(tb + 1) * TB],
                in_=s_ot[im % 2][:]), deps=[apply_done[(tb, m)]], dma=True)
        if DEBUG_INTERNALS[0] and tb == 0:
            S.step("sy", lambda sy: sy.dma_start(out=dbg_xm2, in_=s_xm2[:]),
                   deps=[x2_done[tb * KT_DM + KT_DM - 1],
                         xq if False else x2_done[tb * KT_DM + KT_DM - 1]], dma=True)
        dm_last[tb] = dmm[(tb, KT_DM - 1)]

    S.finalize(nc, sems)
    root.close()
    return nc



_BUILT = {}
_CAPTURE = {}


def _get_built(which):
    if which not in _BUILT:
        _BUILT[which] = build_A() if which == "A" else build_B()
    return _BUILT[which]


_U42 = [None]


def _get_u42():
    if _U42[0] is not None:
        return _U42[0]
    import os
    import subprocess
    import sys
    import tempfile
    fd, path = tempfile.mkstemp(suffix=".npy")
    os.close(fd)
    code = ("import os; os.environ['JAX_PLATFORMS']='cpu'; "
            "import numpy, jax, jax.numpy as jnp; "
            "numpy.save(%r, numpy.asarray(jax.random.uniform("
            "jax.random.key(42), (%d, 8), dtype=jnp.float32)))" % (path, NCORE * L))
    env = dict(os.environ)
    env.pop("PYTHONPATH", None)
    env["JAX_PLATFORMS"] = "cpu"
    try:
        subprocess.run([sys.executable, "-c", code], env=env, check=True)
        arr = np.load(path)
    except Exception:
        import jax
        import jax.numpy as jnp
        with jax.default_device(jax.devices("cpu")[0]):
            arr = np.asarray(jax.random.uniform(jax.random.key(42), (NCORE * L, 8),
                                                dtype=jnp.float32))
    finally:
        try:
            os.remove(path)
        except OSError:
            pass
    _U42[0] = arr
    return arr


def kernel(L0_attn_u, L0_attn_W_in, L0_attn_conv_w, L0_attn_conv_b, L0_attn_dt_bias,
           L0_attn_A_log, L0_attn_D_param, L0_attn_norm_w, L0_attn_W_out,
           L0_ffn_router_w, L0_ffn_router_b, L0_ffn_w_gate, L0_ffn_b_gate,
           L0_ffn_w_up, L0_ffn_b_up, L0_ffn_w_down, L0_ffn_b_down):
    f32 = np.float32
    u = np.asarray(L0_attn_u, dtype=f32)
    W_in = np.ascontiguousarray(np.asarray(L0_attn_W_in, f32))
    conv_w = np.asarray(L0_attn_conv_w, f32)
    conv_b = np.asarray(L0_attn_conv_b, f32)
    dt_bias = np.asarray(L0_attn_dt_bias, f32).reshape(64, 1)
    A_log = np.asarray(L0_attn_A_log, f32).reshape(64, 1)
    D_param = np.asarray(L0_attn_D_param, f32).reshape(64, 1)
    norm_w = np.asarray(L0_attn_norm_w, f32).reshape(1, DI)
    W_out = np.ascontiguousarray(np.asarray(L0_attn_W_out, f32))
    router_w = np.ascontiguousarray(np.asarray(L0_ffn_router_w, f32))
    router_b = np.asarray(L0_ffn_router_b, f32).reshape(8, 1)

    conv_w_p = np.ascontiguousarray(conv_w.reshape(34, 128, 4).transpose(1, 0, 2)
                                    .reshape(128, 34 * 4))
    conv_b_p = np.ascontiguousarray(conv_b.reshape(34, 128).T)
    tril_c = -np.triu(np.ones((128, 128), f32))
    sidx = np.arange(128)
    mask_c = np.where(sidx[None, :] >= sidx[:, None], 0.0, NEG_MASK).astype(f32)
    idf_c = np.eye(128, dtype=f32)
    ones_c = np.ones((1, 128), f32)

    u42_T = _get_u42().T

    nc_a = _get_built("A")
    in_maps = []
    for c in range(NCORE):
        in_maps.append({
            "u_T": np.ascontiguousarray(u[c].T),
            "W_in": W_in,
            "conv_w_p": conv_w_p,
            "conv_b_p": conv_b_p,
            "dt_bias": dt_bias,
            "A_log": A_log,
            "D_param": D_param,
            "norm_w": norm_w,
            "W_out": W_out,
            "router_w": router_w,
            "router_b": router_b,
            "u42_T": np.ascontiguousarray(u42_T[:, c * L:(c + 1) * L]),
            "tril_c": tril_c,
            "mask_c": mask_c,
            "idf_c": idf_c,
            "ones_c": ones_c,
        })
    resA = run_bass_kernel_spmd(nc_a, in_maps, core_ids=list(range(NCORE)))
    _CAPTURE['A'] = resA

    smA = np.stack([np.asarray(resA.results[c]["sm"]) for c in range(NCORE)])
    xlnA = [np.asarray(resA.results[c]["x_ln"]) for c in range(NCORE)]

    smp_cores = []
    for c in range(NCORE):
        arr = np.empty((128, 2048), f32)
        for e in range(8):
            for src in range(NCORE):
                cc = (src - c) % NCORE
                blk = smA[src, e].reshape(16, 256)
                arr[e * 16:(e + 1) * 16, cc * 256:(cc + 1) * 256] = blk
        smp_cores.append(arr)

    pad_g = DFF - 5461
    w_gate = np.ascontiguousarray(np.pad(np.asarray(L0_ffn_w_gate, f32),
                                         ((0, 0), (0, pad_g))))
    w_up = np.ascontiguousarray(np.pad(np.asarray(L0_ffn_w_up, f32),
                                       ((0, 0), (0, pad_g))))
    w_down = np.ascontiguousarray(np.pad(np.asarray(L0_ffn_w_down, f32),
                                         ((0, pad_g), (0, 0))))
    b_gate = np.ascontiguousarray(np.pad(np.asarray(L0_ffn_b_gate, f32), (0, pad_g))
                                  .reshape(KT_FF, 128).T)
    b_up = np.ascontiguousarray(np.pad(np.asarray(L0_ffn_b_up, f32), (0, pad_g))
                                .reshape(KT_FF, 128).T)
    b_down = np.ascontiguousarray(np.asarray(L0_ffn_b_down, f32).reshape(KT_DM, 128).T)
    p128 = np.arange(128)
    sel_e = (p128[:, None] // 16 == np.arange(8)[None, :]).astype(f32)
    sel_et = np.ascontiguousarray(sel_e.T)
    sel_j = (p128[:, None] % 16 == np.arange(16)[None, :]).astype(f32)
    bstep_c = np.broadcast_to(
        -np.asarray([2.0 ** (-(k + 2)) for k in range(N_BISECT)], f32), (8, N_BISECT)
    ).copy()

    nc_b = _get_built("B")
    in_maps_b = []
    for c in range(NCORE):
        in_maps_b.append({
            "x_ln": xlnA[c],
            "smp": smp_cores[c],
            "w_gate": w_gate, "b_gate": b_gate,
            "w_up": w_up, "b_up": b_up,
            "w_down": w_down, "b_down": b_down,
            "sel_e": sel_e, "sel_et": sel_et, "sel_j": sel_j,
            "ones_c": ones_c, "bstep_c": bstep_c,
        })
    resB = run_bass_kernel_spmd(nc_b, in_maps_b, core_ids=list(range(NCORE)))
    _CAPTURE['B'] = resB

    out = np.empty((NCORE, L, DM), f32)
    for c in range(NCORE):
        out[c] = np.asarray(resB.results[c]["out"]).T
    return out


# version 06 (final)
# speedup vs baseline: 1.3391x; 1.1544x over previous
import numpy as np
from contextlib import ExitStack

import concourse.bass as bass
import concourse.mybir as mybir
from concourse.bass_utils import run_bass_kernel_spmd

F32 = mybir.dt.float32
BF16 = mybir.dt.bfloat16
AF = mybir.ActivationFunctionType
ALU = mybir.AluOpType

NCORE = 8
DM = 2048
DI = 4096
PROJ = 8512
L = 4096
CH = 128
NCH = L // CH
NH = 64
HD = 64
HG = 16
NHG = NH // HG
DFF = 5504
KT_DM = DM // 128
KT_DI = DI // 128
KT_FF = DFF // 128
XBC_T = 34
NEG_MASK = -60000.0
N_BISECT = 34


class Sched:

    ENGINES = ("pe", "v", "s", "sy", "gp")

    def __init__(self):
        self.steps = {e: [] for e in self.ENGINES}
        self.counts = {e: 0 for e in self.ENGINES}

    def step(self, eng, emit, deps=(), dma=False):
        inc = 16 if dma else 1
        kept = {}
        for d in deps:
            if d is None:
                continue
            de, dc, ddma = d
            if de == eng and not ddma:
                continue
            kept[de] = max(kept.get(de, 0), dc)
        self.steps[eng].append((emit, kept, inc))
        self.counts[eng] += inc
        return (eng, self.counts[eng], dma)

    def fence(self):
        return [(e, self.counts[e], True) for e in self.ENGINES if self.counts[e] > 0]

    def replay(self, eng, obj, sems):
        for emit, kept, inc in self.steps[eng]:
            for de, dc in kept.items():
                obj.wait_ge(sems[de], dc)
            emit(obj).then_inc(sems[eng], inc)

    def finalize(self, nc, sems):
        with nc.Block() as block:
            @block.tensor
            def _(t):
                self.replay("pe", t, sems)

            @block.vector
            def _(v):
                self.replay("v", v, sems)

            @block.scalar
            def _(s):
                self.replay("s", s, sems)

            @block.gpsimd
            def _(g):
                self.replay("gp", g, sems)

            @block.sync
            def _(sy):
                self.replay("sy", sy, sems)
                for e in self.ENGINES:
                    if self.counts[e] > 0:
                        sy.wait_ge(sems[e], self.counts[e])


def bc_tail(ap_, n_last):
    return ap_.to_broadcast([ap_.shape[0], ap_.shape[1], n_last])


def bc_mid(ap_, n_mid):
    return ap_.unsqueeze(1).to_broadcast([ap_.shape[0], n_mid, ap_.shape[1]])


def r3(ap_, h):
    return ap_.rearrange("p (h q) -> p h q", h=h)



DEBUG_INTERNALS = [False]


def build_A():
    nc = bass.Bass("TRN2", target_bir_lowering=False)
    din = lambda n, s, d=F32: nc.dram_tensor(n, list(s), d, kind="ExternalInput").ap()
    dout = lambda n, s, d=F32: nc.dram_tensor(n, list(s), d, kind="ExternalOutput").ap()
    dint = lambda n, s, d=F32: nc.dram_tensor(
        n, list(s), d, kind=("ExternalOutput" if DEBUG_INTERNALS[0] else "Internal")).ap()

    u_T = din("u_T", (DM, L))
    W_in = din("W_in", (DM, PROJ))
    conv_w_p = din("conv_w_p", (128, 34 * 4))
    conv_b_p = din("conv_b_p", (128, 34))
    dt_bias = din("dt_bias", (64, 1))
    A_log = din("A_log", (64, 1))
    D_param = din("D_param", (64, 1))
    norm_w = din("norm_w", (1, DI))
    W_out = din("W_out", (DI, DM))
    router_w = din("router_w", (DM, 8))
    router_b = din("router_b", (8, 1))
    u42_T = din("u42_T", (8, L))
    tril_c = din("tril_c", (128, 128))
    mask_c = din("mask_c", (128, 128))
    idf_c = din("idf_c", (128, 128))
    ones_c = din("ones_c", (1, 128))

    x_ln = dout("x_ln", (DM, L))
    sm_out = dout("sm", (8, L))

    x_dram = dint("x_dram", (DI, L), BF16)
    dbg_xs = dint("dbg_xs", (128, DI), BF16)
    dbg_gm = dint("dbg_gm", (128, HG * 128), BF16)
    dbg_y = dint("dbg_y", (128, DI))
    dbg_st = dint("dbg_st", (128, DI))
    z_dram = dint("z_dram", (L, DI), BF16)
    acum_dram = dint("acum_dram", (NCH, NH, CH))
    alast_dram = dint("alast_dram", (NCH, 64))
    yn_dram = dint("yn_dram", (L, DI), BF16)
    wo_bf = dint("wo_bf", (KT_DM, DI, 128), BF16)

    S = Sched()
    root = ExitStack()
    sems = {e: root.enter_context(nc.semaphore(f"sem_{e}")) for e in Sched.ENGINES}
    sb = lambda st, n, s, d=F32: st.enter_context(nc.sbuf_tensor(n, list(s), d))
    pt = lambda st, n, s, d=F32: st.enter_context(nc.psum_tensor(n, list(s), d))

    P = root
    s_tril = sb(P, "s_tril", (128, 128))
    s_mask = sb(P, "s_mask", (128, 128))
    s_idf = sb(P, "s_idf", (128, 128))
    s_idb = sb(P, "s_idb", (128, 128), BF16)
    s_ones = sb(P, "s_ones", (1, 128))
    s_onesb = sb(P, "s_onesb", (128, 1), BF16)
    s_ones8 = sb(P, "s_ones8", (8, 1))
    s_convw = sb(P, "s_convw", (128, 34 * 4))
    s_convb = sb(P, "s_convb", (128, 34))
    s_dtb = sb(P, "s_dtb", (64, 1))
    s_alog = sb(P, "s_alog", (64, 1))
    s_dpar = sb(P, "s_dpar", (64, 1))
    s_rb = sb(P, "s_rb", (8, 1))
    sBc = sb(P, "sBc", (128, L), BF16)
    sCc = sb(P, "sCc", (128, L), BF16)

    ld = None
    for t, d in ((s_tril, tril_c), (s_mask, mask_c), (s_idf, idf_c),
                 (s_ones, ones_c), (s_convw, conv_w_p), (s_convb, conv_b_p),
                 (s_dtb, dt_bias), (s_alog, A_log), (s_dpar, D_param),
                 (s_rb, router_b)):
        ld = S.step("sy", lambda sy, t=t, d=d: sy.dma_start(out=t[:], in_=d), dma=True)
    LDC = ld
    idb_cast = S.step("v", lambda v: v.tensor_copy(s_idb[:], s_idf[:]), deps=[LDC])
    onesb_set = S.step("v", lambda v: v.memset(s_onesb[:], 1.0))
    ones8_set = S.step("v", lambda v: v.memset(s_ones8[:], 1.0))
    s_eps5 = sb(P, "s_eps5", (128, 1))
    s_eps10 = sb(P, "s_eps10", (128, 1))
    eps5_set = S.step("v", lambda v: v.memset(s_eps5[:], 1e-5))
    eps10_set = S.step("v", lambda v: v.memset(s_eps10[:], 1e-10))

    P12 = ExitStack()
    s_dtraw = P12.enter_context(nc.sbuf_tensor("s_dtraw", [64, L], F32, side="right"))
    P1 = ExitStack()
    su = sb(P1, "su", (128, KT_DM * L), BF16)
    s_wblk = [sb(P1, f"s_wblk{i}", (128, KT_DM * 128), BF16) for i in range(2)]
    p_mm = [pt(P1, f"p_mm{i}", (128, 512)) for i in range(4)]
    P1a = ExitStack()
    s_stage = [sb(P1a, f"s_stage{i}", (128, 4104), BF16) for i in range(2)]
    s_cacc = sb(P1a, "s_cacc", (128, L))

    stage_zero = S.step("v", lambda v: [v.memset(s_stage[0][:, 0:3], 0.0),
                                        v.memset(s_stage[1][:, 0:3], 0.0)][-1])
    u_last = {}
    for k in range(KT_DM):
        u_last[0] = S.step("gp", lambda g, k=k: g.dma_start(
            out=su[:, k * L:(k + 1) * L], in_=u_T[k * 128:(k + 1) * 128, :]), dma=True)
    U_ALL = [u_last[0]]

    def wblk_load(i, c0, cw, deps):
        return S.step("gp",
                      lambda g, i=i, c0=c0, cw=cw: g.dma_start(
            out=s_wblk[i % 2][:, 0:KT_DM * cw].rearrange("p (k m) -> p k m", k=KT_DM),
            in_=W_in[:, c0:c0 + cw].rearrange("(k p) m -> p k m", p=128)),
            deps=deps, dma=True)

    mm_done = {}
    ev_done = {}
    g = 0
    w_dep = {0: wblk_load(0, DI + 4352, 64, U_ALL)}
    for nt in range(8):
        dt_deps = [w_dep[0]] + U_ALL
        if g >= 4:
            dt_deps.append(ev_done[g - 4])
        mm_done[g] = S.step("pe", lambda t, nt=nt, pb=p_mm[g % 4]: [
            t.matmul(pb[0:64, :], s_wblk[0][:, k * 64:(k + 1) * 64],
                     su[:, k * L + nt * 512:k * L + nt * 512 + 512],
                     start=(k == 0), stop=(k == KT_DM - 1))
            for k in range(KT_DM)][-1], deps=dt_deps)
        ev_done[g] = S.step("s", lambda s, nt=nt, pb=p_mm[g % 4]: s.copy(
            s_dtraw[:, nt * 512:(nt + 1) * 512], pb[0:64, :]), deps=[mm_done[g]])
        g += 1
    w_dep[1] = wblk_load(1, DI, 128, [mm_done[7]])

    stage_free = {}
    spill_prev = None
    for j in range(XBC_T):
        wi = j + 1
        for nt in range(8):
            deps = [w_dep[wi]]
            if g >= 4:
                deps.append(ev_done[g - 4])
            if nt == 0 and stage_free.get(j % 2) is not None:
                deps.append(stage_free[j % 2])
            mm_done[g] = S.step("pe", lambda t, nt=nt, wi=wi, pb=p_mm[g % 4]: [
                t.matmul(pb[:], s_wblk[wi % 2][:, k * 128:(k + 1) * 128],
                         su[:, k * L + nt * 512:k * L + nt * 512 + 512],
                         start=(k == 0), stop=(k == KT_DM - 1))
                for k in range(KT_DM)][-1], deps=deps)
            ev_deps = [mm_done[g], stage_zero]
            if stage_free.get(j % 2) is not None:
                ev_deps.append(stage_free[j % 2])
            ev_done[g] = S.step("s", lambda s, j=j, nt=nt, pb=p_mm[g % 4]: s.copy(
                s_stage[j % 2][:, 3 + nt * 512:3 + nt * 512 + 512], pb[:]), deps=ev_deps)
            g += 1
        if wi + 1 <= XBC_T:
            w_dep[wi + 1] = wblk_load(wi + 1, DI + wi * 128, 128, [mm_done[g - 1]])

        def emit_conv(v, j=j):
            stg = s_stage[j % 2]
            v.tensor_scalar(s_cacc[:], stg[:, 0:L], s_convw[:, j * 4:j * 4 + 1],
                            s_convb[:, j:j + 1], ALU.mult, ALU.add)
            last = None
            for w in (1, 2, 3):
                last = v.scalar_tensor_tensor(
                    s_cacc[:], stg[:, w:w + L], s_convw[:, j * 4 + w:j * 4 + w + 1],
                    s_cacc[:], ALU.mult, ALU.add)
            return last
        conv_deps = [ev_done[g - 1], LDC]
        if j > 0:
            conv_deps.append(prev_sil)
        conv = S.step("v", emit_conv, deps=conv_deps)
        if j < 32:
            sil_deps = [conv]
            if spill_prev is not None:
                sil_deps.append(spill_prev)
            sil = S.step("s", lambda s, j=j: s.activation(
                s_stage[j % 2][:, 4:4 + L], s_cacc[:], AF.Silu), deps=sil_deps)
            spill_prev = S.step("sy", lambda sy, j=j: sy.dma_start(
                out=x_dram[j * 128:(j + 1) * 128, :], in_=s_stage[j % 2][:, 4:4 + L]),
                deps=[sil], dma=True)
            stage_free[j % 2] = spill_prev
            prev_sil = sil
        else:
            dstt = sBc if j == 32 else sCc
            sil = S.step("s", lambda s, dstt=dstt: s.activation(dstt[:], s_cacc[:],
                                                                AF.Silu), deps=[conv])
            stage_free[j % 2] = sil
            prev_sil = sil
    BC_READY = stage_free[33 % 2]
    X_SPILLED = spill_prev

    P1a.close()
    s_wz = [sb(P1, f"s_wz{i}", (128, KT_DM * 512), BF16) for i in range(2)]
    s_zst = [sb(P1, f"s_zst{i}", (128, 512), BF16) for i in range(2)]
    zst_free = {}
    wz_dep = {}
    wo_cast2 = {}
    for nb in range(8):
        deps = [stage_free[0], stage_free[1]] if nb < 2 else [mm_done[("z", nb - 2, 31)]]
        wz_dep[nb] = S.step("gp", lambda gp, nb=nb: gp.dma_start(
            out=s_wz[nb % 2][:].rearrange("p (k m) -> p k m", k=KT_DM),
            in_=W_in[:, nb * 512:(nb + 1) * 512].rearrange("(k p) m -> p k m", p=128)),
            deps=deps, dma=True)
        for tt in range(32):
            deps = [wz_dep[nb]]
            if g >= 4:
                deps.append(ev_done[g - 4])
            mm_done[("z", nb, tt)] = S.step("pe", lambda t, nb=nb, tt=tt, pb=p_mm[g % 4]: [
                t.matmul(pb[:], su[:, k * L + tt * 128:k * L + tt * 128 + 128],
                         s_wz[nb % 2][:, k * 512:(k + 1) * 512],
                         start=(k == 0), stop=(k == KT_DM - 1))
                for k in range(KT_DM)][-1], deps=deps)
            mm_done[g] = mm_done[("z", nb, tt)]
            zi = (nb * 32 + tt) % 2
            ev_deps = [mm_done[g]]
            if zst_free.get(zi) is not None:
                ev_deps.append(zst_free[zi])
            ev_done[g] = S.step("s", lambda s, zi=zi, pb=p_mm[g % 4]: s.copy(
                s_zst[zi][:], pb[:]), deps=ev_deps)
            zst_free[zi] = S.step("sy", lambda sy, nb=nb, tt=tt, zi=zi: sy.dma_start(
                out=z_dram[tt * 128:(tt + 1) * 128, nb * 512:(nb + 1) * 512],
                in_=s_zst[zi][:]), deps=[ev_done[g]], dma=True)
            g += 1
        for q in range(2):
            mt = nb * 2 + q
            wo_cast2[0] = S.step("gp", lambda gq, mt=mt: gq.dma_start(
                out=wo_bf[mt], in_=W_out[:, mt * 128:(mt + 1) * 128]), dma=True)
    WO_CAST = [wo_cast2[0]]
    P1_END = S.fence()
    P1.close()

    P23 = ExitStack()
    s_dtT = sb(P23, "s_dtT", (128, NCH * 64), BF16)
    s_acA = sb(P23, "s_acA", (128, NCH * 64))
    s_ds = sb(P23, "s_ds", (128, NCH * 64), BF16)
    s_dprep = sb(P23, "s_dprep", (128, 64))
    s_od = sb(P23, "s_od", (128, NCH * 64), BF16)
    s_dA = sb(P23, "s_dA", (128, NCH * 64))
    sBT = sb(P23, "sBT", (128, NCH * 128), BF16)
    s_nwrep = sb(P23, "s_nwrep", (128, DI), BF16)
    P2 = ExitStack()
    s_dt = sb(P2, "s_dt", (64, L))
    s_a = sb(P2, "s_a", (64, L))
    s_aT = sb(P2, "s_aT", (128, NCH * 64))
    s_acT = sb(P2, "s_acT", (64, NCH * 128))
    s_nw = sb(P2, "s_nw", (1, DI))
    s_alf = sb(P2, "s_alf", (1, NCH * 64))
    s_dprow = sb(P2, "s_dprow", (1, 64))
    p_tr = pt(P2, "p_tr", (128, 512))
    p_sm2 = pt(P2, "p_sm2", (128, 64))
    p_sm = pt(P2, "p_sm", (128, 64))

    nw_ld = S.step("sy", lambda sy: sy.dma_start(out=s_nw[:], in_=norm_w),
                   deps=P1_END, dma=True)
    dprow_ld = S.step("sy", lambda sy: sy.dma_start(
        out=s_dprow[:], in_=D_param.rearrange("a b -> b a")), deps=P1_END, dma=True)
    c1 = S.step("s", lambda s: s.activation(s_dt[:], s_dtraw[:], AF.Identity,
                                            bias=s_dtb[:, 0:1]), deps=P1_END + [LDC])
    c2 = S.step("s", lambda s: s.activation(s_dtraw[:], s_dt[:], AF.Abs), deps=[c1])
    c3 = S.step("s", lambda s: s.activation(s_dtraw[:], s_dtraw[:], AF.Exp, scale=-1.0),
                deps=[c2])
    c4 = S.step("s", lambda s: s.activation(s_dtraw[:], s_dtraw[:], AF.Ln, bias=1.0),
                deps=[c3])
    c5 = S.step("v", lambda v: v.tensor_relu(s_a[:], s_dt[:]), deps=[c1])
    c6 = S.step("v", lambda v: v.tensor_add(s_dt[:], s_a[:], s_dtraw[:]), deps=[c4, c5])
    e1 = S.step("s", lambda s: s.activation(s_alog[:], s_alog[:], AF.Exp), deps=[LDC])
    a1 = S.step("v", lambda v: v.tensor_scalar(s_a[:], s_dt[:], s_alog[:, 0:1], None,
                                               ALU.mult), deps=[c6, e1])

    prev_ev = None
    for (src, dst, dd) in ((s_a, s_aT, a1), (s_dt, s_dtT, c6)):
        for b in range(4):
            deps = [dd] + ([prev_ev] if prev_ev else [])
            mm = S.step("pe", lambda t, src=src, b=b: [
                t.transpose(p_tr[:, q * 64:(q + 1) * 64],
                            src[:, (b * 8 + q) * 128:(b * 8 + q + 1) * 128],
                            s_idf[0:64, 0:64])
                for q in range(8)][-1], deps=deps)
            prev_ev = S.step("v", lambda v, dst=dst, b=b: v.tensor_copy(
                dst[:, b * 512:(b + 1) * 512], p_tr[:]), deps=[mm])
    for b in range(4):
        mm = S.step("pe", lambda t, b=b: [
            t.matmul(p_tr[:, q * 64:(q + 1) * 64], s_tril[:],
                     s_aT[:, (b * 8 + q) * 64:(b * 8 + q + 1) * 64],
                     start=True, stop=True)
            for q in range(8)][-1], deps=[prev_ev, LDC])
        prev_ev = S.step("v", lambda v, b=b: v.tensor_scalar_mul(
            s_acA[:, b * 512:(b + 1) * 512], p_tr[:], -1.0), deps=[mm])
    CUM = prev_ev
    for b in range(8):
        mm = S.step("pe", lambda t, b=b: [
            t.transpose(p_tr[0:64, q * 128:(q + 1) * 128],
                        s_acA[:, (b * 4 + q) * 64:(b * 4 + q + 1) * 64], s_idf[:])
            for q in range(4)][-1], deps=[CUM, prev_ev])
        prev_ev = S.step("v", lambda v, b=b: v.tensor_copy(
            s_acT[:, b * 512:(b + 1) * 512], p_tr[0:64, :]), deps=[mm])
    ACT_EV = prev_ev
    AC_SPILL = S.step("sy", lambda sy: sy.dma_start(
        out=acum_dram.rearrange("c h l -> h c l"),
        in_=s_acT[:].rearrange("p (c l) -> p c l", c=NCH)),
        deps=[ACT_EV], dma=True)
    al_sp = S.step("sy", lambda sy: sy.dma_start(
        out=alast_dram.rearrange("c h -> h c").unsqueeze(2),
        in_=s_acT[:].rearrange("p (c l) -> p c l", c=NCH)[:, :, 127:128],
        allow_slow_non_contiguous=True), deps=[ACT_EV], dma=True)
    al_ld = S.step("sy", lambda sy: sy.dma_start(
        out=s_alf[:], in_=alast_dram.rearrange("c h -> (c h)").unsqueeze(0),
        allow_slow_non_contiguous=True), deps=[al_sp], dma=True)
    al_ev = None
    for b in range(4):
        mm = S.step("pe", lambda t, b=b: t.matmul(
            p_tr[:], s_ones[:, :], s_alf[:, b * 512:(b + 1) * 512],
            start=True, stop=True), deps=[al_ld, ACT_EV] + ([al_ev] if al_ev else []))
        al_ev = S.step("v", lambda v, b=b: v.tensor_copy(
            s_dA[:, b * 512:(b + 1) * 512], p_tr[:]), deps=[mm])
    od1 = S.step("s", lambda s: s.activation(s_od[:], s_acA[:], AF.Exp, scale=-1.0),
                 deps=[CUM])
    ds1 = S.step("v", lambda v: v.tensor_sub(s_aT[:, 0:NCH * 64], s_acA[:], s_dA[:]),
                 deps=[al_ev])
    ds2 = S.step("s", lambda s: s.activation(s_ds[:], s_aT[:, 0:NCH * 64], AF.Exp),
                 deps=[ds1])
    dA1 = S.step("s", lambda s: s.activation(s_dA[:], s_dA[:], AF.Exp, scale=-1.0),
                 deps=[ds1])
    dtds = S.step("v", lambda v: v.tensor_mul(s_ds[:], s_ds[:], s_dtT[:]),
                  deps=[ds2])
    dp_mm = S.step("pe", lambda t: t.matmul(p_sm2[:, 0:64], s_ones[:, :],
                                            s_dprow[:], start=True, stop=True),
                   deps=[dprow_ld, CUM])
    dp_ev = S.step("v", lambda v: v.tensor_copy(s_dprep[:], p_sm2[:, 0:64]),
                   deps=[dp_mm])
    nw_ev = None
    for i in range(8):
        mm = S.step("pe", lambda t, i=i: t.matmul(
            p_tr[:], s_ones[:, :], s_nw[:, i * 512:(i + 1) * 512],
            start=True, stop=True), deps=[nw_ld, al_ev] + ([nw_ev] if nw_ev else []))
        nw_ev = S.step("v", lambda v, i=i: v.tensor_copy(
            s_nwrep[:, i * 512:(i + 1) * 512], p_tr[:]), deps=[mm])
    bt_prev = nw_ev
    for b in range(8):
        mm = S.step("pe", lambda t, b=b: [
            t.transpose(p_tr.ap().bitcast(BF16)[:, q * 128:(q + 1) * 128],
                        sBc[:, (b * 4 + q) * 128:(b * 4 + q + 1) * 128], s_idb[:])
            for q in range(4)][-1], deps=[idb_cast, bt_prev] + (P1_END if b == 0 else []))
        bt_prev = S.step("v", lambda v, b=b: v.tensor_copy(
            sBT[:, b * 512:(b + 1) * 512], p_tr.ap().bitcast(BF16)[:, 0:512]), deps=[mm])
    P2_END = S.fence()
    P2.close()
    P12.close()

    P3 = ExitStack()
    s_xT = [sb(P3, f"s_xT{i}", (128, DI), BF16) for i in range(2)]
    s_xs = sb(P3, "s_xs", (128, DI), BF16)
    s_xds = sb(P3, "s_xds", (128, DI), BF16)
    s_xcf = sb(P3, "s_xcf", (128, DI), BF16)
    s_arep = [sb(P3, f"s_arep{i}", (128, HG * 128)) for i in range(2)]
    s_D = [sb(P3, f"s_D{i}", (128, HG * 128)) for i in range(2)]
    s_Gm = [sb(P3, f"s_Gm{i}", (128, HG * 128), BF16) for i in range(2)]
    s_Gt = sb(P3, "s_Gt", (128, 128), BF16)
    s_St = sb(P3, "s_St", (128, DI))
    s_Stmp = sb(P3, "s_Stmp", (128, HG * HD))
    s_Sbf = sb(P3, "s_Sbf", (128, DI), BF16)
    s_Y = sb(P3, "s_Y", (128, DI))
    s_yn = sb(P3, "s_yn", (128, DI), BF16)
    s_zT = sb(P3, "s_zT", (128, DI), BF16)
    s_ss = sb(P3, "s_ss", (128, 1))
    s_rr = sb(P3, "s_rr", (128, 1))
    p_gt = pt(P3, "p_gt", (128, 128))
    p_yd = pt(P3, "p_yd", (128, HG * HD))
    p_yo = pt(P3, "p_yo", (128, HG * HD))
    p_cs = pt(P3, "p_cs", (128, HG * HD))

    NU = NCH * NHG
    xt_ld = {}
    z_ld = {}
    arep_ld = {}
    d1_done = {}
    eD_done = {}
    gm_done = {}
    pe_done = {}
    evac_done = {}
    sbf_done = {}
    sil_done = {}
    yf_done = {}
    sq_done = {}
    rr2_done = {}
    yn_spill = {}
    x3_done = {}
    gt_mm_d = {}
    gt_ev_d = {}
    init_S = S.step("v", lambda v: v.memset(s_St[:], 0.0), deps=P2_END)

    def emit_xt(c):
        deps = [x3_done[c - 2]] if c >= 2 else [X_SPILLED] + P2_END
        xt_ld[c] = S.step("sy", lambda sy, c=c: sy.dma_start_transpose(
            s_xT[c % 2][:], x_dram[:, c * CH:(c + 1) * CH]), deps=deps, dma=True)

    def emit_zld(c):
        deps = [sq_done[c - 1]] if c >= 1 else P2_END
        z_ld[c] = S.step("sy", lambda sy, c=c: sy.dma_start(
            out=s_zT[:], in_=z_dram[c * 128:(c + 1) * 128, :]), deps=deps, dma=True)

    def emit_arep_ld(u):
        c, hg = divmod(u, NHG)
        deps = [AC_SPILL]
        if u >= 2:
            deps.append(gm_done[u - 2])
        arep_ld[u] = S.step("gp", lambda gp, c=c, hg=hg, u=u: gp.dma_start(
            out=s_arep[u % 2][:],
            in_=acum_dram[c, hg * HG:(hg + 1) * HG, :]
            .rearrange("h l -> (h l)").unsqueeze(0).partition_broadcast(128)
            .squeeze(1)), deps=deps, dma=True)

    def emit_d1(u):
        deps = [arep_ld[u], LDC]
        if u >= 2:
            deps.append(eD_done[u - 2])
        d1_done[u] = S.step("v", lambda v, u=u: v.tensor_tensor(
            r3(s_D[u % 2][:], HG), r3(s_arep[u % 2][:], HG),
            bc_mid(s_mask[:], HG), ALU.add), deps=deps)

    def emit_eD(u):
        c, hg = divmod(u, NHG)
        eDap = s_arep[u % 2].ap().bitcast(BF16)[:, 0:HG * 128]

        def em(s, u=u, c=c, hg=hg, eDap=eDap):
            last = None
            for hh in range(HG):
                col = c * 64 + hg * HG + hh
                last = s.activation(eDap[:, hh * 128:(hh + 1) * 128],
                                    s_D[u % 2][:, hh * 128:(hh + 1) * 128],
                                    AF.Exp, scale=-1.0, bias=s_acA[:, col:col + 1])
            return last
        eD_done[u] = S.step("s", em, deps=[d1_done[u]])

    def emit_gm(u):
        c, hg = divmod(u, NHG)
        eDap = s_arep[u % 2].ap().bitcast(BF16)[:, 0:HG * 128]
        deps = [eD_done[u], gt_ev_d[c]]
        if u >= 2:
            deps.append(pe_done[u - 2])
        gm_done[u] = S.step("v", lambda v, u=u, eDap=eDap: v.tensor_tensor(
            r3(s_Gm[u % 2][:], HG), r3(eDap, HG), bc_mid(s_Gt[:], HG), ALU.mult),
            deps=deps)
        if DEBUG_INTERNALS[0] and u == NHG:
            gm_done[u] = S.step("sy", lambda sy, u=u: sy.dma_start(
                out=dbg_gm, in_=s_Gm[u % 2][:]), deps=[gm_done[u]], dma=True)

    def emit_gt(c):
        deps = [gt_ev_d[c - 1], gm_done[c * NHG - 1]] if c > 0 else list(P2_END)
        gt_mm_d[c] = S.step("pe", lambda t, c=c: t.matmul(
            p_gt[:], sBc[:, c * CH:(c + 1) * CH], sCc[:, c * CH:(c + 1) * CH],
            start=True, stop=True), deps=deps)
        gt_ev_d[c] = S.step("s", lambda s: s.copy(s_Gt[:], p_gt[:]),
                            deps=[gt_mm_d[c]])

    def emit_x123(c):
        deps1 = [xt_ld[c], dtds, dp_ev]
        if c > 0:
            deps1.append(pe_done[(c - 1) * NHG + NHG - 1])
        x1 = S.step("v", lambda v, c=c: v.tensor_tensor(
            r3(s_xs[:], NH), r3(s_xT[c % 2][:], NH),
            bc_tail(s_dtT[:, c * 64:(c + 1) * 64], HD), ALU.mult), deps=deps1)
        x2 = S.step("v", lambda v, c=c: v.tensor_tensor(
            r3(s_xds[:], NH), r3(s_xT[c % 2][:], NH),
            bc_tail(s_ds[:, c * 64:(c + 1) * 64], HD), ALU.mult), deps=[x1])
        x3 = S.step("v", lambda v, c=c: v.tensor_tensor(
            r3(s_xcf[:], NH), r3(s_xT[c % 2][:], NH),
            bc_tail(s_dprep[:], HD), ALU.mult), deps=[x2])
        if DEBUG_INTERNALS[0] and c == 1:
            x3 = S.step("sy", lambda sy: sy.dma_start(out=dbg_xs, in_=s_xs[:]),
                        deps=[x3], dma=True)
        x3_done[c] = x3

    def emit_pe(u):
        c, hg = divmod(u, NHG)

        def em(t, c=c, hg=hg, u=u):
            for hh in range(HG):
                h = hg * HG + hh
                t.matmul(p_yd[:, hh * HD:(hh + 1) * HD],
                         s_Gm[u % 2][:, hh * 128:(hh + 1) * 128],
                         s_xs[:, h * HD:(h + 1) * HD],
                         start=(hh % 8 == 0), stop=False)
            for q in range(2):
                t.matmul(p_yd[:, q * 512:(q + 1) * 512], s_idb[:],
                         s_xcf[:, hg * 1024 + q * 512:hg * 1024 + q * 512 + 512],
                         start=False, stop=True)
            if c > 0:
                for q in range(2):
                    t.matmul(p_yo[:, q * 512:(q + 1) * 512],
                             sCc[:, c * CH:(c + 1) * CH],
                             s_Sbf[:, hg * 1024 + q * 512:hg * 1024 + q * 512 + 512],
                             start=True, stop=True)
            last = None
            for q in range(2):
                last = t.matmul(p_cs[:, q * 512:(q + 1) * 512],
                                sBT[:, c * CH:(c + 1) * CH],
                                s_xds[:, hg * 1024 + q * 512:hg * 1024 + q * 512 + 512],
                                start=True, stop=True)
            return last
        deps = [gm_done[u], x3_done[c], idb_cast]
        if u >= 1:
            deps.append(evac_done[u - 1])
        if c > 0:
            deps.append(sbf_done[(c - 1) * NHG + hg])
        pe_done[u] = S.step("pe", em, deps=deps)

    def emit_evac(u):
        c, hg = divmod(u, NHG)
        if c > 0:
            y1 = S.step("v", lambda v, c=c, hg=hg: v.tensor_tensor(
                r3(s_Stmp[:], HG), r3(p_yo[:], HG),
                bc_tail(s_od[:, c * 64 + hg * HG:c * 64 + (hg + 1) * HG], HD),
                ALU.mult), deps=[pe_done[u], od1])
            y2 = S.step("v", lambda v, hg=hg: v.tensor_add(
                s_Y[:, hg * 1024:(hg + 1) * 1024], p_yd[:], s_Stmp[:]), deps=[y1])
        else:
            y2 = S.step("v", lambda v, hg=hg: v.tensor_copy(
                s_Y[:, hg * 1024:(hg + 1) * 1024], p_yd[:]), deps=[pe_done[u]])
        st1 = S.step("v", lambda g, c=c, hg=hg: g.tensor_tensor(
            r3(s_Stmp[:], HG), r3(s_St[:, hg * 1024:(hg + 1) * 1024], HG),
            bc_tail(s_dA[:, c * 64 + hg * HG:c * 64 + (hg + 1) * HG], HD),
            ALU.mult), deps=[y2, init_S, dA1])
        st2 = S.step("v", lambda v, hg=hg: v.tensor_add(
            s_St[:, hg * 1024:(hg + 1) * 1024], s_Stmp[:], p_cs[:]), deps=[st1])
        if DEBUG_INTERNALS[0] and u == NHG + NHG - 1:
            st2 = S.step("sy", lambda sy: sy.dma_start(out=dbg_y, in_=s_Y[:]),
                         deps=[st2], dma=True)
            st2 = S.step("sy", lambda sy: sy.dma_start(out=dbg_st, in_=s_St[:]),
                         deps=[st2], dma=True)
        evac_done[u] = st2
        if c < NCH - 1:
            sbf_done[u] = S.step("s", lambda s, hg=hg: s.copy(
                s_Sbf[:, hg * 1024:(hg + 1) * 1024],
                s_St[:, hg * 1024:(hg + 1) * 1024]), deps=[st2])

    def emit_sil(c):
        deps = [z_ld[c]]
        if c > 0:
            deps.append(yf_done[c - 1])
        sil_done[c] = S.step("s", lambda s: s.activation(
            s_zT[:], s_zT[:], AF.Silu), deps=deps)

    def emit_yf(c):
        yf_done[c] = S.step("v", lambda v: v.tensor_mul(s_Y[:], s_Y[:], s_zT[:]),
                            deps=[sil_done[c], evac_done[c * NHG + NHG - 1], ds2])

    def emit_sqrr(c):
        sq_done[c] = S.step("s", lambda s: s.activation(
            s_zT[:], s_Y[:], AF.Square, accum_out=s_ss[:, 0:1]), deps=[yf_done[c]])
        rr1 = S.step("s", lambda s: s.activation(
            s_ss[:], s_ss[:], AF.Ln, scale=1.0 / DI, bias=s_eps5[:, 0:1]),
            deps=[sq_done[c], eps5_set])
        rr2_done[c] = S.step("s", lambda s: s.activation(
            s_rr[:], s_ss[:], AF.Exp, scale=-0.5), deps=[rr1])

    def emit_yn(c):
        deps = [rr2_done[c], nw_ev]
        if c > 0:
            deps.append(yn_spill[c - 1])
        yn = S.step("v", lambda v: v.scalar_tensor_tensor(
            s_yn[:], s_Y[:], s_rr[:, 0:1], s_nwrep[:], ALU.mult, ALU.mult),
            deps=deps)
        yn_spill[c] = S.step("sy", lambda sy, c=c: sy.dma_start(
            out=yn_dram[c * 128:(c + 1) * 128, :], in_=s_yn[:]), deps=[yn], dma=True)

    emit_xt(0)
    emit_zld(0)
    emit_arep_ld(0)
    emit_arep_ld(1)
    emit_x123(0)
    for u in range(NU):
        c, hg = divmod(u, NHG)
        if hg == 0:
            if c + 1 < NCH:
                emit_xt(c + 1)
            emit_gt(c)
        emit_d1(u)
        emit_eD(u)
        if hg == 3 and c > 0:
            emit_yn(c - 1)
        if u >= 3:
            emit_evac(u - 3)
        if hg == 2 and c > 0:
            emit_yf(c - 1)
            emit_x123(c)
            emit_sqrr(c - 1)
            emit_zld(c)
        if hg == 3:
            emit_sil(c)
        if u >= 2:
            emit_pe(u - 2)
        emit_gm(u)
        if u + 2 < NU:
            emit_arep_ld(u + 2)
    emit_evac(NU - 3)
    emit_pe(NU - 2)
    emit_evac(NU - 2)
    emit_pe(NU - 1)
    emit_evac(NU - 1)
    emit_yf(NCH - 1)
    emit_sqrr(NCH - 1)
    emit_yn(NCH - 1)

    P3_END = S.fence()
    P3.close()
    P23.close()

    P4 = ExitStack()
    TGT = 512
    NTG = L // TGT
    s_ynT = [sb(P4, f"s_ynT{i}", (128, KT_DI * TGT), BF16) for i in range(2)]
    s_wo = [sb(P4, f"s_wo{i}", (128, KT_DI * 128), BF16) for i in range(2)]
    s_xmid = sb(P4, "s_xmid", (128, KT_DM * TGT))
    s_ut = [sb(P4, f"s_ut{i}", (128, TGT)) for i in range(2)]
    s_xbf = [sb(P4, f"s_xbf{i}", (128, TGT), BF16) for i in range(2)]
    s_xqb = [sb(P4, f"s_xqb{i}", (128, TGT), BF16) for i in range(2)]
    s_mean = sb(P4, "s_mean", (1, TGT))
    s_var = sb(P4, "s_var", (1, TGT))
    s_m2 = sb(P4, "s_m2", (1, TGT))
    s_rstd = sb(P4, "s_rstd", (1, TGT))
    s_mrep = sb(P4, "s_mrep", (128, TGT))
    s_rrep = sb(P4, "s_rrep", (128, TGT))
    s_xlt = [sb(P4, f"s_xlt{i}", (128, TGT)) for i in range(2)]
    s_xlb = [sb(P4, f"s_xlb{i}", (128, TGT), BF16) for i in range(2)]
    s_rwb = sb(P4, "s_rwb", (128, KT_DM * 8), BF16)
    s_hsc = sb(P4, "s_hsc", (8, L))
    s_u42 = sb(P4, "s_u42", (8, L))
    s_dinv = sb(P4, "s_dinv", (1, L))
    p_o = [pt(P4, f"p_o{i}", (128, TGT)) for i in range(2)]
    p_st1 = pt(P4, "p_st1", (1, TGT))
    p_st2 = pt(P4, "p_st2", (1, TGT))
    p_scg = pt(P4, "p_scg", (8, TGT))
    p_den = pt(P4, "p_den", (1, TGT))
    p_rep = pt(P4, "p_rep", (128, TGT))

    rw_ld = S.step("gp", lambda gp: gp.dma_start(
        out=s_rwb[:].rearrange("p (k m) -> p k m", k=KT_DM),
        in_=router_w.rearrange("(k p) m -> p k m", p=128)), deps=P3_END, dma=True)
    u42_ld = S.step("sy", lambda sy: sy.dma_start(out=s_u42[:], in_=u42_T),
                    deps=P3_END, dma=True)
    g1 = S.step("s", lambda s: s.activation(s_u42[:], s_u42[:], AF.Ln,
                                            bias=s_eps10[0:8, 0:1]),
                deps=[u42_ld, eps10_set])
    g2 = S.step("s", lambda s: s.activation(s_u42[:], s_u42[:], AF.Ln, scale=-1.0,
                                            bias=s_eps10[0:8, 0:1]), deps=[g1])

    mm4 = {}
    xm_done = {}
    stat_done = {}
    cast_q = {}
    cast_b = {}
    store_done = {}
    xlb_done = {}
    sc_mm = {}
    sc_free = None
    ut_ld = {}
    wo_ld = {}
    ynT_ld = {}

    for tg in range(NTG):
        for k in range(KT_DI):
            deps = P3_END if tg < 2 else [mm4[(tg - 2, KT_DM - 1)]]
            ynT_ld[(tg, k)] = S.step("sy", lambda sy, tg=tg, k=k: sy.dma_start_transpose(
                s_ynT[tg % 2][:, k * TGT:(k + 1) * TGT],
                yn_dram[tg * TGT:(tg + 1) * TGT, k * 128:(k + 1) * 128]),
                deps=deps, dma=True)
        def emit_stat(m, i):
            stat_done[i] = S.step("pe", lambda t, m=m, i=i: (
                t.matmul(p_st1[:], s_onesb[:, :], s_xbf[i % 2][:],
                         start=(m == 0), stop=(m == KT_DM - 1)),
                t.matmul(p_st2[:], s_onesb[:, :], s_xqb[i % 2][:],
                         start=(m == 0), stop=(m == KT_DM - 1)))[1],
                deps=[cast_q[i], cast_b[i], onesb_set])

        for m in range(KT_DM):
            i = tg * KT_DM + m
            wdeps = P3_END if i < 2 else [mm4[(tg, m - 2) if m >= 2 else
                                              (tg - 1, KT_DM - 2 + m)]]
            wo_ld[i] = S.step("sy" if m % 3 == 1 else "gp",
                              lambda gp, m=m, i=i: gp.dma_start(
                out=s_wo[i % 2][:].rearrange("p (k m) -> p k m", k=KT_DI),
                in_=wo_bf[m].rearrange("(k p) m -> p k m", p=128)),
                deps=list(wdeps) + WO_CAST, dma=True)
            udeps = P3_END if i < 2 else [xm_done[i - 2]]
            ut_ld[i] = S.step("gp", lambda sy, tg=tg, m=m, i=i: sy.dma_start(
                out=s_ut[i % 2][:],
                in_=u_T[m * 128:(m + 1) * 128, tg * TGT:(tg + 1) * TGT]),
                deps=udeps, dma=True)
            deps = [wo_ld[i], ynT_ld[(tg, KT_DI - 1)]]
            if i >= 2:
                deps.append(xm_done[i - 2])
            mm4[(tg, m)] = S.step("pe", lambda t, i=i, tg=tg: [
                t.matmul(p_o[i % 2][:], s_wo[i % 2][:, k * 128:(k + 1) * 128],
                         s_ynT[tg % 2][:, k * TGT:(k + 1) * TGT],
                         start=(k == 0), stop=(k == KT_DI - 1))
                for k in range(KT_DI)][-1], deps=deps)
            if m >= 2:
                emit_stat(m - 2, i - 2)
            cdeps = [mm4[(tg, m)], ut_ld[i]]
            if i >= 2:
                cdeps.append(stat_done.get(i - 2) or xm_done[i - 2])
            xm_done[i] = S.step("v", lambda v, m=m, i=i: v.tensor_add(
                s_xmid[:, m * TGT:(m + 1) * TGT], p_o[i % 2][:], s_ut[i % 2][:]),
                deps=cdeps)
            cast_q[i] = S.step("v", lambda v, m=m, i=i: v.tensor_mul(
                s_xqb[i % 2][:], s_xmid[:, m * TGT:(m + 1) * TGT],
                s_xmid[:, m * TGT:(m + 1) * TGT]),
                deps=[xm_done[i]] + ([stat_done[i - 2]] if (i - 2) in stat_done else []))
            cast_b[i] = S.step("s", lambda s, m=m, i=i: s.copy(
                s_xbf[i % 2][:], s_xmid[:, m * TGT:(m + 1) * TGT]),
                deps=[xm_done[i]] + ([stat_done[i - 2]] if (i - 2) in stat_done else []))
        emit_stat(KT_DM - 2, tg * KT_DM + KT_DM - 2)
        emit_stat(KT_DM - 1, tg * KT_DM + KT_DM - 1)
        ilast = tg * KT_DM + KT_DM - 1
        v1 = S.step("v", lambda v: v.tensor_scalar_mul(s_mean[:], p_st1[:], 1.0 / DM),
                    deps=[stat_done[ilast]])
        v2 = S.step("v", lambda v: v.tensor_mul(s_m2[:], s_mean[:], s_mean[:]), deps=[v1])
        v3 = S.step("v", lambda v: v.scalar_tensor_tensor(
            s_var[:], p_st2[:], 1.0 / DM, s_m2[:], ALU.mult, ALU.subtract), deps=[v2])
        v4 = S.step("s", lambda s: s.activation(s_var[:], s_var[:], AF.Ln,
                                                bias=s_eps5[0:1, 0:1]),
                    deps=[v3, eps5_set])
        v5 = S.step("s", lambda s: s.activation(s_rstd[:], s_var[:], AF.Exp, scale=-0.5),
                    deps=[v4])
        mmr1 = S.step("pe", lambda t: t.matmul(p_rep[:], s_ones[:, :], s_mean[:],
                                               start=True, stop=True), deps=[v5])
        evr1 = S.step("v", lambda v: v.tensor_copy(s_mrep[:], p_rep[:]), deps=[mmr1])
        mmr2 = S.step("pe", lambda t: t.matmul(p_rep[:], s_ones[:, :], s_rstd[:],
                                               start=True, stop=True), deps=[evr1, v5])
        evr2 = S.step("v", lambda v: v.tensor_copy(s_rrep[:], p_rep[:]), deps=[mmr2])
        for m in range(KT_DM):
            i = tg * KT_DM + m
            adeps = [evr2]
            if m >= 2:
                adeps += [store_done[(tg, m - 2)], sc_mm[(tg, m - 2)]]
            elif tg >= 1:
                adeps += [store_done[(tg - 1, KT_DM - 2 + m)],
                          sc_mm[(tg - 1, KT_DM - 2 + m)]]
            ap1 = S.step("v", lambda v, m=m: v.tensor_sub(
                s_xlt[m % 2][:], s_xmid[:, m * TGT:(m + 1) * TGT], s_mrep[:]),
                deps=adeps)
            ap2 = S.step("v", lambda v, m=m: v.tensor_mul(
                s_xlt[m % 2][:], s_xlt[m % 2][:], s_rrep[:]), deps=[ap1])
            store_done[(tg, m)] = S.step("sy", lambda sy, tg=tg, m=m: sy.dma_start(
                out=x_ln[m * 128:(m + 1) * 128, tg * TGT:(tg + 1) * TGT],
                in_=s_xlt[m % 2][:]), deps=[ap2], dma=True)
            xlb_done[(tg, m)] = S.step("s", lambda s, m=m: s.copy(
                s_xlb[m % 2][:], s_xlt[m % 2][:]), deps=[ap2])
            scdeps = [xlb_done[(tg, m)], rw_ld]
            if m == 0 and sc_free is not None:
                scdeps.append(sc_free)
            sc_mm[(tg, m)] = S.step("pe", lambda t, m=m: t.matmul(
                p_scg[:], s_rwb[:, m * 8:(m + 1) * 8], s_xlb[m % 2][:],
                start=(m == 0), stop=(m == KT_DM - 1)), deps=scdeps)
        sc_ev = S.step("s", lambda s, tg=tg: s.activation(
            s_hsc[:, tg * TGT:(tg + 1) * TGT], p_scg[:], AF.Identity,
            bias=s_rb[:, 0:1]), deps=[sc_mm[(tg, KT_DM - 1)], LDC])
        sc_free = S.step("v", lambda v, tg=tg: v.tensor_sub(
            s_hsc[:, tg * TGT:(tg + 1) * TGT], s_hsc[:, tg * TGT:(tg + 1) * TGT],
            s_u42[:, tg * TGT:(tg + 1) * TGT]), deps=[sc_ev, g2])

    he = S.step("s", lambda s: s.activation(s_hsc[:], s_hsc[:], AF.Exp), deps=[sc_free])
    den_prev = None
    sm_deps = []
    for q in range(NTG):
        dmm = S.step("pe", lambda t, q=q: t.matmul(
            p_den[:], s_ones8[:, :], s_hsc[:, q * TGT:(q + 1) * TGT],
            start=True, stop=True), deps=[he, ones8_set] + ([den_prev] if den_prev else []))
        den_prev = S.step("v", lambda v, q=q: v.reciprocal(
            s_dinv[:, q * TGT:(q + 1) * TGT], p_den[:]), deps=[dmm])
    rep_prev = None
    for q in range(NTG):
        rmm = S.step("pe", lambda t, q=q: t.matmul(
            p_scg[:], s_ones[:, 0:8], s_dinv[:, q * TGT:(q + 1) * TGT],
            start=True, stop=True), deps=[den_prev] + ([rep_prev] if rep_prev else []))
        rep_prev = S.step("v", lambda v, q=q: v.tensor_mul(
            s_hsc[:, q * TGT:(q + 1) * TGT], s_hsc[:, q * TGT:(q + 1) * TGT],
            p_scg[:]), deps=[rmm])
        sm_deps.append(rep_prev)
    S.step("sy", lambda sy: sy.dma_start(out=sm_out, in_=s_hsc[:]),
           deps=sm_deps, dma=True)
    P4.close()

    S.finalize(nc, sems)
    root.close()
    return nc



def build_B():
    nc = bass.Bass("TRN2", target_bir_lowering=False)
    din = lambda n, s, d=F32: nc.dram_tensor(n, list(s), d, kind="ExternalInput").ap()
    dout = lambda n, s, d=F32: nc.dram_tensor(n, list(s), d, kind="ExternalOutput").ap()
    dint = lambda n, s, d=F32: nc.dram_tensor(
        n, list(s), d, kind=("ExternalOutput" if DEBUG_INTERNALS[0] else "Internal")).ap()

    x_ln = din("x_ln", (DM, L))
    smp = din("smp", (128, 2048))
    w_gate = din("w_gate", (DM, DFF))
    b_gate = din("b_gate", (128, KT_FF))
    w_up = din("w_up", (DM, DFF))
    b_up = din("b_up", (128, KT_FF))
    w_down = din("w_down", (DFF, DM))
    b_down = din("b_down", (128, KT_DM))
    sel_e = din("sel_e", (128, 8))
    sel_et = din("sel_et", (8, 128))
    sel_j = din("sel_j", (128, 16))
    ones_c = din("ones_c", (1, 128))
    bstep_c = din("bstep_c", (8, N_BISECT))

    out = dout("out", (DM, L))
    mfl_dram = dint("mfl_dram", (16, 256))
    wg_bf = dint("wg_bf", (KT_FF, DM, 128), BF16)
    wu_bf = dint("wu_bf", (KT_FF, DM, 128), BF16)
    wd_bf = dint("wd_bf", (KT_DM, DFF, 128), BF16)
    dbg_cnt = dint("dbg_cnt", (8, N_BISECT))
    dbg_thr = dint("dbg_thr", (128, 1))
    dbg_lo = dint("dbg_lo", (8, 1))
    dbg_xm2 = dint("dbg_xm2", (128, KT_DM * 512))
    dbg_hb = dint("dbg_hb", (128, KT_FF * 512), BF16)

    S = Sched()
    root = ExitStack()
    sems = {e: root.enter_context(nc.semaphore(f"sem_{e}")) for e in Sched.ENGINES}
    sb = lambda st, n, s, d=F32: st.enter_context(nc.sbuf_tensor(n, list(s), d))
    pt = lambda st, n, s, d=F32: st.enter_context(nc.psum_tensor(n, list(s), d))
    P = root

    s_smp = sb(P, "s_smp", (128, 2048))
    s_sele = sb(P, "s_sele", (128, 8))
    s_selet = sb(P, "s_selet", (8, 128))
    s_selj = sb(P, "s_selj", (128, 16))
    s_ones = sb(P, "s_ones", (1, 128))
    s_onesb = sb(P, "s_onesb", (128, 1), BF16)
    s_lo = sb(P, "s_lo", (8, 1))
    s_hi = sb(P, "s_hi", (8, 1))
    s_thr8 = sb(P, "s_thr8", (8, 1))
    s_step = sb(P, "s_step", (8, 1))
    s_lstep = sb(P, "s_lstep", (8, 1))
    s_bstep = sb(P, "s_bstep", (8, N_BISECT))
    s_c8 = sb(P, "s_c8", (8, 1))
    s_thr = sb(P, "s_thr", (128, 1))
    s_pc = sb(P, "s_pc", (128, 1))
    s_mrep = sb(P, "s_mrep", (128, L))
    s_bg = sb(P, "s_bg", (128, KT_FF))
    s_bu = sb(P, "s_bu", (128, KT_FF))
    s_bd = sb(P, "s_bd", (128, KT_DM))
    s_cntlog = sb(P, "s_cntlog", (8, N_BISECT))

    lds = None
    for t, d in ((s_smp, smp), (s_sele, sel_e), (s_selet, sel_et), (s_selj, sel_j),
                 (s_ones, ones_c), (s_bg, b_gate), (s_bu, b_up), (s_bd, b_down)):
        lds = S.step("sy", lambda sy, t=t, d=d: sy.dma_start(out=t[:], in_=d), dma=True)
    LDC = lds
    LD_BS = S.step("sy", lambda sy: sy.dma_start(out=s_bstep[:], in_=bstep_c), dma=True)
    wg_cast_d = {}
    wu_cast_d = {}
    wd_cast_d = {}
    for f in range(KT_FF):
        wg_cast_d[f] = S.step("gp", lambda gp, f=f: gp.dma_start(
            out=wg_bf[f], in_=w_gate[:, f * 128:(f + 1) * 128]), dma=True)
        wu_cast_d[f] = S.step("gp", lambda gp, f=f: gp.dma_start(
            out=wu_bf[f], in_=w_up[:, f * 128:(f + 1) * 128]), dma=True)
        if f >= 2 and f % 2 == 0 and (f - 2) // 2 < KT_DM:
            mt = (f - 2) // 2
            wd_cast_d[mt] = S.step("gp", lambda gp, mt=mt: gp.dma_start(
                out=wd_bf[mt], in_=w_down[:, mt * 128:(mt + 1) * 128]), dma=True)
    onesb_set = S.step("v", lambda v: v.memset(s_onesb[:], 1.0))
    s_eps5 = sb(P, "s_eps5", (1, 1))
    eps5_set = S.step("v", lambda v: v.memset(s_eps5[:], 1e-5))

    PB = ExitStack()
    s_junk = sb(PB, "s_junk", (128, 2048))
    s_mfl = sb(PB, "s_mfl", (16, 256))
    s_m1 = sb(PB, "s_m1", (1, L))
    p_thr = pt(PB, "p_thr", (128, 1))
    p_c8 = pt(PB, "p_c8", (8, 1))
    p_m16 = pt(PB, "p_m16", (16, 256))
    p_repm = pt(PB, "p_repm", (128, 512))

    z0 = S.step("v", lambda v: v.memset(s_lo[:], 0.0))
    z2 = S.step("v", lambda v: v.memset(s_thr8[:], 0.5))
    t_prev = None
    bstep_prev = None
    for it in range(N_BISECT):
        tm = S.step("pe", lambda t: t.matmul(p_thr[:], s_selet[:, :], s_thr8[:],
                                             start=True, stop=True),
                    deps=[z2, LDC] + ([t_prev] if t_prev else []) +
                         ([u2] if it > 0 else []))
        te = S.step("s", lambda s: s.copy(s_thr[:], p_thr[:]), deps=[tm])
        t_prev = te
        c_a = S.step("v", lambda v: v.tensor_scalar(
            s_junk[:], s_smp[:], s_thr[:, 0:1], None, ALU.is_gt), deps=[te, LDC])
        c_b = S.step("v", lambda v: v.reduce_sum(s_pc[:, 0:1], s_junk[:],
                                                 mybir.AxisListType.X), deps=[c_a])
        csum = S.step("pe", lambda t: t.matmul(p_c8[:], s_sele[:, :], s_pc[:],
                                               start=True, stop=True), deps=[c_b])
        if DEBUG_INTERNALS[0]:
            S.step("v", lambda v, it=it: v.tensor_copy(s_cntlog[:, it:it + 1], p_c8[:]),
                   deps=[csum])
        b1 = S.step("v", lambda v: v.tensor_scalar(s_c8[:], p_c8[:], 4096.0, None,
                                                   ALU.is_ge), deps=[csum, z0])
        u1 = S.step("s", lambda s, it=it: s.activation(
            s_step[:], s_c8[:], AF.Identity, scale=float(2.0 ** (-(it + 2)) * 2),
            bias=s_bstep[:, it:it + 1]), deps=[b1, LD_BS] + ([u4] if it > 0 else []))
        u2 = S.step("v", lambda v: v.tensor_add(s_thr8[:], s_thr8[:], s_step[:]),
                    deps=[u1])
        u3 = S.step("s", lambda s, it=it: s.activation(
            s_lstep[:], s_c8[:], AF.Identity, scale=float(2.0 ** (-(it + 1)))),
            deps=[b1, u1])
        u4 = S.step("v", lambda v: v.tensor_add(s_lo[:], s_lo[:], s_lstep[:]),
                    deps=[u3, u2])
        tm = S.step("pe", lambda t: t.matmul(p_thr[:], s_selet[:, :], s_lo[:],
                                         start=True, stop=True), deps=[u4, t_prev])
    te = S.step("s", lambda s: s.copy(s_thr[:], p_thr[:]), deps=[tm])
    if DEBUG_INTERNALS[0]:
        S.step("sy", lambda sy: sy.dma_start(out=dbg_cnt, in_=s_cntlog[:]),
               deps=[te], dma=True)
        te = S.step("sy", lambda sy: sy.dma_start(out=dbg_thr, in_=s_thr[:]),
                    deps=[te], dma=True)
        te = S.step("sy", lambda sy: sy.dma_start(out=dbg_lo, in_=s_lo[:]),
                    deps=[te], dma=True)
    ws = S.step("v", lambda v: v.scalar_tensor_tensor(
        s_junk[:], s_smp[:], s_thr[:, 0:1], s_smp[:], ALU.is_gt, ALU.mult), deps=[te])
    m16 = S.step("pe", lambda t: t.matmul(p_m16[:], s_selj[:, :], s_junk[:, 0:256],
                                          start=True, stop=True), deps=[ws])
    mf_ev = S.step("v", lambda v: v.tensor_copy(s_mfl[:], p_m16[:]), deps=[m16])
    mf_sp = S.step("sy", lambda sy: sy.dma_start(out=mfl_dram, in_=s_mfl[:]),
                   deps=[mf_ev], dma=True)
    m1_ld = S.step("sy", lambda sy: sy.dma_start(
        out=s_m1[:], in_=mfl_dram.rearrange("a b -> (a b)").unsqueeze(0)),
        deps=[mf_sp], dma=True)
    mrep_ev = None
    for q in range(8):
        rmm = S.step("pe", lambda t, q=q: t.matmul(
            p_repm[:], s_ones[:, :], s_m1[:, q * 512:(q + 1) * 512],
            start=True, stop=True), deps=[m1_ld] + ([mrep_ev] if mrep_ev else []))
        mrep_ev = S.step("v", lambda v, q=q: v.tensor_copy(
            s_mrep[:, q * 512:(q + 1) * 512], p_repm[:]), deps=[rmm])
    M_READY = mrep_ev
    B_END = S.fence()
    PB.close()

    TB = 512
    NTB = L // TB
    s_xb = sb(P, "s_xb", (128, KT_DM * TB), BF16)
    s_hb = sb(P, "s_hb", (128, KT_FF * TB), BF16)
    s_wg = [sb(P, f"s_wg{i}", (128, KT_DM * 128), BF16) for i in range(3)]
    s_wu = [sb(P, f"s_wu{i}", (128, KT_DM * 128), BF16) for i in range(3)]
    s_wd = [sb(P, f"s_wd{i}", (128, KT_FF * 128), BF16) for i in range(3)]
    s_gbf = sb(P, "s_gbf", (128, TB), BF16)
    s_xm2 = sb(P, "s_xm2", (128, KT_DM * TB))
    s_xr = [sb(P, f"s_xr{i}", (128, TB)) for i in range(2)]
    s_xqb = [sb(P, f"s_xqb{i}", (128, TB), BF16) for i in range(2)]
    s_xbf = [sb(P, f"s_xbf{i}", (128, TB), BF16) for i in range(2)]
    s_mean = sb(P, "s_mean", (1, TB))
    s_var = sb(P, "s_var", (1, TB))
    s_m2 = sb(P, "s_m2", (1, TB))
    s_rstd = sb(P, "s_rstd", (1, TB))
    s_mrep2 = sb(P, "s_mrep2", (128, TB))
    s_rrep2 = sb(P, "s_rrep2", (128, TB))
    s_ot = [sb(P, f"s_ot{i}", (128, TB)) for i in range(2)]
    p_gu = [pt(P, f"p_gua{i}", (128, TB)) for i in range(2)]
    p_gu2 = [pt(P, f"p_gub{i}", (128, TB)) for i in range(2)]
    p_dn = [pt(P, f"p_dn{i}", (128, TB)) for i in range(2)]
    p_st1 = pt(P, "p_st1", (1, TB))
    p_st2 = pt(P, "p_st2", (1, TB))

    xb_ld = {}
    gu_ld = {}
    gmm = {}
    umm = {}
    sil = {}
    hb_done = {}
    wd_ld = {}
    dmm = {}
    x1_done = {}
    x2_done = {}
    stat_done = {}
    apply_done = {}
    ot_store = {}
    xr_ld = {}
    dm_last = {}

    for tb in range(NTB):
        xdeps = B_END if tb == 0 else [gmm[(tb - 1, KT_FF - 1)], umm[(tb - 1, KT_FF - 1)]]
        for k in range(KT_DM):
            xb_ld[tb] = S.step("gp", lambda gp, tb=tb, k=k: gp.dma_start(
                out=s_xb[:, k * TB:(k + 1) * TB],
                in_=x_ln[k * 128:(k + 1) * 128, tb * TB:(tb + 1) * TB]),
                deps=xdeps, dma=True)
        for f in range(KT_FF):
            i = tb * KT_FF + f
            gdeps = [hb_done[i - 2]] if i >= 2 else (B_END if tb == 0 else [])

            wgl = S.step("sy", lambda sy, f=f, i=i: sy.dma_start(
                out=s_wg[i % 3][:].rearrange("p (k m) -> p k m", k=KT_DM),
                in_=wg_bf[f].rearrange("(k p) m -> p k m", p=128)),
                deps=list(gdeps) + [wg_cast_d[f]], dma=True)
            gu_ld[i] = S.step("gp", lambda sy, f=f, i=i: sy.dma_start(
                out=s_wu[i % 3][:].rearrange("p (k m) -> p k m", k=KT_DM),
                in_=wu_bf[f].rearrange("(k p) m -> p k m", p=128)),
                deps=list(gdeps) + [wu_cast_d[f]], dma=True)
            gm_deps = [wgl, xb_ld[tb]]
            if i >= 2:
                gm_deps.append(sil[i - 2])
            gmm[(tb, f)] = S.step("pe", lambda t, i=i: [
                t.matmul(p_gu[i % 2][:], s_wg[i % 3][:, k * 128:(k + 1) * 128],
                         s_xb[:, k * TB:(k + 1) * TB],
                         start=(k == 0), stop=(k == KT_DM - 1))
                for k in range(KT_DM)][-1], deps=gm_deps)
            um_deps = [gu_ld[i], xb_ld[tb]]
            if i >= 2:
                um_deps.append(hb_done[i - 2])
            umm[(tb, f)] = S.step("pe", lambda t, i=i: [
                t.matmul(p_gu2[i % 2][:], s_wu[i % 3][:, k * 128:(k + 1) * 128],
                         s_xb[:, k * TB:(k + 1) * TB],
                         start=(k == 0), stop=(k == KT_DM - 1))
                for k in range(KT_DM)][-1], deps=um_deps)
            sil_deps = [gmm[(tb, f)], LDC]
            if i >= 1:
                sil_deps.append(hb_done[i - 1])
            sil[i] = S.step("s", lambda s, f=f, i=i: s.activation(
                s_gbf[:], p_gu[i % 2][:], AF.Silu, bias=s_bg[:, f:f + 1]), deps=sil_deps)
            hbdeps = [umm[(tb, f)], sil[i]]
            if tb >= 1 and f < 2:
                hbdeps.append(dm_last[tb - 1])
            hb_done[i] = S.step("v", lambda v, f=f, i=i: v.scalar_tensor_tensor(
                s_hb[:, f * TB:(f + 1) * TB], p_gu2[i % 2][:], s_bu[:, f:f + 1],
                s_gbf[:], ALU.add, ALU.mult), deps=hbdeps)
        HB_ALL = hb_done[tb * KT_FF + KT_FF - 1]
        if DEBUG_INTERNALS[0] and tb == 0:
            HB_ALL = S.step("sy", lambda sy: sy.dma_start(out=dbg_hb, in_=s_hb[:]),
                            deps=[HB_ALL], dma=True)
        for m in range(KT_DM):
            im = tb * KT_DM + m
            wdeps = [dmm[(tb, m - 3)]] if m >= 3 else (
                [dm_last[tb - 1]] if tb >= 1 else [])
            wd_ld[im] = S.step("sy", lambda sy, m=m, im=im: sy.dma_start(
                out=s_wd[im % 3][:].rearrange("p (k m) -> p k m", k=KT_FF),
                in_=wd_bf[m].rearrange("(k p) m -> p k m", p=128)),
                deps=list(wdeps) + [wd_cast_d[m]], dma=True)
            xr_deps = [x2_done[im - 2]] if m >= 2 else (
                [apply_done[(tb - 1, KT_DM - 1)]] if tb >= 1 else [])
            xr_ld[im] = S.step("sy", lambda sy, tb=tb, m=m, im=im: sy.dma_start(
                out=s_xr[im % 2][:],
                in_=x_ln[m * 128:(m + 1) * 128, tb * TB:(tb + 1) * TB]),
                deps=xr_deps, dma=True)
            dm_deps = [wd_ld[im], HB_ALL]
            if m >= 2:
                dm_deps.append(x1_done[im - 2])
            dmm[(tb, m)] = S.step("pe", lambda t, m=m, im=im: [
                t.matmul(p_dn[m % 2][:], s_wd[im % 3][:, k * 128:(k + 1) * 128],
                         s_hb[:, k * TB:(k + 1) * TB],
                         start=(k == 0), stop=(k == KT_FF - 1))
                for k in range(KT_FF)][-1], deps=dm_deps)
            x1_deps = [dmm[(tb, m)], M_READY, LDC]
            if tb >= 1:
                x1_deps.append(apply_done[(tb - 1, m)])
            x1_done[im] = S.step("v", lambda v, tb=tb, m=m: v.scalar_tensor_tensor(
                s_xm2[:, m * TB:(m + 1) * TB], p_dn[m % 2][:], s_bd[:, m:m + 1],
                s_mrep[:, tb * TB:(tb + 1) * TB], ALU.add, ALU.mult),
                deps=x1_deps)
            x2_done[im] = S.step("v", lambda v, m=m, im=im: v.tensor_add(
                s_xm2[:, m * TB:(m + 1) * TB], s_xm2[:, m * TB:(m + 1) * TB],
                s_xr[im % 2][:]), deps=[x1_done[im], xr_ld[im]])
            cdeps = [x2_done[im]]
            if m >= 2:
                cdeps.append(stat_done[im - 2])
            xq = S.step("v", lambda v, m=m, im=im: v.tensor_mul(
                s_xqb[im % 2][:], s_xm2[:, m * TB:(m + 1) * TB],
                s_xm2[:, m * TB:(m + 1) * TB]), deps=cdeps)
            xb2 = S.step("s", lambda s, m=m, im=im: s.copy(
                s_xbf[im % 2][:], s_xm2[:, m * TB:(m + 1) * TB]), deps=cdeps)
            stdeps = [xq, xb2, onesb_set]
            if m == 0 and tb >= 1:
                stdeps.append(vv3[tb - 1])
            stat_done[im] = S.step("pe", lambda t, m=m, im=im: (
                t.matmul(p_st1[:], s_onesb[:, :], s_xbf[im % 2][:],
                         start=(m == 0), stop=(m == KT_DM - 1)),
                t.matmul(p_st2[:], s_onesb[:, :], s_xqb[im % 2][:],
                         start=(m == 0), stop=(m == KT_DM - 1)))[1], deps=stdeps)
        if tb == 0:
            vv3 = {}
        ilast = tb * KT_DM + KT_DM - 1
        v1 = S.step("v", lambda v: v.tensor_scalar_mul(s_mean[:], p_st1[:], 1.0 / DM),
                    deps=[stat_done[ilast]])
        v2 = S.step("v", lambda v: v.tensor_mul(s_m2[:], s_mean[:], s_mean[:]), deps=[v1])
        vv3[tb] = S.step("v", lambda v: v.scalar_tensor_tensor(
            s_var[:], p_st2[:], 1.0 / DM, s_m2[:], ALU.mult, ALU.subtract), deps=[v2])
        v4 = S.step("s", lambda s: s.activation(s_var[:], s_var[:], AF.Ln,
                                                bias=s_eps5[0:1, 0:1]),
                    deps=[vv3[tb], eps5_set])
        v5 = S.step("s", lambda s: s.activation(s_rstd[:], s_var[:], AF.Exp, scale=-0.5),
                    deps=[v4])
        mm1 = S.step("pe", lambda t: t.matmul(p_dn[0][:], s_ones[:, :], s_mean[:],
                                              start=True, stop=True),
                     deps=[v5, x1_done[ilast]])
        ev1 = S.step("v", lambda v: v.tensor_copy(s_mrep2[:], p_dn[0][:]), deps=[mm1])
        mm2 = S.step("pe", lambda t: t.matmul(p_dn[0][:], s_ones[:, :], s_rstd[:],
                                              start=True, stop=True), deps=[ev1, v5])
        ev2 = S.step("v", lambda v: v.tensor_copy(s_rrep2[:], p_dn[0][:]), deps=[mm2])
        for m in range(KT_DM):
            im = tb * KT_DM + m
            adeps = [ev2, x2_done[im]]
            if m >= 2:
                adeps.append(ot_store[(tb, m - 2)])
            elif tb >= 1:
                adeps.append(ot_store[(tb - 1, KT_DM - 2 + m)])
            a1 = S.step("v", lambda v, m=m, im=im: v.tensor_sub(
                s_ot[im % 2][:], s_xm2[:, m * TB:(m + 1) * TB], s_mrep2[:]), deps=adeps)
            apply_done[(tb, m)] = S.step("v", lambda v, m=m, im=im: v.tensor_mul(
                s_ot[im % 2][:], s_ot[im % 2][:], s_rrep2[:]), deps=[a1])
            ot_store[(tb, m)] = S.step("sy", lambda sy, tb=tb, m=m, im=im: sy.dma_start(
                out=out[m * 128:(m + 1) * 128, tb * TB:(tb + 1) * TB],
                in_=s_ot[im % 2][:]), deps=[apply_done[(tb, m)]], dma=True)
        if DEBUG_INTERNALS[0] and tb == 0:
            S.step("sy", lambda sy: sy.dma_start(out=dbg_xm2, in_=s_xm2[:]),
                   deps=[x2_done[tb * KT_DM + KT_DM - 1],
                         xq if False else x2_done[tb * KT_DM + KT_DM - 1]], dma=True)
        dm_last[tb] = dmm[(tb, KT_DM - 1)]

    S.finalize(nc, sems)
    root.close()
    return nc



_BUILT = {}
_CAPTURE = {}


def _get_built(which):
    if which not in _BUILT:
        _BUILT[which] = build_A() if which == "A" else build_B()
    return _BUILT[which]


_U42 = [None]


def _get_u42():
    if _U42[0] is not None:
        return _U42[0]
    import os
    import subprocess
    import sys
    import tempfile
    fd, path = tempfile.mkstemp(suffix=".npy")
    os.close(fd)
    code = ("import os; os.environ['JAX_PLATFORMS']='cpu'; "
            "import numpy, jax, jax.numpy as jnp; "
            "numpy.save(%r, numpy.asarray(jax.random.uniform("
            "jax.random.key(42), (%d, 8), dtype=jnp.float32)))" % (path, NCORE * L))
    env = dict(os.environ)
    env.pop("PYTHONPATH", None)
    env["JAX_PLATFORMS"] = "cpu"
    try:
        subprocess.run([sys.executable, "-c", code], env=env, check=True)
        arr = np.load(path)
    except Exception:
        import jax
        import jax.numpy as jnp
        with jax.default_device(jax.devices("cpu")[0]):
            arr = np.asarray(jax.random.uniform(jax.random.key(42), (NCORE * L, 8),
                                                dtype=jnp.float32))
    finally:
        try:
            os.remove(path)
        except OSError:
            pass
    _U42[0] = arr
    return arr


def kernel(L0_attn_u, L0_attn_W_in, L0_attn_conv_w, L0_attn_conv_b, L0_attn_dt_bias,
           L0_attn_A_log, L0_attn_D_param, L0_attn_norm_w, L0_attn_W_out,
           L0_ffn_router_w, L0_ffn_router_b, L0_ffn_w_gate, L0_ffn_b_gate,
           L0_ffn_w_up, L0_ffn_b_up, L0_ffn_w_down, L0_ffn_b_down):
    f32 = np.float32
    u = np.asarray(L0_attn_u, dtype=f32)
    W_in = np.ascontiguousarray(np.asarray(L0_attn_W_in, f32))
    conv_w = np.asarray(L0_attn_conv_w, f32)
    conv_b = np.asarray(L0_attn_conv_b, f32)
    dt_bias = np.asarray(L0_attn_dt_bias, f32).reshape(64, 1)
    A_log = np.asarray(L0_attn_A_log, f32).reshape(64, 1)
    D_param = np.asarray(L0_attn_D_param, f32).reshape(64, 1)
    norm_w = np.asarray(L0_attn_norm_w, f32).reshape(1, DI)
    W_out = np.ascontiguousarray(np.asarray(L0_attn_W_out, f32))
    router_w = np.ascontiguousarray(np.asarray(L0_ffn_router_w, f32))
    router_b = np.asarray(L0_ffn_router_b, f32).reshape(8, 1)

    conv_w_p = np.ascontiguousarray(conv_w.reshape(34, 128, 4).transpose(1, 0, 2)
                                    .reshape(128, 34 * 4))
    conv_b_p = np.ascontiguousarray(conv_b.reshape(34, 128).T)
    tril_c = -np.triu(np.ones((128, 128), f32))
    sidx = np.arange(128)
    mask_c = np.where(sidx[None, :] >= sidx[:, None], 0.0, -NEG_MASK).astype(f32)
    idf_c = np.eye(128, dtype=f32)
    ones_c = np.ones((1, 128), f32)

    u42_T = _get_u42().T

    nc_a = _get_built("A")
    in_maps = []
    for c in range(NCORE):
        in_maps.append({
            "u_T": np.ascontiguousarray(u[c].T),
            "W_in": W_in,
            "conv_w_p": conv_w_p,
            "conv_b_p": conv_b_p,
            "dt_bias": dt_bias,
            "A_log": A_log,
            "D_param": D_param,
            "norm_w": norm_w,
            "W_out": W_out,
            "router_w": router_w,
            "router_b": router_b,
            "u42_T": np.ascontiguousarray(u42_T[:, c * L:(c + 1) * L]),
            "tril_c": tril_c,
            "mask_c": mask_c,
            "idf_c": idf_c,
            "ones_c": ones_c,
        })
    resA = run_bass_kernel_spmd(nc_a, in_maps, core_ids=list(range(NCORE)))
    _CAPTURE['A'] = resA

    smA = np.stack([np.asarray(resA.results[c]["sm"]) for c in range(NCORE)])
    xlnA = [np.asarray(resA.results[c]["x_ln"]) for c in range(NCORE)]

    smp_cores = []
    for c in range(NCORE):
        arr = np.empty((128, 2048), f32)
        for e in range(8):
            for src in range(NCORE):
                cc = (src - c) % NCORE
                blk = smA[src, e].reshape(16, 256)
                arr[e * 16:(e + 1) * 16, cc * 256:(cc + 1) * 256] = blk
        smp_cores.append(arr)

    pad_g = DFF - 5461
    w_gate = np.ascontiguousarray(np.pad(np.asarray(L0_ffn_w_gate, f32),
                                         ((0, 0), (0, pad_g))))
    w_up = np.ascontiguousarray(np.pad(np.asarray(L0_ffn_w_up, f32),
                                       ((0, 0), (0, pad_g))))
    w_down = np.ascontiguousarray(np.pad(np.asarray(L0_ffn_w_down, f32),
                                         ((0, pad_g), (0, 0))))
    b_gate = np.ascontiguousarray(np.pad(np.asarray(L0_ffn_b_gate, f32), (0, pad_g))
                                  .reshape(KT_FF, 128).T)
    b_up = np.ascontiguousarray(np.pad(np.asarray(L0_ffn_b_up, f32), (0, pad_g))
                                .reshape(KT_FF, 128).T)
    b_down = np.ascontiguousarray(np.asarray(L0_ffn_b_down, f32).reshape(KT_DM, 128).T)
    p128 = np.arange(128)
    sel_e = (p128[:, None] // 16 == np.arange(8)[None, :]).astype(f32)
    sel_et = np.ascontiguousarray(sel_e.T)
    sel_j = (p128[:, None] % 16 == np.arange(16)[None, :]).astype(f32)
    bstep_c = np.broadcast_to(
        -np.asarray([2.0 ** (-(k + 2)) for k in range(N_BISECT)], f32), (8, N_BISECT)
    ).copy()

    nc_b = _get_built("B")
    in_maps_b = []
    for c in range(NCORE):
        in_maps_b.append({
            "x_ln": xlnA[c],
            "smp": smp_cores[c],
            "w_gate": w_gate, "b_gate": b_gate,
            "w_up": w_up, "b_up": b_up,
            "w_down": w_down, "b_down": b_down,
            "sel_e": sel_e, "sel_et": sel_et, "sel_j": sel_j,
            "ones_c": ones_c, "bstep_c": bstep_c,
        })
    resB = run_bass_kernel_spmd(nc_b, in_maps_b, core_ids=list(range(NCORE)))
    _CAPTURE['B'] = resB

    out = np.empty((NCORE, L, DM), f32)
    for c in range(NCORE):
        out[c] = np.asarray(resB.results[c]["out"]).T
    return out
